# version 10 (final)
# speedup vs baseline: 1.1278x; 1.0264x over previous
import os
import numpy as np
import ml_dtypes
from contextlib import ExitStack

import concourse.bass as bass
import concourse.mybir as mybir
from concourse.bass_utils import run_bass_kernel_spmd
from concourse import library_config
from bass_rust import ReduceOp

F32 = mybir.dt.float32
BF16 = mybir.dt.bfloat16
I32 = mybir.dt.int32
AF = mybir.ActivationFunctionType
ALU = mybir.AluOpType
AX = mybir.AxisListType

MAGIC = 16384.0
_TRUNC_CAST = (os.environ.get("BASS_SIM_FP64") == "1"
               or os.environ.get("BASS_INTERP_TRUNC") == "1")
RND = MAGIC + 0.5 if _TRUNC_CAST else 0.0
NMAG = -MAGIC if _TRUNC_CAST else 0.0
NEG = -1e30

def real_cfg():
    return dict(d=2048, hc=512, B=2, nc=8, dff0=8192, dff1=5632,
                inter=4096, N=128, P=64, K=4, eps=1e-5)

def derived(cfg):
    c = dict(cfg)
    c["T"] = 2 * c["hc"]
    c["L"] = c["nc"] * c["hc"]
    c["heads0"] = c["d"] // 128
    c["hssm"] = c["inter"] // c["P"]
    c["convdim"] = c["inter"] + 2 * c["N"]
    c["dch"] = c["d"] // 128
    return c


class Builder:

    ENGINES = ("tensor", "vector", "scalar", "gpsimd", "sync")

    def __init__(self, cfg):
        self.cfg = derived(cfg)
        self.nc = bass.Bass("TRN2", target_bir_lowering=False, debug=False)
        self.ops = {e: [] for e in self.ENGINES}
        self.cnt = {}
        self.waited = {e: {} for e in self.ENGINES}
        self.sems = {}
        self.sem_names = []
        self.stack = None
        self._sb = {}
        self._ps = {}
        self._dram = {}
        self._finalized = False

    def sem(self, name):
        if name not in self.cnt:
            self.cnt[name] = 0
            self.sem_names.append(name)
        return name

    def sbuf(self, name, shape, dtype):
        self._sb[name] = (shape, dtype)
        return name

    def alias(self, name, base, c0, shape, dtype=None):
        self._alias = getattr(self, "_alias", {})
        self._alias[name] = (base, c0, shape, dtype)
        return name

    def psum(self, name, shape, dtype=F32):
        self._ps[name] = (shape, dtype)
        return name

    def dram(self, name, shape, dtype, kind=None):
        if kind is None:
            h = self.nc.dram_tensor(name, list(shape), dtype)
        else:
            h = self.nc.dram_tensor(name, list(shape), dtype, kind=kind)
        self._dram[name] = h
        return name

    SEM_ENGINE = {"t": "tensor", "v": "vector", "s": "scalar", "g": "gpsimd",
                  "d": None, "c": None}

    def eng_of_sem(self, s):
        return self.SEM_ENGINE.get(s.split("_")[0], None)

    def op(self, eng, fn, waits=(), inc=None, inc_k=1):
        flat = []
        for w in waits:
            if w is None:
                continue
            if isinstance(w, list):
                flat.extend(x for x in w if x is not None)
            else:
                flat.append(w)
        waits = flat
        for (s, v) in waits:
            assert v <= self.cnt[s], (s, v, self.cnt[s])
        tok = None
        if inc is not None:
            self.cnt[inc] += inc_k
            tok = (inc, self.cnt[inc])
        byname = {}
        for (s, v) in waits:
            byname[s] = max(v, byname.get(s, 0))
        waits = list(byname.items())
        wcur = self.waited[eng]
        need = []
        drain = False
        for (s, v) in waits:
            if self.eng_of_sem(s) == eng:
                if wcur.get(s, 0) < v:
                    drain = True
                    wcur[s] = v
                continue
            if wcur.get(s, 0) < v:
                need.append((s, v))
                wcur[s] = v
        self.ops[eng].append((fn, tuple(need), inc, inc_k, drain))
        return tok

    def barrier_token(self, sem):
        return (sem, self.cnt[sem])

    def finalize(self):
        assert not self._finalized
        self._finalized = True
        nc = self.nc
        self.stack = ExitStack()
        E = self.stack.enter_context
        sb = {n: E(nc.sbuf_tensor(n, list(s), d)) for n, (s, d) in self._sb.items()}
        ps = {n: E(nc.psum_tensor(n, list(s), d)) for n, (s, d) in self._ps.items()}
        sems = {n: E(nc.semaphore(n)) for n in self.sem_names}
        self.sb_h, self.ps_h, self.sem_h = sb, ps, sems
        block = E(nc.Block())
        handles = dict(sb)
        handles.update(ps)
        for name, (base, c0, shape, dtype) in getattr(self, "_alias", {}).items():
            ap = sb[base].ap()
            if dtype is not None:
                ap = ap.bitcast(dtype)
            handles[name] = ap[0:shape[0], c0:c0 + shape[1]]

        def run_engine(eng_name):
            def body(eng):
                for (fn, waits, inc, inc_k, drain) in self.ops[eng_name]:
                    if drain:
                        eng.drain()
                    for (s, v) in waits:
                        eng.wait_ge(sems[s], v)
                    inst = fn(eng, handles)
                    if inc is not None:
                        assert inst is not None, f"op on {eng_name} has inc but returned None"
                        inst.then_inc(sems[inc], inc_k)
            return body

        block.sync(run_engine("sync"))
        block.vector(run_engine("vector"))
        block.scalar(run_engine("scalar"))
        block.tensor(run_engine("tensor"))
        block.gpsimd(run_engine("gpsimd"))
        self.stack.close()
        return nc


class Prog:
    def __init__(self, b: Builder):
        self.b = b
        self.c = b.cfg
        self.tok = {}
        self.bankfree = {}
        self.wfree = {"wa": None, "wb": None}
        self._dk = 0
        self._qk = 0

    def declare(self):
        b, c = self.b, self.c
        d, T, hc = c["d"], c["T"], c["hc"]
        dch = c["dch"]
        s8 = d // 8
        b.dram("xs", [d, T], F32, kind="ExternalInput")
        b.dram("l0w1t_s", [s8, c["dff0"]], F32, kind="ExternalInput")
        b.dram("l0w2t_s", [c["dff0"] // 8, d], F32, kind="ExternalInput")
        b.dram("l0att_s", [s8, 4 * d], F32, kind="ExternalInput")
        b.dram("l1w1t_s", [s8, c["dff1"]], F32, kind="ExternalInput")
        b.dram("l1w3t_s", [s8, c["dff1"]], F32, kind="ExternalInput")
        b.dram("l1w2t_s", [c["dff1"] // 8, d], F32, kind="ExternalInput")
        b.dram("l1zx_s", [s8, c["inter"] + c["convdim"]], F32, kind="ExternalInput")
        b.dram("l1wo_s", [c["inter"] // 8, d], F32, kind="ExternalInput")
        b.dram("l1wdt", [d, c["hssm"]], F32, kind="ExternalInput")
        b.dram("convw", [128, (c["convdim"] // 128) * c["K"]], F32, kind="ExternalInput")
        b.dram("convb", [128, c["convdim"] // 128], F32, kind="ExternalInput")
        b.dram("smallv", [c["hssm"], 4], F32, kind="ExternalInput")
        b.dram("normw", [128, c["inter"] // 128], F32, kind="ExternalInput")
        b.dram("ident", [128, 128], BF16, kind="ExternalInput")
        b.dram("identf", [128, 128], F32, kind="ExternalInput")
        b.dram("ssdmask", [128, 128], F32, kind="ExternalInput")
        b.dram("onesb", [128, 8], BF16, kind="ExternalInput")
        b.dram("cconst", [128, 8], F32, kind="ExternalInput")
        b.dram("qidxb", [128, hc], F32, kind="ExternalInput")
        b.dram("qposr", [1, 2], F32, kind="ExternalInput")
        b.dram("dcolx", [128, c["inter"] // 128], F32, kind="ExternalInput")
        b.dram("onest", [c["hssm"], 128], F32, kind="ExternalInput")
        b.dram("smk", [1, 16], F32, kind="ExternalInput")
        b.dram("bsel", [128, 16], F32, kind="ExternalInput")
        b.dram("knm_i", [1, c["heads0"]], F32)
        b.dram("qtmp", [d, hc], BF16)
        b.dram("ytmp", [c["inter"], T], BF16)
        b.dram("drow", [1, 64], F32)
        b.dram("knm_o", [1, c["heads0"]], F32)
        b.dram("ys", [d, T], F32, kind="ExternalOutput")
        if os.environ.get("KDBG_F1"):
            b.dram("dbg_h", [128, 24576], BF16, kind="ExternalOutput")
            b.dram("dbg_q", [128, 24576], BF16, kind="ExternalOutput")
            b.dram("dbg_s", [128, 512], F32, kind="ExternalOutput")
        b.dram("q_l0w1", [d, c["dff0"]], BF16)
        b.dram("q_l0w2", [c["dff0"], d], BF16)
        b.dram("q_l0att", [d, 4 * d], BF16)
        b.dram("q_l1w1", [d, c["dff1"]], BF16)
        b.dram("q_l1w3", [d, c["dff1"]], BF16)
        b.dram("q_l1w2", [c["dff1"], d], BF16)
        b.dram("q_l1zx", [d, c["inter"] + c["convdim"]], BF16)
        b.dram("q_l1wo", [c["inter"], d], BF16)
        for nm, sh in [("i_l0w1", [s8, c["dff0"]]), ("i_l0w2", [c["dff0"] // 8, d]),
                       ("i_l0att", [s8, 4 * d]), ("i_l1w1", [s8, c["dff1"]]),
                       ("i_l1w3", [s8, c["dff1"]]), ("i_l1w2", [c["dff1"] // 8, d]),
                       ("i_l1zx", [s8, c["inter"] + c["convdim"]]), ("i_l1wo", [c["inter"] // 8, d])]:
            b.dram(nm, sh, BF16)
        b.dram("arin", [1, 8], F32)
        b.dram("arout", [1, 8], F32)
        b.dram("kin", [d, T], BF16)
        b.dram("vin", [T, d], BF16)
        b.dram("kout", [8 * d, T], BF16)
        b.dram("vout", [8 * T, d], BF16)
        b.dram("bdin", [d, 6], F32)
        b.dram("bdout", [8 * d, 6], F32)
        b.dram("dsin", [256, c["inter"]], F32)
        b.dram("dsout", [8 * 256, c["inter"]], F32)
        b.dram("dcin", [2, c["hssm"]], F32)
        b.dram("dcout", [16, c["hssm"]], F32)

        cvch = c["convdim"] // 128
        ich = c["inter"] // 128
        hh = hc // 2
        big_cols = max((c["dff0"] // 128) * hc,
                       2 * (c["dff1"] // 128) * hh,
                       2 * cvch * hh + 2 * c["inter"] + 8 * hh + c["inter"],
                       3 * dch * hc + 2 * hc + (hc // 128) * hc)
        self.big_cols = big_cols
        b.sbuf("res", [128, dch * T], F32)
        b.sbuf("act", [128, dch * hc], BF16)
        b.sbuf("big", [128, big_cols], BF16)
        b.sbuf("wa", [128, 1536], BF16)
        b.sbuf("wb", [128, 1536], BF16)
        b.sbuf("wsp", [128, 2048], F32)
        b.sbuf("bc", [128, 5 * hc], F32)
        b.sbuf("rb", [1, 3 * hc], F32)
        b.sbuf("stg", [128, 1024], BF16)
        b.sbuf("sml", [128, 64], F32)
        b.sbuf("sqb", [128, 512], BF16)
        b.sbuf("gtl", [128, 384], BF16)
        b.sbuf("bdx", [128, cvch * 3], BF16)
        b.sbuf("bdx2", [128, cvch * 3], BF16)
        b.sbuf("blnx", [128, dch * 3], BF16)
        b.sbuf("ident_s", [128, 128], BF16)
        b.sbuf("identf_s", [128, 128], F32)
        b.sbuf("ssdm_s", [128, 128], F32)
        b.sbuf("ones_s", [128, 8], BF16)
        b.sbuf("cc_s", [128, 8], F32)
        b.sbuf("normw_s", [128, ich], F32)
        b.sbuf("dcolx_s", [128, ich], F32)
        b.sbuf("bsel_s", [128, 16], F32)
        b.sbuf("smk_s", [1, 16], F32)
        b.sbuf("qpos_s", [1, 2], F32)
        b.sbuf("smv_s", [c["hssm"], 4], F32)
        hss = c["hssm"]
        UF = 7040
        b.sbuf("uf32", [128, UF], F32)
        al = b.alias
        al("expand", "uf32", 0, [128, c["inter"] // 2])
        al("dts", "uf32", c["inter"] // 2, [hss, hc])
        al("ea2", "uf32", c["inter"] // 2 + hc, [hss, T])
        al("hcd", "uf32", c["inter"] // 2 + hc + T, [1, 16 * hss])
        base = c["inter"] // 2 + hc + T + 16 * hss
        for nm, cols, rows in [("alr", 4 * hss, 1), ("cgr", 8 * hss, 1),
                               ("eav", 128, hss), ("dta", 128, hss), ("afm", 128, hss),
                               ("aT", hss, 128), ("aTn", hss, 128), ("dw", hss, 128),
                               ("wexp", hss, 128), ("sst", 128, 128), ("emt", 128, 128),
                               ("bres", dch * 3, 128), ("facc", 16, hss),
                               ("onest_s", 128, hss), ("convw_s", cvch * c["K"], 128),
                               ("convb_s", cvch, 128), ("bdl", dch * 4 * 6, 128)]:
            al(nm, "uf32", base, [rows, cols])
            base += cols
        assert base <= UF, base
        al("dn", "uf32", 0, [1, c["heads0"] * hc], BF16)
        al("attM", "uf32", c["heads0"] * hc // 2, [128, 40])
        al("arow", "uf32", c["heads0"] * hc // 2 + 40, [1, 64])
        al("qsq", "uf32", c["heads0"] * hc // 2 + 104, [128, hc])
        al("qidx_s", "uf32", c["heads0"] * hc // 2 + 104 + hc, [128, hc])
        al("chmx", "uf32", 0, [128, hc])
        al("chmn", "uf32", hc, [128, hc])
        al("wld", "uf32", 0, [128, hc])
        al("wld2", "uf32", hc, [128, hc])
        al("wi32", "uf32", 2 * hc, [128, hc], I32)
        al("wfq", "uf32", 6 * hc, [128, 2 * hc], BF16)
        al("dxt", "big", 0, [128, c["inter"]])
        al("dxw", "big", c["inter"], [128, c["inter"]])
        CP0 = 2 * cvch * hh
        al("cp", "big", CP0, [128, T])
        al("ssm_sb", "big", CP0 + T, [128, c["inter"]])
        al("ssm_s", "big", (CP0 + T + c["inter"]) // 2, [128, c["inter"]], F32)
        assert CP0 + T + 3 * c["inter"] <= big_cols, (CP0, big_cols)
        assert 2 * c["inter"] <= cvch * hh, (c["inter"], cvch * hh)
        for i in range(6):
            b.psum(f"ps{i}", [128, 512], F32)
        b.psum("pst", [128, 1024], BF16)
        b.psum("pstf", [128, 512], F32)
        for s in ["t_m", "v_m", "s_m", "g_m", "d_c", "d_a", "d_as", "d_b", "d_s",
                  "d_y", "d_kv", "d_kv2", "d_q", "d_kg", "d_v0", "d_v1", "d_v2", "d_v3",
                  "d_sm1", "d_sm2", "d_g", "c_c"]:
            b.sem(s)

    def T(self, fn, waits=(), inc=True):
        return self.b.op("tensor", fn, waits, "t_m" if inc else None)

    def V(self, fn, waits=(), inc=True):
        return self.b.op("vector", fn, waits, "v_m" if inc else None)

    def S(self, fn, waits=(), inc=True):
        return self.b.op("scalar", fn, waits, "s_m" if inc else None)

    def G(self, fn, waits=(), inc=True):
        return self.b.op("gpsimd", fn, waits, "g_m" if inc else None)

    CHQ = {"d_as": "scalar"}

    def D(self, fn, waits=(), ch="d_a"):
        return self.b.op(self.CHQ.get(ch, "sync"), fn, waits, ch, inc_k=16)

    def GD(self, fn, waits=()):
        return self.b.op("gpsimd", fn, waits, "d_g", inc_k=16)

    def CC(self, fn, waits=()):
        t = self.b.op("gpsimd", fn, list(waits) + [self.tok.get("cc_prev")], "c_c", inc_k=1)
        self.tok["cc_prev"] = t
        return t

    def pbc(self, bank, W, srcf, dstf=None, waits=(), cwaits=(), bf16=False, P=128):
        ones = (lambda H: H["ones_s"][0:1, 0:1].broadcast_to([1, P])) if bf16 \
            else (lambda H: H["identf_s"][0:1, 0:1].broadcast_to([1, P]))
        mm = self.T(lambda e, H: e.matmul(
            H[bank][0:P, 0:W], ones(H), srcf(H), start=True, stop=True),
            waits=list(waits) + [self.bankfree.get(bank), self.tok["consts"]])
        if dstf is None:
            return mm, None
        cp = self.V(lambda e, H: e.tensor_copy(dstf(H), H[bank][0:P, 0:W]),
                    waits=[mm] + list(cwaits))
        self.bankfree[bank] = cp
        return mm, cp

    def prologue(self):
        b, c = self.b, self.c
        d, T, hc, dch = c["d"], c["T"], c["hc"], c["dch"]

        consts = [("ident_s", "ident"),
                  ("ssdm_s", "ssdmask"), ("ones_s", "onesb"), ("cc_s", "cconst"),
                  ("normw_s", "normw"),
                  ("identf_s", "identf"),
                  ("qpos_s", "qposr"), ("dcolx_s", "dcolx"),
                  ("smk_s", "smk"), ("bsel_s", "bsel")]
        for dst, src in consts:
            self.D(lambda e, H, dst=dst, src=src: e.dma_start(out=H[dst][:], in_=self.dram_ap(src)),
                   ch="d_c")
        self.D(lambda e, H: e.dma_start(out=H["smv_s"][:], in_=self.dram_ap("smallv")), ch="d_c")
        for ch in range(dch):
            self.D(lambda e, H, ch=ch: e.dma_start(
                out=H["res"][:, ch * T:(ch + 1) * T],
                in_=self.dram_ap("xs")[ch * 128:(ch + 1) * 128, :]), ch="d_c")
        self.tok["consts"] = b.barrier_token("d_c")
        self.tok["x"] = self.tok["consts"]

        tc = self.S(lambda e, H: e.activation(H["smv_s"][:, 3:4], H["smv_s"][:, 1:2], AF.Exp),
                    waits=[self.tok["consts"]])
        self.tok["acol"] = self.V(lambda e, H: e.tensor_scalar(
            H["smv_s"][:, 1:2], H["smv_s"][:, 3:4], -1.0, None, ALU.mult, ALU.bypass),
            waits=[tc])



        self.WMETA = [
            ("l0w1", "l0w1t_s", "i_l0w1", "q_l0w1", d // 8, c["dff0"], d * c["dff0"]),
            ("l0w2", "l0w2t_s", "i_l0w2", "q_l0w2", c["dff0"] // 8, d, d * c["dff0"]),
            ("l1w1", "l1w1t_s", "i_l1w1", "q_l1w1", d // 8, c["dff1"], d * c["dff1"]),
            ("l1w3", "l1w3t_s", "i_l1w3", "q_l1w3", d // 8, c["dff1"], d * c["dff1"]),
            ("l1w2", "l1w2t_s", "i_l1w2", "q_l1w2", c["dff1"] // 8, d, d * c["dff1"]),
        ]
        CW = c["hc"]
        vdone = [None, None]
        tz = self.V(lambda e, H: e.memset(H["sml"][:, 0:64], 0.0), waits=[])
        last_ta = tz
        k = 0
        for j, (nm, snm, inm, qnm, rows, cols, nw) in enumerate(self.WMETA):
            for r0 in range(0, rows, 128):
                rh = min(128, rows - r0)
                for c0 in range(0, cols, CW):
                    cw = min(CW, cols - c0)
                    buf = "wld" if (k % 2 == 0) else "wld2"
                    tl = self.D(lambda e, H, buf=buf, snm=snm, r0=r0, rh=rh, c0=c0, cw=cw:
                                e.dma_start(out=H[buf][0:rh, 0:cw],
                                            in_=self.dram_ap(snm)[r0:r0 + rh, c0:c0 + cw]),
                                waits=[vdone[k % 2]], ch=("d_a" if k % 2 == 0 else "d_as"))
                    tr = self.V(lambda e, H, buf=buf, rh=rh, cw=cw: e.tensor_reduce(
                        H["sml"][0:rh, 8:9], H[buf][0:rh, 0:cw], axis=AX.X, op=ALU.add,
                        apply_absolute_value=True), waits=[tl, last_ta])
                    ta = self.V(lambda e, H, j=j, rh=rh: e.tensor_tensor(
                        H["sml"][0:rh, j:j + 1], H["sml"][0:rh, j:j + 1],
                        H["sml"][0:rh, 8:9], ALU.add), waits=[tr])
                    vdone[k % 2] = ta
                    last_ta = ta
                    k += 1
        tsum = b.barrier_token("v_m")
        tp = self.G(lambda e, H: e.reduce_sum(
            H["sml"][0:1, 16:24], H["sml"][:, 0:8], axis=AX.C),
            waits=[tsum])
        tg1 = self.GD(lambda e, H: e.dma_start(out=self.dram_ap("arin"),
                                               in_=H["sml"][0:1, 16:24]), waits=[tp])
        tar = self.CC(lambda e, H: e.collective_compute(
            "AllReduce", ALU.add, replica_groups=[list(range(c["nc"]))],
            ins=[self.dram_h("arin").ap()], outs=[self.dram_h("arout").ap()]),
            waits=[tg1])
        tg2 = self.GD(lambda e, H: e.dma_start(out=H["sml"][0:1, 24:32],
                                               in_=self.dram_ap("arout")), waits=[tar])
        t2s = []
        for j, (nm, snm, inm, qnm, rows, cols, nw) in enumerate(self.WMETA):
            t1 = self.V(lambda e, H, j=j, nw=nw: e.tensor_scalar(
                H["sml"][0:1, 32 + j:33 + j], H["sml"][0:1, 24 + j:25 + j],
                1.0 / nw, 1e-5, ALU.mult, ALU.max), waits=[tg2])
            t2 = self.V(lambda e, H, j=j: e.reciprocal(
                H["sml"][0:1, 40 + j:41 + j], H["sml"][0:1, 32 + j:33 + j]), waits=[t1])
            t2s.append(t2)
        _, tb2 = self.pbc("ps0", 16, lambda H: H["sml"][0:1, 32:48],
                          lambda H: H["sml"][:, 48:64], waits=t2s)
        self.tok["scales"] = tb2

        wfq_free = [None]
        prev = {"t2": None, "t3": None}
        def quant_matrix(j, nm, snm, inm, qnm, rows, cols):
            for r0 in range(0, rows, 128):
                rh = min(128, rows - r0)
                for c0 in range(0, cols, CW):
                    cw = min(CW, cols - c0)
                    buf = "wld" if (self._qk % 2 == 0) else "wld2"
                    tl = self.D(lambda e, H, buf=buf, snm=snm, r0=r0, rh=rh, c0=c0, cw=cw:
                                e.dma_start(out=H[buf][0:rh, 0:cw],
                                            in_=self.dram_ap(snm)[r0:r0 + rh, c0:c0 + cw]),
                                waits=[self._qv[self._qk % 2]],
                                ch=("d_a" if self._qk % 2 == 0 else "d_as"))
                    t1 = self.V(lambda e, H, buf=buf, j=j, rh=rh, cw=cw: e.tensor_scalar(
                        H["wi32"][0:rh, 0:cw], H[buf][0:rh, 0:cw],
                        H["sml"][0:rh, 56 + j:57 + j], RND, ALU.mult, ALU.add),
                        waits=[tl, self.tok["scales"], prev["t2"]])
                    self._qv[self._qk % 2] = t1
                    t2 = self.V(lambda e, H, rh=rh, cw=cw: e.tensor_scalar(
                        H["wsp"][0:rh, 0:cw], H["wi32"][0:rh, 0:cw],
                        NMAG, 1.0, ALU.add, ALU.min), waits=[t1, prev["t3"]])
                    prev["t2"] = t2
                    t3 = self.V(lambda e, H, rh=rh, cw=cw: e.tensor_scalar(
                        H["wfq"][0:rh, 0:cw], H["wsp"][0:rh, 0:cw],
                        -1.0, None, ALU.max, ALU.bypass), waits=[t2, wfq_free[0]])
                    prev["t3"] = t3
                    ts = self.D(lambda e, H, inm=inm, r0=r0, rh=rh, c0=c0, cw=cw:
                                e.dma_start(out=self.dram_ap(inm)[r0:r0 + rh, c0:c0 + cw],
                                            in_=H["wfq"][0:rh, 0:cw]), waits=[t3], ch="d_s")
                    wfq_free[0] = ts
                    self._qk += 1
            tag = self.CC(lambda e, H, inm=inm, qnm=qnm: e.collective_compute(
                "AllGather", ALU.bypass, replica_groups=[list(range(c["nc"]))],
                ins=[self.dram_h(inm).ap()], outs=[self.dram_h(qnm).ap()]),
                waits=[wfq_free[0]])
            self.tok[qnm] = tag

        def cast_matrix(snm, inm, qnm, rows, cols):
            for r0 in range(0, rows, 128):
                rh = min(128, rows - r0)
                for c0 in range(0, cols, CW):
                    cw = min(CW, cols - c0)
                    buf = "wld" if (self._qk % 2 == 0) else "wld2"
                    tl = self.D(lambda e, H, buf=buf, snm=snm, r0=r0, rh=rh, c0=c0, cw=cw:
                                e.dma_start(out=H[buf][0:rh, 0:cw],
                                            in_=self.dram_ap(snm)[r0:r0 + rh, c0:c0 + cw]),
                                waits=[self._qv[self._qk % 2]],
                                ch=("d_a" if self._qk % 2 == 0 else "d_as"))
                    t3 = self.V(lambda e, H, buf=buf, rh=rh, cw=cw: e.tensor_copy(
                        H["wfq"][0:rh, 0:cw], H[buf][0:rh, 0:cw]), waits=[tl, wfq_free[0]])
                    self._qv[self._qk % 2] = t3
                    ts = self.D(lambda e, H, inm=inm, r0=r0, rh=rh, c0=c0, cw=cw:
                                e.dma_start(out=self.dram_ap(inm)[r0:r0 + rh, c0:c0 + cw],
                                            in_=H["wfq"][0:rh, 0:cw]), waits=[t3], ch="d_s")
                    wfq_free[0] = ts
                    self._qk += 1
            tag = self.CC(lambda e, H, inm=inm, qnm=qnm: e.collective_compute(
                "AllGather", ALU.bypass, replica_groups=[list(range(c["nc"]))],
                ins=[self.dram_h(inm).ap()], outs=[self.dram_h(qnm).ap()]),
                waits=[wfq_free[0]])
            self.tok[qnm] = tag

        self._qk = 0
        self._qv = [vdone[0], vdone[1]]
        for j in (0, 1):
            nm, snm, inm, qnm, rows, cols, nw = self.WMETA[j]
            quant_matrix(j, nm, snm, inm, qnm, rows, cols)
        cast_matrix("l0att_s", "i_l0att", "q_l0att", d // 8, 4 * d)
        for j in (2, 3, 4):
            nm, snm, inm, qnm, rows, cols, nw = self.WMETA[j]
            quant_matrix(j, nm, snm, inm, qnm, rows, cols)
        cast_matrix("l1zx_s", "i_l1zx", "q_l1zx", d // 8, c["inter"] + c["convdim"])
        cast_matrix("l1wo_s", "i_l1wo", "q_l1wo", c["inter"] // 8, d)
        self.tok["wfq_free"] = wfq_free[0]
        self.tok["uf_free"] = [b.barrier_token(s) for s in ("v_m", "s_m", "g_m", "t_m")] + \
                              ([wfq_free[0]] if wfq_free[0] else [])

    def dram_h(self, name):
        return self.b._dram[name]

    def dram_ap(self, name):
        return self.b._dram[name].ap()

    def ln_quant(self, t0, quant, wj=None, res_tok=None):
        b, c = self.b, self.c
        d, hc, dch, T = c["d"], c["hc"], c["dch"], c["T"]
        W = hc
        stats_mm = []
        pv = {"cast": [None, None], "sq": [None, None], "mx": None, "mn": None,
              "m1": [self.tok.get("stg_free"), self.tok.get("stg_free")],
              "m2": [self.tok.get("stg_free"), self.tok.get("stg_free")]}
        for ch in range(dch):
            sl = 0
            o0, o1 = 0, W
            rc = (slice(None), slice(ch * T + t0, ch * T + t0 + W))
            tcast = self.V(lambda e, H, rc=rc, o0=o0: e.tensor_copy(
                H["stg"][:, o0:o0 + W], H["res"][rc]),
                waits=[res_tok, pv["sq"][sl], self.tok["consts"], pv["m1"][sl],
                       self.tok.get("stg_free2"), self.tok.get("stg_free3")])
            tsq = self.V(lambda e, H, o0=o0, o1=o1: e.tensor_tensor(
                H["stg"][:, o1:o1 + W], H["stg"][:, o0:o0 + W],
                H["stg"][:, o0:o0 + W], ALU.mult), waits=[tcast, pv["m2"][sl]])
            pv["sq"][sl] = tsq
            m1 = self.T(lambda e, H, ch=ch, o0=o0: e.matmul(
                H["ps5"][0:1, 0:W], H["ones_s"][:, 0:1], H["stg"][:, o0:o0 + W],
                start=(ch == 0), stop=(ch == dch - 1)),
                waits=[tcast, self.tok["consts"]] + ([self.bankfree.get("ps5")] if ch == 0 else []))
            m2 = self.T(lambda e, H, ch=ch, o1=o1: e.matmul(
                H["ps4"][0:1, 0:W], H["ones_s"][:, 0:1], H["stg"][:, o1:o1 + W],
                start=(ch == 0), stop=(ch == dch - 1)),
                waits=[tsq] + ([self.bankfree.get("ps4")] if ch == 0 else []))
            stats_mm += [m1, m2]
            pv["m1"][sl] = m1
            pv["m2"][sl] = m2
            if quant:
                if ch == 0:
                    tmx = self.V(lambda e, H, rc=rc: e.tensor_copy(H["chmx"][:, 0:W], H["res"][rc]),
                                 waits=[res_tok, self.tok.get("chmx_free"),
                                        self.tok.get("uf_free")])
                    tmn = self.V(lambda e, H, rc=rc: e.tensor_copy(H["chmn"][:, 0:W], H["res"][rc]),
                                 waits=[res_tok, self.tok.get("chmx_free"),
                                        self.tok.get("uf_free")])
                else:
                    tmx = self.V(lambda e, H, rc=rc: e.tensor_tensor(
                        H["chmx"][:, 0:W], H["chmx"][:, 0:W], H["res"][rc], ALU.max),
                        waits=[pv["mx"]])
                    tmn = self.V(lambda e, H, rc=rc: e.tensor_tensor(
                        H["chmn"][:, 0:W], H["chmn"][:, 0:W], H["res"][rc], ALU.min),
                        waits=[pv["mn"]])
                pv["mx"], pv["mn"] = tmx, tmn
        tS1, tS2 = stats_mm[-2], stats_mm[-1]
        self.tok["stg_free"] = self.b.barrier_token("t_m")
        t_mean = self.V(lambda e, H: e.tensor_scalar(H["rb"][0:1, 0 * W:0 * W + W], H["ps5"][0:1, 0:W],
                                                     1.0 / d, None, ALU.mult, ALU.bypass),
                        waits=[tS1])
        t_ex2 = self.V(lambda e, H: e.tensor_scalar(H["rb"][0:1, 1 * W:1 * W + W], H["ps4"][0:1, 0:W],
                                                    1.0 / d, None, ALU.mult, ALU.bypass),
                       waits=[tS2])
        self.bankfree["ps5"] = t_mean
        self.bankfree["ps4"] = t_ex2
        t_m2 = self.V(lambda e, H: e.tensor_tensor(H["rb"][0:1, 2 * W:2 * W + W], H["rb"][0:1, 0 * W:0 * W + W],
                                                   H["rb"][0:1, 0 * W:0 * W + W], ALU.mult), waits=[t_mean])
        t_var = self.V(lambda e, H: e.tensor_tensor(H["rb"][0:1, 2 * W:2 * W + W], H["rb"][0:1, 1 * W:1 * W + W],
                                                    H["rb"][0:1, 2 * W:2 * W + W], ALU.subtract),
                       waits=[t_ex2, t_m2])
        t_ln = self.S(lambda e, H: e.activation(H["rb"][0:1, 1 * W:1 * W + W], H["rb"][0:1, 2 * W:2 * W + W], AF.Ln,
                                                bias=H["cc_s"][0:1, 0:1], scale=1.0),
                      waits=[t_var])
        t_rstd = self.S(lambda e, H: e.activation(H["rb"][0:1, 1 * W:1 * W + W], H["rb"][0:1, 1 * W:1 * W + W], AF.Exp,
                                                  scale=-0.5), waits=[t_ln])
        _, t_bm = self.pbc("ps5", W, lambda H: H["rb"][0:1, 0 * W:0 * W + W],
                           lambda H: H["bc"][:, 0:W], waits=[t_mean])
        _, t_brs = self.pbc("ps4", W, lambda H: H["rb"][0:1, 1 * W:1 * W + W],
                            lambda H: H["bc"][:, W:2 * W], waits=[t_rstd])
        B0 = (slice(None), slice(0, W))
        B1 = (slice(None), slice(W, 2 * W))
        B2 = (slice(None), slice(2 * W, 3 * W))
        B3 = (slice(None), slice(3 * W, 4 * W))
        B4 = (slice(None), slice(4 * W, 5 * W))
        B5 = None
        if quant:
            t1 = self.V(lambda e, H: e.tensor_tensor(H["chmx"][:, 0:W], H["chmx"][:, 0:W],
                                                     H["bc"][B0], ALU.subtract),
                        waits=[pv["mx"], t_bm])
            t2 = self.V(lambda e, H: e.tensor_tensor(H["chmn"][:, 0:W], H["bc"][B0],
                                                     H["chmn"][:, 0:W], ALU.subtract),
                        waits=[pv["mn"], t_bm])
            t3 = self.V(lambda e, H: e.tensor_tensor(H["chmx"][:, 0:W], H["chmx"][:, 0:W],
                                                     H["chmn"][:, 0:W], ALU.max), waits=[t1, t2])
            t4 = self.V(lambda e, H: e.tensor_tensor(H["chmx"][:, 0:W], H["chmx"][:, 0:W],
                                                     H["bc"][B1], ALU.mult), waits=[t3, t_brs])
            t5 = self.V(lambda e, H: e.tensor_scalar(H["chmx"][:, 0:W], H["chmx"][:, 0:W],
                                                     1e-5, None, ALU.max, ALU.bypass), waits=[t4])
            t6a = self.G(lambda e, H: e.reduce_max(
                H["chmn"][0:1, 0:W], H["chmx"][:, 0:W], axis=AX.C),
                waits=[t5, t3])
            self.tok["chmx_free"] = t6a
            _, t6 = self.pbc("ps5", W, lambda H: H["chmn"][0:1, 0:W],
                             lambda H: H["chmn"][:, 0:W], waits=[t6a])
            t7 = self.V(lambda e, H: e.tensor_scalar(H["bc"][B4], H["chmn"][:, 0:W], 1.0 / 127.0,
                                                     None, ALU.mult, ALU.bypass), waits=[t6])
            t8 = self.V(lambda e, H: e.reciprocal(H["chmn"][:, 0:W], H["chmn"][:, 0:W]), waits=[t7, t6])
            t9 = self.V(lambda e, H: e.tensor_scalar(H["bc"][B2], H["chmn"][:, 0:W], 127.0, None,
                                                     ALU.mult, ALU.bypass), waits=[t8])
            t10 = self.V(lambda e, H: e.tensor_tensor(H["bc"][B2], H["bc"][B2], H["bc"][B1],
                                                      ALU.mult), waits=[t9, t_brs])
            t11 = self.V(lambda e, H: e.tensor_tensor(H["bc"][B3], H["bc"][B0], H["bc"][B2],
                                                      ALU.mult), waits=[t10, t_bm])
            prevq = {"p1": None, "p2": None, "p3": None}
            self.tok["ln_act_toks"] = []
            for ch in range(dch):
                rc = (slice(None), slice(ch * T + t0, ch * T + t0 + W))
                ac = (slice(None), slice(ch * hc, (ch + 1) * hc))
                p1 = self.V(lambda e, H, rc=rc: e.scalar_tensor_tensor(
                    H["wsp"][:, 0:W], H["res"][rc], 1.0, H["bc"][B2], ALU.mult, ALU.mult),
                    waits=[t10, prevq["p2"], res_tok])
                p2 = self.V(lambda e, H: e.scalar_tensor_tensor(
                    H["wi32"][:, 0:W], H["wsp"][:, 0:W], RND, H["bc"][B3],
                    ALU.add, ALU.subtract), waits=[p1, t11, prevq["p3"]])
                prevq["p2"] = p2
                p3 = self.V(lambda e, H, ac=ac: e.tensor_scalar(
                    H["act"][ac], H["wi32"][:, 0:W], NMAG, None, ALU.add, ALU.bypass),
                    waits=[p2, self.tok.get("act_free")])
                prevq["p3"] = p3
                self.tok["ln_act_toks"].append(p3)
            return b.barrier_token("v_m")
        else:
            prevq = {"p1": None, "p2": None}
            for ch in range(dch):
                rc = (slice(None), slice(ch * T + t0, ch * T + t0 + W))
                ac = (slice(None), slice(ch * hc, (ch + 1) * hc))
                p1 = self.V(lambda e, H, rc=rc: e.scalar_tensor_tensor(
                    H["wsp"][:, 0:W], H["res"][rc], 1.0, H["bc"][B0], ALU.mult, ALU.subtract),
                    waits=[t_bm, prevq["p2"], res_tok])
                p2 = self.V(lambda e, H, ac=ac: e.tensor_tensor(
                    H["act"][ac], H["wsp"][:, 0:W], H["bc"][B1], ALU.mult),
                    waits=[p1, t_brs, self.tok.get("act_free")])
                prevq["p2"] = p2
            return b.barrier_token("v_m")

    def ffn(self, layer):
        b, c = self.b, self.c
        d, hc, dch, T = c["d"], c["hc"], c["dch"], c["T"]
        if layer == 0:
            dff, tw = c["dff0"], hc
            qw1, qw2, qw3 = "q_l0w1", "q_l0w2", None
            j1, j2, j3 = 0, 1, None
        else:
            dff, tw = c["dff1"], hc // 2
            qw1, qw2, qw3 = "q_l1w1", "q_l1w2", "q_l1w3"
            j1, j2, j3 = 2, 4, 3
        fch = dff // 128
        A, Bk = ["ps0", "ps1", "ps2"], ["ps3", "ps4", "ps5"]
        wrot = ["wa", "wb"]
        wfree = self.wfree
        wsl = {}
        hc_ = hc

        def wsp_slot(i):
            return (slice(None), slice(i * 512, i * 512 + tw))

        for tt in range(T // hc):
            res_tok = self.tok.get("res")
            lt = self.ln_quant(tt * hc, True, wj=j1, res_tok=res_tok)
            lntoks = list(self.tok["ln_act_toks"])
            for p in range(hc // tw):
                hq_off = 0
                h3_off = fch * tw
                chain = {"mx": None, "mn": None}

                def h_sweep(qname, wj, mode):
                    for gi, fc0 in enumerate(range(0, fch, 3)):
                        gsz = min(3, fch - fc0)
                        banks = A if gi % 2 == 0 else Bk
                        mlast = {}
                        for ich in range(0, dch, 4):
                            ics = min(4, dch - ich)
                            wbuf = wrot[self._dk % 2]
                            wch = "d_a" if self._dk % 2 == 0 else "d_b"
                            self._dk += 1
                            tl = self.D(lambda e, H, wbuf=wbuf, qname=qname, ich=ich, ics=ics,
                                        fc0=fc0, gsz=gsz: e.dma_start(
                                out=H[wbuf][:, 0:ics * gsz * 128].rearrange(
                                    "p (i f) -> p i f", i=ics),
                                in_=self.dram_ap(qname).rearrange(
                                    "(i p) f -> p i f", p=128)[:, ich:ich + ics,
                                    fc0 * 128:(fc0 + gsz) * 128]),
                                waits=[wfree[wbuf], self.tok[qname]], ch=wch)
                            for ii in range(ics):
                                ic = ich + ii
                                for fi in range(gsz):
                                    bk = banks[fi]
                                    wt = [tl, lntoks[ic]]
                                    if ic == 0:
                                        wt.append(self.bankfree.get(bk))
                                    mm = self.T(lambda e, H, bk=bk, wbuf=wbuf, ii=ii, fi=fi,
                                                gsz=gsz, ic=ic, p=p: e.matmul(
                                        H[bk][:, 0:tw],
                                        H[wbuf][:, (ii * gsz + fi) * 128:(ii * gsz + fi + 1) * 128],
                                        H["act"][:, ic * hc_ + p * tw: ic * hc_ + p * tw + tw],
                                        start=(ic == 0), stop=(ic == dch - 1)),
                                        waits=wt)
                                    mlast[fi] = mm
                            wfree[wbuf] = mlast[gsz - 1]
                        for fi in range(gsz):
                            fc = fc0 + fi
                            bk = banks[fi]
                            sl = fi % 2
                            if mode == "h3":
                                ev = self.V(lambda e, H, bk=bk, fc=fc, wj=wj, p=p: e.scalar_tensor_tensor(
                                    H["big"][:, h3_off + fc * tw: h3_off + (fc + 1) * tw],
                                    H[bk][:, 0:tw], H["sml"][:, 48 + wj:49 + wj],
                                    H["bc"][:, 4 * hc_ + p * tw: 4 * hc_ + p * tw + tw],
                                    ALU.mult, ALU.mult),
                                    waits=[mlast[fi], lt, self.tok["scales"],
                                           self.tok.get("big_free")])
                                self.bankfree[bk] = ev
                            else:
                                e1 = self.V(lambda e, H, bk=bk, sl=sl, wj=wj, p=p: e.scalar_tensor_tensor(
                                    H["wsp"][wsp_slot(sl)], H[bk][:, 0:tw],
                                    H["sml"][:, 48 + wj:49 + wj],
                                    H["bc"][:, 4 * hc_ + p * tw: 4 * hc_ + p * tw + tw],
                                    ALU.mult, ALU.mult),
                                    waits=[mlast[fi], lt, self.tok["scales"], wsl.get(("s", sl))])
                                self.bankfree[bk] = e1
                                e2 = self.S(lambda e, H, sl=sl: e.activation(
                                    H["wsp"][wsp_slot(2 + sl)], H["wsp"][wsp_slot(sl)], AF.Silu),
                                    waits=[e1, wsl.get(("u", sl))])
                                wsl[("s", sl)] = e2
                                if mode == "l0":
                                    e3 = self.V(lambda e, H, sl=sl, fc=fc: e.tensor_copy(
                                        H["big"][:, hq_off + fc * tw: hq_off + (fc + 1) * tw],
                                        H["wsp"][wsp_slot(2 + sl)]),
                                        waits=[e2, self.tok.get("big_free")])
                                else:
                                    e3 = self.V(lambda e, H, sl=sl, fc=fc: e.tensor_tensor(
                                        H["big"][:, hq_off + fc * tw: hq_off + (fc + 1) * tw],
                                        H["wsp"][wsp_slot(2 + sl)],
                                        H["big"][:, h3_off + fc * tw: h3_off + (fc + 1) * tw],
                                        ALU.mult), waits=[e2])
                                wsl[("u", sl)] = e3
                                if chain["mx"] is None:
                                    cmx = self.V(lambda e, H, fc=fc: e.tensor_copy(
                                        H["chmx"][:, 0:tw],
                                        H["big"][:, hq_off + fc * tw: hq_off + (fc + 1) * tw]),
                                        waits=[e3, self.tok.get("chmx_free")])
                                    cmn = self.V(lambda e, H, fc=fc: e.tensor_copy(
                                        H["chmn"][:, 0:tw],
                                        H["big"][:, hq_off + fc * tw: hq_off + (fc + 1) * tw]),
                                        waits=[e3, self.tok.get("chmx_free")])
                                else:
                                    cmx = self.V(lambda e, H, fc=fc: e.tensor_tensor(
                                        H["chmx"][:, 0:tw], H["chmx"][:, 0:tw],
                                        H["big"][:, hq_off + fc * tw: hq_off + (fc + 1) * tw],
                                        ALU.max), waits=[e3, chain["mx"]])
                                    cmn = self.V(lambda e, H, fc=fc: e.tensor_tensor(
                                        H["chmn"][:, 0:tw], H["chmn"][:, 0:tw],
                                        H["big"][:, hq_off + fc * tw: hq_off + (fc + 1) * tw],
                                        ALU.min), waits=[e3, chain["mn"]])
                                chain["mx"], chain["mn"] = cmx, cmn

                if layer == 1:
                    h_sweep(qw3, j3, "h3")
                    h_sweep(qw1, j1, "h1")
                else:
                    h_sweep(qw1, j1, "l0")

                if os.environ.get("KDBG_F1") and layer == 1 and tt == 0:
                    hbar = self.b.barrier_token("v_m")
                    self.D(lambda e, H, p=p: e.dma_start(
                        out=self.dram_ap("dbg_h")[:, p * 12288:p * 12288 + fch * tw],
                        in_=H["big"][:, hq_off:hq_off + fch * tw]),
                        waits=[hbar], ch="d_y")
                BH_IS = (slice(None), slice(3 * hc_, 3 * hc_ + tw))
                self.tok["act_free"] = self.b.barrier_token("t_m")
                q2 = self.V(lambda e, H: e.tensor_scalar(
                    H["chmn"][:, 0:tw], H["chmn"][:, 0:tw], -1.0, None, ALU.mult, ALU.bypass),
                    waits=[chain["mn"]])
                q3 = self.V(lambda e, H: e.tensor_tensor(
                    H["chmx"][:, 0:tw], H["chmx"][:, 0:tw], H["chmn"][:, 0:tw], ALU.max),
                    waits=[q2, chain["mx"]])
                q4 = self.V(lambda e, H: e.tensor_scalar(
                    H["chmx"][:, 0:tw], H["chmx"][:, 0:tw], 1e-5, None, ALU.max, ALU.bypass),
                    waits=[q3])
                q5a = self.G(lambda e, H: e.reduce_max(
                    H["chmn"][0:1, 0:tw], H["chmx"][:, 0:tw], axis=AX.C),
                    waits=[q4, q3])
                self.tok["chmx_free"] = q5a
                _, q5 = self.pbc("ps0", tw, lambda H: H["chmn"][0:1, 0:tw],
                                 lambda H: H["chmn"][:, 0:tw], waits=[q5a])
                q6 = self.V(lambda e, H: e.tensor_scalar(
                    H["bc"][BH_IS], H["chmn"][:, 0:tw], 0.5 / 127.0, None, ALU.mult, ALU.bypass),
                    waits=[q5])
                q7 = self.V(lambda e, H: e.reciprocal(H["chmn"][:, 0:tw], H["chmn"][:, 0:tw]),
                            waits=[q6, q5])
                q8 = self.V(lambda e, H: e.tensor_scalar(
                    H["chmn"][:, 0:tw], H["chmn"][:, 0:tw], 127.0, None, ALU.mult, ALU.bypass),
                    waits=[q7])
                pq = {"p1": None, "p2": None, "p3": None}
                hq_toks = []
                for fc in range(fch):
                    p1 = self.V(lambda e, H, fc=fc: e.scalar_tensor_tensor(
                        H["wsp"][wsp_slot(0)],
                        H["big"][:, hq_off + fc * tw: hq_off + (fc + 1) * tw], 1.0,
                        H["chmn"][:, 0:tw], ALU.mult, ALU.mult),
                        waits=[q8, pq["p2"]])
                    p2 = self.V(lambda e, H: e.tensor_scalar(
                        H["wi32"][:, 0:tw], H["wsp"][wsp_slot(0)], RND, None,
                        ALU.add, ALU.bypass), waits=[p1, pq["p3"]])
                    pq["p2"] = p2
                    p3 = self.V(lambda e, H, fc=fc: e.tensor_scalar(
                        H["big"][:, hq_off + fc * tw: hq_off + (fc + 1) * tw],
                        H["wi32"][:, 0:tw], NMAG, None, ALU.add, ALU.bypass), waits=[p2])
                    pq["p3"] = p3
                    hq_toks.append(p3)
                hq_tok = self.b.barrier_token("v_m")
                if os.environ.get("KDBG_F1") and layer == 1 and tt == 0:
                    self.D(lambda e, H, p=p: e.dma_start(
                        out=self.dram_ap("dbg_q")[:, p * 12288:p * 12288 + fch * tw],
                        in_=H["big"][:, hq_off:hq_off + fch * tw]),
                        waits=[hq_tok], ch="d_y")
                    self.D(lambda e, H, p=p: e.dma_start(
                        out=self.dram_ap("dbg_s")[:, p * 256:p * 256 + tw],
                        in_=H["bc"][BH_IS]), waits=[hq_tok, q6], ch="d_y")

                for gi, oc0 in enumerate(range(0, dch, 3)):
                    gsz = min(3, dch - oc0)
                    banks = A if gi % 2 == 0 else Bk
                    mlast = {}
                    for fcb in range(0, fch, 4):
                        fcs = min(4, fch - fcb)
                        wbuf = wrot[self._dk % 2]
                        wch = "d_a" if self._dk % 2 == 0 else "d_b"
                        self._dk += 1
                        tl = self.D(lambda e, H, wbuf=wbuf, fcb=fcb, fcs=fcs, oc0=oc0,
                                    gsz=gsz: e.dma_start(
                            out=H[wbuf][:, 0:fcs * gsz * 128].rearrange(
                                "p (i f) -> p i f", i=fcs),
                            in_=self.dram_ap(qw2).rearrange(
                                "(i p) f -> p i f", p=128)[:, fcb:fcb + fcs,
                                oc0 * 128:(oc0 + gsz) * 128]),
                            waits=[wfree[wbuf], self.tok[qw2]], ch=wch)
                        for ii in range(fcs):
                            fc = fcb + ii
                            for oi in range(gsz):
                                bk = banks[oi]
                                wt = [tl, hq_toks[fc]]
                                if fc == 0:
                                    wt.append(self.bankfree.get(bk))
                                mm = self.T(lambda e, H, bk=bk, wbuf=wbuf, ii=ii, oi=oi,
                                            gsz=gsz, fc=fc: e.matmul(
                                    H[bk][:, 0:tw],
                                    H[wbuf][:, (ii * gsz + oi) * 128:(ii * gsz + oi + 1) * 128],
                                    H["big"][:, hq_off + fc * tw: hq_off + (fc + 1) * tw],
                                    start=(fc == 0), stop=(fc == fch - 1)),
                                    waits=wt)
                                mlast[oi] = mm
                        wfree[wbuf] = mlast[gsz - 1]
                    for oi in range(gsz):
                        oc = oc0 + oi
                        bk = banks[oi]
                        sl = 2 + (oc % 2)
                        e1 = self.V(lambda e, H, bk=bk, sl=sl: e.scalar_tensor_tensor(
                            H["wsp"][wsp_slot(sl)], H[bk][:, 0:tw],
                            H["sml"][:, 48 + j2:49 + j2], H["bc"][BH_IS], ALU.mult, ALU.mult),
                            waits=[mlast[oi], self.tok["scales"], wsl.get(("o", sl))])
                        self.bankfree[bk] = e1
                        rc = (slice(None), slice(oc * T + tt * hc + p * tw,
                                                 oc * T + tt * hc + p * tw + tw))
                        e2 = self.V(lambda e, H, rc=rc, sl=sl: e.tensor_tensor(
                            H["res"][rc], H["res"][rc], H["wsp"][wsp_slot(sl)], ALU.add),
                            waits=[e1, lt])
                        wsl[("o", sl)] = e2
                self.tok["big_free"] = self.b.barrier_token("t_m")
                self.tok["uf_free"] = [self.b.barrier_token(s) for s in ("v_m", "s_m", "g_m", "t_m")]
            self.tok["res"] = self.b.barrier_token("v_m")

    def epilogue(self):
        c, T, dch = self.c, self.c["T"], self.c["dch"]
        rt = self.tok.get("res")
        for ch in range(dch):
            self.D(lambda e, H, ch=ch: e.dma_start(
                out=self.dram_ap("ys")[ch * 128:(ch + 1) * 128, :],
                in_=H["res"][:, ch * T:(ch + 1) * T]), waits=[rt], ch="d_y")
        fin = self.b.barrier_token("d_y")
        self.b.op("sync", lambda e, H: None, waits=[fin], inc=None)


FULL_STEPS = ("f0", "at", "f0", "f1", "ss", "f1")

def build_program(cfg, steps=FULL_STEPS):
    b = Builder(cfg)
    p = Prog(b)
    p.declare()
    p.prologue()
    for s in steps:
        if s == "f0":
            p.ffn(0)
        elif s == "f1":
            p.ffn(1)
        elif s == "at":
            p.attn()
        elif s == "ss":
            p.ssm()
    p.epilogue()
    nc = b.finalize()
    return nc, p


def _bf(x):
    return np.ascontiguousarray(x).astype(ml_dtypes.bfloat16)


def prep_inputs(cfg, inp):
    c = derived(cfg)
    d, hc, T, L = c["d"], c["hc"], c["T"], c["L"]
    ncores = c["nc"]
    K = c["K"]
    cvch = c["convdim"] // 128
    ich = c["inter"] // 128
    f32 = np.float32

    x = np.asarray(inp["L0_attn_x"], f32)
    w1t = np.ascontiguousarray(np.asarray(inp["L0_ffn_W1"], f32).T)
    w2t = np.ascontiguousarray(np.asarray(inp["L0_ffn_W2"], f32).T)
    att = np.ascontiguousarray(np.concatenate(
        [np.asarray(inp["L0_attn_w_q"], f32), np.asarray(inp["L0_attn_w_k"], f32),
         np.asarray(inp["L0_attn_w_v"], f32), np.asarray(inp["L0_attn_w_o"], f32)], axis=1))
    l1w1t = np.ascontiguousarray(np.asarray(inp["L1_ffn_w1"], f32).T)
    l1w3t = np.ascontiguousarray(np.asarray(inp["L1_ffn_w3"], f32).T)
    l1w2t = np.ascontiguousarray(np.asarray(inp["L1_ffn_w2"], f32).T)
    zx = np.ascontiguousarray(np.concatenate(
        [np.asarray(inp["L1_attn_W_z"], f32), np.asarray(inp["L1_attn_W_xBC"], f32)], axis=1))
    wout = np.ascontiguousarray(np.asarray(inp["L1_attn_W_out"], f32))
    wdt = np.ascontiguousarray(np.asarray(inp["L1_attn_W_dt"], f32))
    convw = np.ascontiguousarray(
        np.asarray(inp["L1_attn_conv_w"], f32).reshape(cvch, 128, K).transpose(1, 0, 2)
        .reshape(128, cvch * K))
    convb = np.ascontiguousarray(np.asarray(inp["L1_attn_conv_b"], f32).reshape(cvch, 128).T)
    smallv = np.stack([np.asarray(inp["L1_attn_dt_bias"], f32),
                       np.asarray(inp["L1_attn_A_log"], f32),
                       np.asarray(inp["L1_attn_D"], f32),
                       np.zeros(c["hssm"], f32)], axis=1)
    normw = np.ascontiguousarray(np.asarray(inp["L1_attn_norm_w"], f32).reshape(ich, 128).T)
    ident = np.eye(128)
    ssdm = np.where(np.arange(128)[:, None] <= np.arange(128)[None, :], 0.0, NEG).astype(f32)
    onesb = np.zeros((128, 8), f32); onesb[:, 0] = 1.0; onesb[:, 1] = 1.0
    cconst = np.zeros((128, 8), f32)
    cconst[:, 0] = c["eps"]; cconst[:, 1] = 1.0; cconst[:, 2] = 10.0
    cconst[:, 3] = np.arange(128); cconst[:, 4] = 1e-19
    cconst[:, 5] = -0.5 * np.log(128.0)
    qidxb = np.broadcast_to(np.arange(hc, dtype=f32)[None, :], (128, hc)).copy()

    dcolx = np.ascontiguousarray(
        np.repeat(np.asarray(inp["L1_attn_D"], f32), c["P"]).reshape(ich, 128).T)
    onest = np.ones((c["hssm"], 128), f32)
    s8 = d // 8
    in_maps = []
    for i in range(ncores):
        j = ncores - 1 - i
        smkr = np.zeros((1, 16), f32)
        smkr[0, 0:i] = 1.0
        smkr[0, 8:8 + j] = 1.0
        bselr = np.zeros((128, 16), f32)
        if i > 0:
            bselr[:, i - 1] = 1.0
        if j > 0:
            bselr[:, 8 + (ncores - 1 - (j - 1))] = 1.0

        xs = np.ascontiguousarray(
            np.concatenate([x[0, i * hc:(i + 1) * hc], x[1, j * hc:(j + 1) * hc]], 0).T)
        m = {
            "xs": xs,
            "l0w1t_s": w1t[i * s8:(i + 1) * s8],
            "l0w2t_s": w2t[i * (c["dff0"] // 8):(i + 1) * (c["dff0"] // 8)],
            "l0att_s": att[i * s8:(i + 1) * s8],
            "l1w1t_s": l1w1t[i * s8:(i + 1) * s8],
            "l1w3t_s": l1w3t[i * s8:(i + 1) * s8],
            "l1w2t_s": l1w2t[i * (c["dff1"] // 8):(i + 1) * (c["dff1"] // 8)],
            "l1zx_s": zx[i * s8:(i + 1) * s8],
            "l1wo_s": wout[i * (c["inter"] // 8):(i + 1) * (c["inter"] // 8)],
            "l1wdt": wdt,
            "convw": convw, "convb": convb, "smallv": smallv, "normw": normw,
            "ident": _bf(ident), "identf": ident.astype(f32),
            "ssdmask": ssdm, "onesb": _bf(onesb), "cconst": cconst,
            "qidxb": qidxb,
            "qposr": np.array([[i * hc, j * hc]], f32),
            "dcolx": dcolx, "onest": onest, "smk": smkr, "bsel": bselr,
        }
        m = {k: np.ascontiguousarray(v) for k, v in m.items()}
        in_maps.append(m)
    return in_maps


def assemble_output(cfg, results):
    c = derived(cfg)
    d, hc, L = c["d"], c["hc"], c["L"]
    ncores = c["nc"]
    out = np.empty((2, L, d), np.float32)
    for i in range(ncores):
        j = ncores - 1 - i
        y = np.asarray(results[i]["ys"]).T
        out[0, i * hc:(i + 1) * hc] = y[0:hc]
        out[1, j * hc:(j + 1) * hc] = y[hc:2 * hc]
    return out


def kernel(**inputs):
    cfg = real_cfg()
    nc, _ = build_program(cfg)
    in_maps = prep_inputs(cfg, inputs)
    res = run_bass_kernel_spmd(nc, in_maps, core_ids=list(range(8)))
    return assemble_output(cfg, res.results)


def _attn(self):
    b, c = self.b, self.c
    d, hc, dch, T = c["d"], c["hc"], c["dch"], c["T"]
    nh = c["heads0"]
    nkt = hc // 128
    wrot = ["wa", "wb"]
    wfree = self.wfree
    csc = 1.0 / np.sqrt(128.0)
    Q0 = 0
    KG = nh * hc
    VG = KG + dch * hc
    PP = VG + nkt * d
    M0 = PP + 2 * hc
    assert M0 + nkt * hc <= self.big_cols

    def qkv_proj(s, lt):
        t0 = s * hc
        for proj in (0, 1):
            for gi, oc0 in enumerate(range(0, dch, 2)):
                gsz = min(2, dch - oc0)
                banks = ["ps0", "ps1"] if gi % 2 == 0 else ["ps2", "ps3"]
                mlast = {}
                for ich in range(0, dch, 4):
                    ics = min(4, dch - ich)
                    wbuf = wrot[self._dk % 2]
                    wch = "d_a" if self._dk % 2 == 0 else "d_b"
                    self._dk += 1
                    tl = self.D(lambda e, H, wbuf=wbuf, ich=ich, ics=ics, oc0=oc0,
                                gsz=gsz, proj=proj: e.dma_start(
                        out=H[wbuf][:, 0:ics * gsz * 128].rearrange(
                            "p (i f) -> p i f", i=ics),
                        in_=self.dram_ap("q_l0att").rearrange(
                            "(i p) f -> p i f", p=128)[:, ich:ich + ics,
                            proj * d + oc0 * 128: proj * d + (oc0 + gsz) * 128]),
                        waits=[wfree[wbuf], self.tok["q_l0att"]], ch=wch)
                    for ii in range(ics):
                        ic = ich + ii
                        for oi in range(gsz):
                            bk = banks[oi]
                            wt = [tl, lt]
                            if ic == 0:
                                wt.append(self.bankfree.get(bk))
                            mm = self.T(lambda e, H, bk=bk, wbuf=wbuf, ii=ii, oi=oi,
                                        gsz=gsz, ic=ic: e.matmul(
                                H[bk][:, 0:hc],
                                H[wbuf][:, (ii * gsz + oi) * 128:(ii * gsz + oi + 1) * 128],
                                H["act"][:, ic * hc:(ic + 1) * hc],
                                start=(ic == 0), stop=(ic == dch - 1)), waits=wt)
                            mlast[oi] = mm
                    wfree[wbuf] = mlast[gsz - 1]
                for oi in range(gsz):
                    h = oc0 + oi
                    bk = banks[oi]
                    if proj == 0:
                        if s == 0:
                            ev = self.V(lambda e, H, bk=bk, h=h: e.tensor_copy(
                                H["big"][:, Q0 + h * hc: Q0 + (h + 1) * hc],
                                H[bk][:, 0:hc]),
                                waits=[mlast[oi], self.tok.get("big_free")])
                        else:
                            ev = self.V(lambda e, H, bk=bk: e.tensor_copy(
                                H["stg"][:, 0:hc], H[bk][:, 0:hc]),
                                waits=[mlast[oi], self.tok.get("stgK_dma")])
                            td = self.D(lambda e, H, h=h: e.dma_start(
                                out=self.dram_ap("qtmp")[h * 128:(h + 1) * 128, :],
                                in_=H["stg"][:, 0:hc]), waits=[ev], ch="d_q")
                            self.tok["stgK_dma"] = td
                    else:
                        ev = self.V(lambda e, H, bk=bk: e.tensor_copy(
                            H["stg"][:, 0:hc], H[bk][:, 0:hc]),
                            waits=[mlast[oi], self.tok.get("stgK_dma")])
                        td = self.D(lambda e, H, h=h, t0=t0: e.dma_start(
                            out=self.dram_ap("kin")[h * 128:(h + 1) * 128, t0:t0 + hc],
                            in_=H["stg"][:, 0:hc]), waits=[ev], ch="d_kv")
                        self.tok["stgK_dma"] = td
                    if proj == 0 and s == 0:
                        sqsrc = (lambda H, h=h: H["big"][:, Q0 + h * hc:
                                                         Q0 + (h + 1) * hc])
                    else:
                        sqsrc = (lambda H: H["stg"][:, 0:hc])
                    sq = self.V(lambda e, H, sqsrc=sqsrc: e.tensor_tensor(
                        H["sqb"][:, 0:hc], sqsrc(H), sqsrc(H), ALU.mult),
                        waits=[ev, self.tok.get("wsp0_free")])
                    self.bankfree[bk] = [sq, ev]
                    nm = self.T(lambda e, H: e.matmul(
                        H["ps4"][0:1, 0:hc], H["ones_s"][:, 0:1], H["sqb"][:, 0:hc],
                        start=True, stop=True),
                        waits=[sq, self.bankfree.get("ps4"), self.tok["consts"]])
                    col = (s * 16 + h) if proj == 0 else (32 + s * 16 + h)
                    rd = self.V(lambda e, H, col=col: e.reduce_max(
                        H["arow"][0:1, col:col + 1], H["ps4"][0:1, 0:hc], axis=AX.X),
                        waits=[nm])
                    self.tok["wsp0_free"] = rd
                    self.bankfree["ps4"] = rd
        for gi, oc0 in enumerate(range(0, dch, 2)):
            gsz = min(2, dch - oc0)
            for ts in range(nkt):
                bk = ["ps0", "ps1", "ps2", "ps3"][(gi * nkt + ts) % 4]
                mlast2 = None
                for ich in range(0, dch, 4):
                    ics = min(4, dch - ich)
                    wbuf = wrot[self._dk % 2]
                    wch = "d_a" if self._dk % 2 == 0 else "d_b"
                    self._dk += 1
                    tl = self.D(lambda e, H, wbuf=wbuf, ich=ich, ics=ics, oc0=oc0,
                                gsz=gsz: e.dma_start(
                        out=H[wbuf][:, 0:ics * gsz * 128].rearrange(
                            "p (i f) -> p i f", i=ics),
                        in_=self.dram_ap("q_l0att").rearrange(
                            "(i p) f -> p i f", p=128)[:, ich:ich + ics,
                            2 * d + oc0 * 128: 2 * d + (oc0 + gsz) * 128]),
                        waits=[wfree[wbuf], self.tok["q_l0att"]], ch=wch)
                    for ii in range(ics):
                        ic = ich + ii
                        wt = [tl, lt]
                        if ic == 0:
                            wt.append(self.bankfree.get(bk))
                        mm = self.T(lambda e, H, bk=bk, wbuf=wbuf, ii=ii, gsz=gsz,
                                    ic=ic, ts=ts, ics=ics: e.matmul(
                            H[bk][:, 0:gsz * 128],
                            H["act"][:, ic * hc + ts * 128: ic * hc + (ts + 1) * 128],
                            H[wbuf][:, 0:ics * gsz * 128].rearrange(
                                "p (i f) -> p i f", i=ics)[:, ii, :],
                            start=(ic == 0), stop=(ic == dch - 1)), waits=wt)
                        mlast2 = mm
                    wfree[wbuf] = mlast2
                ev = self.V(lambda e, H, bk=bk, gsz=gsz: e.tensor_copy(
                    H["stg"][:, hc:hc + gsz * 128], H[bk][:, 0:gsz * 128]),
                    waits=[mlast2, self.tok.get("stgV_dma")])
                self.bankfree[bk] = ev
                td = self.D(lambda e, H, oc0=oc0, gsz=gsz, ts=ts, t0=t0: e.dma_start(
                    out=self.dram_ap("vin")[t0 + ts * 128: t0 + (ts + 1) * 128,
                                            oc0 * 128:(oc0 + gsz) * 128],
                    in_=H["stg"][:, hc:hc + gsz * 128]), waits=[ev], ch="d_kv2")
                self.tok["stgV_dma"] = td

    res_tok = self.tok.get("res")
    tqx = self.D(lambda e, H: e.dma_start(out=H["qidx_s"][:, 0:hc],
                                          in_=self.dram_ap("qidxb")),
                 waits=self.tok.get("uf_free") or [], ch="d_sm1")
    self.tok["kg_free"] = None
    for s in (0, 1):
        if s == 1:
            self.tok["stg_free"] = b.barrier_token("t_m")
            self.tok["stg_free2"] = self.tok.get("stgK_dma")
            self.tok["stg_free3"] = self.tok.get("stgV_dma")
            self.tok["act_free"] = b.barrier_token("t_m")
        if s == 1 and os.environ.get("KATT_CUT", "").startswith("0"):
            break
        lt = self.ln_quant(s * hc, False, res_tok=res_tok)
        self.tok["lnx" + str(s)] = lt
        if os.environ.get("KATT_CUT") == "0a":
            continue
        qkv_proj(s, lt)
    if os.environ.get("KATT_CUT", "").startswith("0") \
            or os.environ.get("KATT_CUT") == "1":
        return
    kv_done = [b.barrier_token("d_kv"), b.barrier_token("d_kv2")]
    t1 = self.V(lambda e, H: e.tensor_tensor(
        H["arow"][0:1, 32:32 + nh], H["arow"][0:1, 32:32 + nh],
        H["arow"][0:1, 48:48 + nh], ALU.max),
        waits=[b.barrier_token("v_m")])
    tg = self.GD(lambda e, H: e.dma_start(out=self.dram_ap("knm_i")[0:1, 0:nh],
                                          in_=H["arow"][0:1, 32:32 + nh]), waits=[t1])
    tar = self.CC(lambda e, H: e.collective_compute(
        "AllReduce", ALU.max, replica_groups=[list(range(c["nc"]))],
        ins=[self.dram_h("knm_i").ap()], outs=[self.dram_h("knm_o").ap()]),
        waits=[tg])
    tg2 = self.GD(lambda e, H: e.dma_start(out=H["arow"][0:1, 32:32 + nh],
                                           in_=self.dram_ap("knm_o")[0:1, 0:nh]),
                  waits=[tar])
    tagk = self.CC(lambda e, H: e.collective_compute(
        "AllGather", ALU.bypass, replica_groups=[list(range(c["nc"]))],
        ins=[self.dram_h("kin").ap()], outs=[self.dram_h("kout").ap()]),
        waits=kv_done)
    tagkv = self.CC(lambda e, H: e.collective_compute(
        "AllGather", ALU.bypass, replica_groups=[list(range(c["nc"]))],
        ins=[self.dram_h("vin").ap()], outs=[self.dram_h("vout").ap()]),
        waits=kv_done)
    if os.environ.get("KATT_CUT") == "2":
        return
    lnk = self.S(lambda e, H: e.activation(H["arow"][0:1, 48:48 + nh],
                                           H["arow"][0:1, 32:32 + nh],
                                           AF.Ln, bias=H["cc_s"][0:1, 4:5]), waits=[tg2])
    for s in (0, 1):
        l1 = self.S(lambda e, H, s=s: e.activation(
            H["arow"][0:1, s * 16:s * 16 + nh], H["arow"][0:1, s * 16:s * 16 + nh],
            AF.Ln, bias=H["cc_s"][0:1, 4:5]), waits=[b.barrier_token("v_m"), lnk])
        l2 = self.V(lambda e, H, s=s: e.tensor_tensor(
            H["arow"][0:1, s * 16:s * 16 + nh], H["arow"][0:1, s * 16:s * 16 + nh],
            H["arow"][0:1, 48:48 + nh], ALU.add), waits=[l1])
        l3 = self.S(lambda e, H, s=s: e.activation(
            H["arow"][0:1, s * 16:s * 16 + nh], H["arow"][0:1, s * 16:s * 16 + nh],
            AF.Exp, scale=0.5, bias=H["cc_s"][0:1, 5:6]), waits=[l2])
        l4 = self.V(lambda e, H, s=s: e.tensor_scalar(
            H["arow"][0:1, s * 16:s * 16 + nh], H["arow"][0:1, s * 16:s * 16 + nh],
            1.0, 60.0, ALU.add, ALU.min), waits=[l3])
        l5 = self.V(lambda e, H, s=s: e.tensor_scalar(
            H["arow"][0:1, s * 16:s * 16 + nh], H["arow"][0:1, s * 16:s * 16 + nh],
            -1.0, None, ALU.mult, ALU.bypass), waits=[l4])
        self.pbc("ps4", nh, lambda H, s=s: H["arow"][0:1, s * 16:s * 16 + nh],
                 lambda H, s=s: H["attM"][:, s * 16:s * 16 + nh], waits=[l5],
                 cwaits=(self.tok.get("uf_free") or []))
    mbar = b.barrier_token("v_m")
    for s in (0, 1):
        _, pp = self.pbc("ps5", 1, lambda H, s=s: H["qpos_s"][0:1, s:s + 1],
                         lambda H, s=s: H["attM"][:, 34 + s:35 + s],
                         waits=[self.tok["consts"]],
                         cwaits=(self.tok.get("uf_free") or []))
        self.V(lambda e, H, s=s: e.tensor_scalar(
            H["attM"][:, 34 + s:35 + s], H["attM"][:, 34 + s:35 + s],
            -1.0, None, ALU.mult, ALU.bypass), waits=[pp])
    qbar = b.barrier_token("v_m")
    if os.environ.get("KATT_CUT") == "3":
        return

    self.tok["slab_free"] = b.barrier_token("t_m")
    for s in (0, 1):
        if s == 1 and os.environ.get("KATT_CUT") == "4":
            break
        if s == 1:
            qload = self.D(lambda e, H: e.dma_start(
                out=H["big"][:, Q0:Q0 + nh * hc].rearrange("p (i f) -> p i f", i=nh),
                in_=self.dram_ap("qtmp").rearrange("(i p) f -> p i f", p=128)),
                waits=[self.tok["qs_free"], b.barrier_token("d_q")], ch="d_q")
        else:
            qload = None
        slab_init = [False] * nh
        for g in range(c["nc"]):
            kr = (g if s == 0 else (c["nc"] - 1 - g)) * dch
            kc = 0 if s == 0 else hc
            vr = (g * T) if s == 0 else ((c["nc"] - 1 - g) * T + hc)
            tkg = self.D(lambda e, H, kr=kr, kc=kc: e.dma_start(
                out=H["big"][:, KG:KG + dch * hc].rearrange("p (i f) -> p i f", i=dch),
                in_=self.dram_ap("kout").rearrange(
                    "(i p) f -> p i f", p=128)[:, kr:kr + dch, kc:kc + hc]),
                waits=[tagk, self.tok.get("kg_free")], ch="d_kg")
            mks = []
            for kt in range(nkt):
                th = self.V(lambda e, H, s=s, g=g, kt=kt: e.tensor_scalar(
                    H["attM"][:, 36:37], H["cc_s"][:, 3:4],
                    H["attM"][:, 34 + s:35 + s], float(g * hc + kt * 128),
                    ALU.add, ALU.add), waits=[qbar, tqx, self.tok.get("m01_free"),
                                              self.tok.get("th_free")])
                mk = self.V(lambda e, H, kt=kt: e.tensor_scalar(
                    H["big"][:, M0 + kt * hc: M0 + (kt + 1) * hc],
                    H["qidx_s"][:, 0:hc], H["attM"][:, 36:37], None,
                    ALU.is_ge, ALU.bypass), waits=[th, self.tok.get("m01_free")])
                self.tok["th_free"] = mk
                mks.append(mk)
            vtok = {}
            for kt in range(nkt):
                vr2 = vr + kt * 128
                vtok[kt] = self.D(lambda e, H, vr2=vr2, kt=kt: e.dma_start(
                    out=H["big"][:, VG + kt * d: VG + (kt + 1) * d],
                    in_=self.dram_ap("vout")[vr2:vr2 + 128, :]),
                    waits=[tagkv, self.tok.get("vg_free")], ch=f"d_v{kt}")
            for h in range(nh):
                for kt in range(nkt):
                    scb = "ps0" if (h * nkt + kt) % 2 == 0 else "ps1"
                    psl = (h * nkt + kt) % 2
                    wtq = [tkg, mbar] + ([qload] if qload else [])
                    mm = self.T(lambda e, H, scb=scb, h=h, kt=kt: e.matmul(
                        H[scb][:, 0:hc],
                        H["big"][:, KG + h * hc + kt * 128: KG + h * hc + (kt + 1) * 128],
                        H["big"][:, Q0 + h * hc: Q0 + (h + 1) * hc],
                        start=True, stop=True),
                        waits=wtq + [self.bankfree.get(scb)])
                    ex = self.S(lambda e, H, scb=scb, psl=psl, s=s, h=h: e.activation(
                        H["big"][:, PP + psl * hc: PP + (psl + 1) * hc], H[scb][:, 0:hc],
                        AF.Exp, scale=csc, bias=H["attM"][:, s * 16 + h: s * 16 + h + 1]),
                        waits=[mm, self.tok.get(("pp_free", psl))])
                    self.bankfree[scb] = ex
                    mq = self.V(lambda e, H, psl=psl, kt=kt: e.tensor_tensor(
                        H["big"][:, PP + psl * hc: PP + (psl + 1) * hc],
                        H["big"][:, PP + psl * hc: PP + (psl + 1) * hc],
                        H["big"][:, M0 + kt * hc: M0 + (kt + 1) * hc], ALU.mult),
                        waits=[ex, mks[kt]])
                    pvb = "ps2" if h % 2 == 0 else "ps3"
                    dnb = "ps4" if h % 2 == 0 else "ps5"
                    pv = self.T(lambda e, H, pvb=pvb, kt=kt, h=h, psl=psl: e.matmul(
                        H[pvb][:, 0:hc],
                        H["big"][:, VG + kt * d + h * 128: VG + kt * d + (h + 1) * 128],
                        H["big"][:, PP + psl * hc: PP + (psl + 1) * hc],
                        start=(kt == 0), stop=(kt == nkt - 1)),
                        waits=[mq, vtok[kt]] + ([self.bankfree.get(pvb)] if kt == 0 else []))
                    dnm = self.T(lambda e, H, dnb=dnb, psl=psl, kt=kt: e.matmul(
                        H[dnb][0:1, 0:hc], H["ones_s"][:, 0:1],
                        H["big"][:, PP + psl * hc: PP + (psl + 1) * hc],
                        start=(kt == 0), stop=(kt == nkt - 1)),
                        waits=[mq] + ([self.bankfree.get(dnb)] if kt == 0 else []))
                    self.tok[("pp_free", psl)] = dnm
                    if kt == nkt - 1:
                        if not slab_init[h]:
                            ev = self.V(lambda e, H, pvb=pvb, h=h: e.tensor_copy(
                                H["act"][:, h * hc:(h + 1) * hc], H[pvb][:, 0:hc]),
                                waits=[pv, self.tok.get("slab_free")])
                            dv = self.V(lambda e, H, dnb=dnb, h=h: e.tensor_copy(
                                H["dn"][0:1, h * hc:(h + 1) * hc], H[dnb][0:1, 0:hc]),
                                waits=[dnm, self.tok.get("dn_free"),
                                       self.tok.get("uf_free")])
                            slab_init[h] = True
                        else:
                            ev = self.V(lambda e, H, pvb=pvb, h=h: e.tensor_tensor(
                                H["act"][:, h * hc:(h + 1) * hc],
                                H["act"][:, h * hc:(h + 1) * hc], H[pvb][:, 0:hc],
                                ALU.add), waits=[pv])
                            dv = self.V(lambda e, H, dnb=dnb, h=h: e.tensor_tensor(
                                H["dn"][0:1, h * hc:(h + 1) * hc],
                                H["dn"][0:1, h * hc:(h + 1) * hc], H[dnb][0:1, 0:hc],
                                ALU.add), waits=[dnm])
                        self.bankfree[pvb] = ev
                        self.bankfree[dnb] = dv
            self.tok["kg_free"] = b.barrier_token("t_m")
            self.tok["vg_free"] = b.barrier_token("t_m")
            self.tok["m01_free"] = b.barrier_token("v_m")
        for h in range(nh):
            dnb2 = "ps0" if h % 2 == 0 else "ps1"
            rc1 = self.V(lambda e, H, h=h: e.reciprocal(
                H["qsq"][0:1, 0:hc], H["dn"][0:1, h * hc:(h + 1) * hc]),
                waits=[b.barrier_token("v_m"), self.tok.get("bc5_free")])
            pb1, _ = self.pbc(dnb2, hc, lambda H: H["qsq"][0:1, 0:hc],
                              waits=[rc1])
            no = self.V(lambda e, H, h=h, dnb2=dnb2: e.tensor_tensor(
                H["act"][:, h * hc:(h + 1) * hc], H["act"][:, h * hc:(h + 1) * hc],
                H[dnb2][0:128, 0:hc], ALU.mult), waits=[pb1])
            self.bankfree[dnb2] = no
            self.tok["bc5_free"] = no
        self.tok["dn_free"] = b.barrier_token("v_m")
        slab_tok = b.barrier_token("v_m")
        self.tok["qs_free"] = b.barrier_token("t_m")
        for gi, oc0 in enumerate(range(0, dch, 3)):
            gsz = min(3, dch - oc0)
            banks = ["ps0", "ps1", "ps2"] if gi % 2 == 0 else ["ps3", "ps4", "ps5"]
            mlast = {}
            for ich in range(0, dch, 4):
                ics = min(4, dch - ich)
                wbuf = wrot[self._dk % 2]
                wch = "d_a" if self._dk % 2 == 0 else "d_b"
                self._dk += 1
                tl = self.D(lambda e, H, wbuf=wbuf, ich=ich, ics=ics, oc0=oc0,
                            gsz=gsz: e.dma_start(
                    out=H[wbuf][:, 0:ics * gsz * 128].rearrange(
                        "p (i f) -> p i f", i=ics),
                    in_=self.dram_ap("q_l0att").rearrange(
                        "(i p) f -> p i f", p=128)[:, ich:ich + ics,
                        3 * d + oc0 * 128: 3 * d + (oc0 + gsz) * 128]),
                    waits=[wfree[wbuf], self.tok["q_l0att"]], ch=wch)
                for ii in range(ics):
                    ic = ich + ii
                    for oi in range(gsz):
                        bk = banks[oi]
                        wt = [tl, slab_tok]
                        if ic == 0:
                            wt.append(self.bankfree.get(bk))
                        mm = self.T(lambda e, H, bk=bk, wbuf=wbuf, ii=ii, oi=oi,
                                    gsz=gsz, ic=ic: e.matmul(
                            H[bk][:, 0:hc],
                            H[wbuf][:, (ii * gsz + oi) * 128:(ii * gsz + oi + 1) * 128],
                            H["act"][:, ic * hc:(ic + 1) * hc],
                            start=(ic == 0), stop=(ic == dch - 1)), waits=wt)
                        mlast[oi] = mm
                wfree[wbuf] = mlast[gsz - 1]
            for oi in range(gsz):
                oc = oc0 + oi
                bk = banks[oi]
                rc = (slice(None), slice(oc * T + s * hc, oc * T + s * hc + hc))
                e2 = self.V(lambda e, H, rc=rc, bk=bk: e.tensor_tensor(
                    H["res"][rc], H["res"][rc], H[bk][:, 0:hc], ALU.add),
                    waits=[mlast[oi], b.barrier_token("v_m")])
                self.bankfree[bk] = e2
        self.tok["slab_free"] = b.barrier_token("t_m")
    self.tok["res"] = b.barrier_token("v_m")
    self.tok["act_free"] = b.barrier_token("t_m")
    self.tok["big_free"] = b.barrier_token("t_m")
    self.tok["stg_free"] = b.barrier_token("t_m")
    self.tok["uf_free"] = [b.barrier_token(s) for s in ("v_m", "s_m", "g_m", "t_m")]


Prog.attn = _attn


def _ssm(self):
    b, c = self.b, self.c
    d, hc, dch, T = c["d"], c["hc"], c["dch"], c["T"]
    hss, inter, cvd = c["hssm"], c["inter"], c["convdim"]
    cvch, ich = cvd // 128, inter // 128
    hh = hc // 2
    nch = hc // 128
    wrot = ["wa", "wb"]
    wfree = self.wfree
    RA = 0
    RB = cvch * hh
    Y0 = 2 * cvch * hh
    assert Y0 + ich * hh <= self.big_cols

    def yoff(half):
        return Y0 if half == 0 else RA

    res_tok = self.tok.get("res")
    self.tok["ra_free"] = self.tok.get("big_free")
    self.tok["rb_free"] = self.tok.get("big_free")
    self.tok["y_region_free"] = self.tok.get("big_free")
    prevs = self.tok.get("bd_stage")
    for ch in range(dch):
        for sd in (0, 1):
            cc0 = ch * T + (sd + 1) * hc - 3
            pv = self.D(lambda e, H, ch=ch, sd=sd, cc0=cc0: e.dma_start(
                out=self.dram_ap("bdin")[ch * 128:(ch + 1) * 128, sd * 3:sd * 3 + 3],
                in_=H["res"][:, cc0:cc0 + 3]), waits=[res_tok], ch="d_sm1")
    for dst, srcn in (("convw_s", "convw"), ("convb_s", "convb"), ("onest_s", "onest")):
        self.D(lambda e, H, dst=dst, srcn=srcn: e.dma_start(
            out=H[dst][:, :] if dst != "onest_s" else H[dst][0:c["hssm"], :],
            in_=self.dram_ap(srcn) if dst != "onest_s"
                else self.dram_ap(srcn)[0:c["hssm"], :]),
            waits=self.tok.get("uf_free") or [], ch="d_sm1")
    self.tok["ssmconsts"] = b.barrier_token("d_sm1")
    bd_done = b.barrier_token("d_sm1")
    tag1 = self.CC(lambda e, H: e.collective_compute(
        "AllGather", ALU.bypass, replica_groups=[list(range(c["nc"]))],
        ins=[self.dram_h("bdin").ap()], outs=[self.dram_h("bdout").ap()]),
        waits=[bd_done])
    self.tok["kg_free"] = None

    for sd in (0, 1):
        t0 = sd * hc
        if sd == 1:
            self.tok["stg_free"] = [t for t in (self.tok.get(("ydma_last", 0)),
                                                self.tok.get(("ydma_last", 1))) if t]
        lt = self.ln_quant(t0, False, res_tok=res_tok)
        self.tok["act_free"] = b.barrier_token("t_m")
        twd = self.GD(lambda e, H: e.dma_start(
            out=H["wfq"][:, 0:dch * hss].rearrange("p (i h) -> p i h", i=dch),
            in_=self.dram_ap("l1wdt").rearrange("(i p) h -> p i h", p=128)),
            waits=[self.tok.get("wfq_free"), self.tok["consts"]])
        pdt = None
        for ic in range(dch):
            mm = self.T(lambda e, H, ic=ic: e.matmul(
                H["ps0"][0:hss, 0:hc], H["wfq"][:, ic * hss:(ic + 1) * hss],
                H["act"][:, ic * hc:(ic + 1) * hc],
                start=(ic == 0), stop=(ic == dch - 1)),
                waits=[lt, twd] + ([self.bankfree.get("ps0")] if ic == 0 else []))
            pdt = mm
        self.tok["wfq_free"] = b.barrier_token("t_m")
        e1 = self.V(lambda e, H: e.tensor_scalar(
            H["dts"][0:hss, 0:hc], H["ps0"][0:hss, 0:hc], H["smv_s"][0:hss, 0:1], None,
            ALU.add, ALU.bypass),
            waits=[pdt, self.tok.get("dts_free")] + (self.tok.get("uf_free") or []))
        self.bankfree["ps0"] = e1
        e2 = self.V(lambda e, H: e.tensor_scalar(
            H["wsp"][0:hss, 0:hc], H["dts"][0:hss, 0:hc], 10.0, None,
            ALU.min, ALU.bypass), waits=[e1])
        e3 = self.S(lambda e, H: e.activation(
            H["wsp"][0:hss, 0:hc], H["wsp"][0:hss, 0:hc], AF.Exp), waits=[e2])
        e4 = self.S(lambda e, H: e.activation(
            H["wsp"][0:hss, 0:hc], H["wsp"][0:hss, 0:hc], AF.Ln,
            bias=H["cc_s"][0:1, 1:2] if hss == 1 else H["cc_s"][0:hss, 1:2]), waits=[e3])
        e5 = self.V(lambda e, H: e.tensor_scalar(
            H["wsp"][0:hss, hc:2 * hc], H["dts"][0:hss, 0:hc], 10.0, None,
            ALU.is_ge, ALU.bypass), waits=[e2])
        e6 = self.V(lambda e, H: e.tensor_tensor(
            H["dts"][0:hss, 0:hc], H["dts"][0:hss, 0:hc], H["wsp"][0:hss, 0:hc],
            ALU.subtract), waits=[e4, e5])
        e7 = self.V(lambda e, H: e.tensor_tensor(
            H["dts"][0:hss, 0:hc], H["dts"][0:hss, 0:hc], H["wsp"][0:hss, hc:2 * hc],
            ALU.mult), waits=[e6])
        e8 = self.V(lambda e, H: e.tensor_tensor(
            H["dts"][0:hss, 0:hc], H["dts"][0:hss, 0:hc], H["wsp"][0:hss, 0:hc],
            ALU.add), waits=[e7])
        dt_tok = e8
        if os.environ.get("KSSM_CUT") == "2":
            return

        pv = None
        for rh4 in range(2):
            tbl = self.D(lambda e, H, rh4=rh4: e.dma_start(
                out=H["bdl"][:, 0:dch * 4 * 6].rearrange("p (a c) -> p a c", c=6),
                in_=self.dram_ap("bdout").rearrange(
                    "(a p) c -> p a c", p=128)[:, rh4 * 4 * dch:(rh4 + 1) * 4 * dch, :]),
                waits=[tag1, pv, b.barrier_token("v_m")]
                      + (self.tok.get("uf_free") or []), ch="d_kg")
            for ch in range(dch):
                for rr in range(4):
                    r = rh4 * 4 + rr
                    if r == 0:
                        t1 = self.V(lambda e, H, ch=ch, sd=sd: e.tensor_scalar(
                            H["bres"][:, ch * 3:(ch + 1) * 3],
                            H["bdl"][:, ch * 6 + sd * 3:ch * 6 + sd * 3 + 3],
                            H["bsel_s"][:, sd * 8:sd * 8 + 1], None,
                            ALU.mult, ALU.bypass),
                            waits=[tbl, pv, self.tok.get("bres_free")])
                    else:
                        t1 = self.V(lambda e, H, ch=ch, rr=rr, r=r, sd=sd:
                                    e.scalar_tensor_tensor(
                            H["bres"][:, ch * 3:(ch + 1) * 3],
                            H["bdl"][:, (rr * dch + ch) * 6 + sd * 3:
                                      (rr * dch + ch) * 6 + sd * 3 + 3],
                            H["bsel_s"][:, sd * 8 + r:sd * 8 + r + 1],
                            H["bres"][:, ch * 3:(ch + 1) * 3], ALU.mult, ALU.add),
                            waits=[tbl, pv])
                    pv = t1
        W3 = 3
        sm1 = sm2 = None
        for ch in range(dch):
            tc = self.V(lambda e, H, ch=ch: e.tensor_copy(
                H["sqb"][:, 0:3], H["bres"][:, ch * 3:(ch + 1) * 3]),
                waits=[pv, sm1, self.tok.get("sqb_free"), self.tok.get("ts2_prev")])
            ts2 = self.V(lambda e, H: e.tensor_tensor(
                H["sqb"][:, 4:7], H["sqb"][:, 0:3], H["sqb"][:, 0:3], ALU.mult),
                waits=[tc, sm2])
            self.tok["ts2_prev"] = ts2
            sm1 = self.T(lambda e, H, ch=ch: e.matmul(
                H["ps1"][0:1, 0:3], H["ones_s"][:, 0:1], H["sqb"][:, 0:3],
                start=(ch == 0), stop=(ch == dch - 1)),
                waits=[tc] + ([self.bankfree.get("ps1")] if ch == 0 else []))
            sm2 = self.T(lambda e, H, ch=ch: e.matmul(
                H["ps2"][0:1, 0:3], H["ones_s"][:, 0:1], H["sqb"][:, 4:7],
                start=(ch == 0), stop=(ch == dch - 1)),
                waits=[ts2] + ([self.bankfree.get("ps2")] if ch == 0 else []))
        bm = self.V(lambda e, H: e.tensor_scalar(
            H["alr"][0:1, 0:3], H["ps1"][0:1, 0:3], 1.0 / d, None, ALU.mult, ALU.bypass),
            waits=[sm1])
        bm2 = self.V(lambda e, H: e.tensor_scalar(
            H["alr"][0:1, 4:7], H["ps2"][0:1, 0:3], 1.0 / d, None, ALU.mult, ALU.bypass),
            waits=[sm2])
        self.bankfree["ps1"] = bm
        self.bankfree["ps2"] = bm2
        bv1 = self.V(lambda e, H: e.tensor_tensor(
            H["alr"][0:1, 8:11], H["alr"][0:1, 0:3], H["alr"][0:1, 0:3], ALU.mult),
            waits=[bm])
        bv2 = self.V(lambda e, H: e.tensor_tensor(
            H["alr"][0:1, 8:11], H["alr"][0:1, 4:7], H["alr"][0:1, 8:11], ALU.subtract),
            waits=[bv1, bm2])
        bl1 = self.S(lambda e, H: e.activation(H["alr"][0:1, 8:11], H["alr"][0:1, 8:11],
                                               AF.Ln, bias=H["cc_s"][0:1, 0:1]), waits=[bv2])
        bl2 = self.S(lambda e, H: e.activation(H["alr"][0:1, 8:11], H["alr"][0:1, 8:11],
                                               AF.Exp, scale=-0.5), waits=[bl1])
        _, bb1 = self.pbc("ps1", 3, lambda H: H["alr"][0:1, 0:3],
                          lambda H: H["bc"][:, 0:3],
                          waits=[bm], cwaits=[self.tok.get("bc_free")])
        _, bb2 = self.pbc("ps2", 3, lambda H: H["alr"][0:1, 8:11],
                          lambda H: H["bc"][:, 8:11],
                          waits=[bl2], cwaits=[self.tok.get("bc_free")])
        pv2 = None
        for ch in range(dch):
            a1 = self.V(lambda e, H, ch=ch: e.scalar_tensor_tensor(
                H["sqb"][:, 8:11], H["bres"][:, ch * 3:(ch + 1) * 3], 1.0,
                H["bc"][:, 0:3], ALU.mult, ALU.subtract), waits=[bb1, pv2])
            a2 = self.V(lambda e, H, ch=ch: e.tensor_tensor(
                H["blnx"][:, ch * 3:(ch + 1) * 3], H["sqb"][:, 8:11], H["bc"][:, 8:11],
                ALU.mult), waits=[a1, bb2])
            pv2 = a2
        self.tok["bres_free"] = pv2
        for ocb in range(0, cvd // 128, 2):
            ocs = min(2, cvch - ocb)
            for icb in range(0, dch, 4):
                ics = min(4, dch - icb)
                wbuf = wrot[self._dk % 2]
                wch = "d_a" if self._dk % 2 == 0 else "d_b"
                self._dk += 1
                tl = self.D(lambda e, H, wbuf=wbuf, icb=icb, ics=ics, ocb=ocb, ocs=ocs:
                            e.dma_start(
                    out=H[wbuf][:, 0:ics * ocs * 128].rearrange("p (i f) -> p i f", i=ics),
                    in_=self.dram_ap("q_l1zx").rearrange(
                        "(i p) f -> p i f", p=128)[:, icb:icb + ics,
                        inter + ocb * 128: inter + (ocb + ocs) * 128]),
                    waits=[wfree[wbuf], self.tok["q_l1zx"]], ch=wch)
                for ii in range(ics):
                    ic = icb + ii
                    for oi in range(ocs):
                        oc = ocb + oi
                        bk = f"ps{3 + (oc % 2)}"
                        wt = [tl, pv2]
                        if ic == 0:
                            wt.append(self.bankfree.get(bk))
                        mm = self.T(lambda e, H, bk=bk, wbuf=wbuf, ii=ii, oi=oi, ocs=ocs,
                                    ic=ic, oc=oc: e.matmul(
                            H[bk][:, (oc % 2) * 0:3],
                            H[wbuf][:, (ii * ocs + oi) * 128:(ii * ocs + oi + 1) * 128],
                            H["blnx"][:, ic * 3:(ic + 1) * 3],
                            start=(ic == 0), stop=(ic == dch - 1)), waits=wt)
                        self.tok[("bmm", oc)] = mm
                    wfree[wbuf] = self.tok[("bmm", ocb + ocs - 1)]
            for oi in range(ocs):
                oc = ocb + oi
                bk = f"ps{3 + (oc % 2)}"
                ev = self.V(lambda e, H, bk=bk, oc=oc: e.tensor_copy(
                    H["bdx"][:, oc * 3:(oc + 1) * 3], H[bk][:, 0:3]),
                    waits=[self.tok[("bmm", oc)], self.tok.get("bdx_free")])
                self.bankfree[bk] = ev
        bdx_tok = b.barrier_token("v_m")
        if os.environ.get("KSSM_CUT") == "3":
            return

        side_scan_done = []
        for half in range(2):
            th0 = t0 + half * hh
            for gi, oc0 in enumerate(range(0, cvch, 3)):
                gsz = min(3, cvch - oc0)
                banks = ["ps0", "ps1", "ps2"] if gi % 2 == 0 else ["ps3", "ps4", "ps5"]
                mlast = {}
                for icb in range(0, dch, 4):
                    ics = min(4, dch - icb)
                    wbuf = wrot[self._dk % 2]
                    wch = "d_a" if self._dk % 2 == 0 else "d_b"
                    self._dk += 1
                    tl = self.D(lambda e, H, wbuf=wbuf, icb=icb, ics=ics, oc0=oc0,
                                gsz=gsz: e.dma_start(
                        out=H[wbuf][:, 0:ics * gsz * 128].rearrange(
                            "p (i f) -> p i f", i=ics),
                        in_=self.dram_ap("q_l1zx").rearrange(
                            "(i p) f -> p i f", p=128)[:, icb:icb + ics,
                            inter + oc0 * 128: inter + (oc0 + gsz) * 128]),
                        waits=[wfree[wbuf], self.tok["q_l1zx"]], ch=wch)
                    for ii in range(ics):
                        ic = icb + ii
                        for oi in range(gsz):
                            bk = banks[oi]
                            wt = [tl, lt]
                            if ic == 0:
                                wt.append(self.bankfree.get(bk))
                            mm = self.T(lambda e, H, bk=bk, wbuf=wbuf, ii=ii, oi=oi,
                                        gsz=gsz, ic=ic, half=half: e.matmul(
                                H[bk][:, 0:hh],
                                H[wbuf][:, (ii * gsz + oi) * 128:(ii * gsz + oi + 1) * 128],
                                H["act"][:, ic * hc + half * hh: ic * hc + half * hh + hh],
                                start=(ic == 0), stop=(ic == dch - 1)), waits=wt)
                            mlast[oi] = mm
                    wfree[wbuf] = mlast[gsz - 1]
                for oi in range(gsz):
                    oc = oc0 + oi
                    bk = banks[oi]
                    ev = self.V(lambda e, H, bk=bk, oc=oc: e.tensor_copy(
                        H["big"][:, RA + oc * hh: RA + (oc + 1) * hh], H[bk][:, 0:hh]),
                        waits=[mlast[oi], self.tok.get("ra_free")])
                    self.bankfree[bk] = ev
            raw_tok = b.barrier_token("v_m")
            K = c["K"]
            bdsrc = "bdx" if half == 0 else "bdx2"
            cv_toks = []
            pvc = self.tok.get("rb_free")
            for ch in range(cvch):
                acc = None
                for k in range(K):
                    sh = k - (K - 1)
                    lo = max(0, -sh)
                    t1 = self.V(lambda e, H, ch=ch, k=k, sh=sh, lo=lo, acc=acc: (
                        e.tensor_scalar(
                            H["big"][:, RB + ch * hh + lo: RB + (ch + 1) * hh],
                            H["big"][:, RA + ch * hh + lo + sh: RA + (ch + 1) * hh + sh],
                            H["convw_s"][:, ch * K + k:ch * K + k + 1], None,
                            ALU.mult, ALU.bypass)
                        if acc is None else
                        e.scalar_tensor_tensor(
                            H["big"][:, RB + ch * hh + lo: RB + (ch + 1) * hh],
                            H["big"][:, RA + ch * hh + lo + sh: RA + (ch + 1) * hh + sh],
                            H["convw_s"][:, ch * K + k:ch * K + k + 1],
                            H["big"][:, RB + ch * hh + lo: RB + (ch + 1) * hh],
                            ALU.mult, ALU.add)),
                        waits=[raw_tok, pvc, self.tok["ssmconsts"], acc,
                               self.tok.get("rb_free"), self.tok.get("rb_free2")])
                    pvc = t1
                    if lo > 0:
                        t2 = self.V(lambda e, H, ch=ch, k=k, sh=sh, lo=lo, acc=acc, bdsrc=bdsrc: (
                            e.tensor_scalar(
                                H["big"][:, RB + ch * hh: RB + ch * hh + lo],
                                H[bdsrc][:, ch * 3 + 3 + sh: ch * 3 + 3 + sh + lo],
                                H["convw_s"][:, ch * K + k:ch * K + k + 1], None,
                                ALU.mult, ALU.bypass)
                            if acc is None else
                            e.scalar_tensor_tensor(
                                H["big"][:, RB + ch * hh: RB + ch * hh + lo],
                                H[bdsrc][:, ch * 3 + 3 + sh: ch * 3 + 3 + sh + lo],
                                H["convw_s"][:, ch * K + k:ch * K + k + 1],
                                H["big"][:, RB + ch * hh: RB + ch * hh + lo],
                                ALU.mult, ALU.add)),
                            waits=[bdx_tok if half == 0 else self.tok.get("bdx2_tok"), t1])
                        pvc = t2
                    acc = pvc
                sl = self.S(lambda e, H, ch=ch: e.activation(
                    H["big"][:, RB + ch * hh: RB + (ch + 1) * hh],
                    H["big"][:, RB + ch * hh: RB + (ch + 1) * hh], AF.Silu,
                    bias=H["convb_s"][:, ch:ch + 1]),
                    waits=[pvc, self.tok["ssmconsts"]])
                cv_toks.append(sl)
                pvc = sl
            if half == 0:
                sv = None
                for ch2 in range(cvch):
                    sv = self.V(lambda e, H, ch2=ch2: e.tensor_copy(
                        H["bdx2"][:, ch2 * 3:(ch2 + 1) * 3],
                        H["big"][:, RA + (ch2 + 1) * hh - 3: RA + (ch2 + 1) * hh]),
                        waits=[raw_tok, sv])
                self.tok["bdx2_tok"] = sv
            cv_tok = b.barrier_token("s_m")
            self.tok["ra_free"] = b.barrier_token("v_m")
            if half == 1:
                self.tok["y_region_free"] = self.tok["ra_free"]
            if os.environ.get("KSSM_CUT") == "4":
                return
            self._ssm_scan(sd, half, cv_tok, dt_tok)
            if os.environ.get("KSSM_CUT") == "5":
                return
            self.tok["rb_free"] = self.tok["scan_y_done"]
            self.tok["rb_free2"] = self.tok["scan_rb_done"]
            self.tok["ra_free"] = [self.tok["scan_y_done"], self.tok["scan_rb_done"]]
            if half == 0:
                self.tok["bdx_free"] = self.tok["ra_free"]
        self._ssm_ag_send(sd)
        self.tok["dts_free"] = b.barrier_token("v_m")
        if os.environ.get("KSSM_CUT") == "6":
            return
    if os.environ.get("KSSM_CUT") == "7":
        return
    self._ssm_post()


Prog.ssm = _ssm


def _ssm_scan(self, sd, half, cv_tok, dt_tok):
    b, c = self.b, self.c
    hss, inter, cvd = c["hssm"], c["inter"], c["convdim"]
    cvch, ich = cvd // 128, inter // 128
    hc, hh = c["hc"], c["hc"] // 2
    RA = 0
    RB = cvch * hh
    Y0 = 2 * cvch * hh
    yo = Y0 if half == 0 else RA
    XOF, BOF, COF = RB, RB + ich * hh, RB + (ich + 1) * hh
    npair = hss // 2

    if half == 0:
        z1 = self.V(lambda e, H: e.memset(H["ssm_s"][:, :], 0.0),
                    waits=[self.tok.get("ssm_s_free"), dt_tok]
                          + (self.tok.get("uf_free") or []))
        z2 = self.V(lambda e, H: e.memset(H["ssm_sb"][:, :], 0.0), waits=[z1])
        self.tok["sstate"] = z2
        zf = self.V(lambda e, H: e.memset(H["facc"][0:hss, 0:1], 0.0),
                    waits=[dt_tok, b.barrier_token("t_m")])
        self.tok["facc"] = zf

    nqh = hh // 128
    for q in range(nqh):
        cc = half * nqh + q
        col0 = cc * 128
        t1 = self.V(lambda e, H, col0=col0: e.tensor_scalar(
            H["dta"][0:hss, 0:128], H["dts"][0:hss, col0:col0 + 128],
            H["smv_s"][0:hss, 1:2], None, ALU.mult, ALU.bypass),
            waits=[dt_tok, self.tok.get("dta_free"), self.tok["consts"]])
        t2 = self.V(lambda e, H: e.tensor_tensor_scan(
            H["afm"][0:hss, 0:128], H["onest_s"][0:hss, 0:128], H["dta"][0:hss, 0:128],
            0.0, ALU.mult, ALU.add),
            waits=[t1, self.tok.get("afm_free"), self.tok["ssmconsts"]])
        e2 = self.S(lambda e, H, cc=cc, sd=sd, col0=col0: e.activation(
            H["ea2"][0:hss, sd * hc + col0: sd * hc + col0 + 128],
            H["afm"][0:hss, 0:128], AF.Exp,
            bias=H["facc"][0:hss, cc:cc + 1]), waits=[t2, self.tok["facc"],
                                                      self.tok.get("ea2_free")])
        fn = self.V(lambda e, H, cc=cc: e.tensor_tensor(
            H["facc"][0:hss, cc + 1:cc + 2], H["facc"][0:hss, cc:cc + 1],
            H["afm"][0:hss, 127:128], ALU.add), waits=[t2, self.tok["facc"]])
        self.tok["facc"] = fn
        ev1 = self.S(lambda e, H: e.activation(H["eav"][0:hss, 0:128],
                                               H["afm"][0:hss, 0:128], AF.Exp),
                     waits=[t2, self.tok.get("eav_free")])
        tr1 = self.T(lambda e, H: e.transpose(H["pstf"][:, 0:hss], H["afm"][0:hss, 0:128],
                                              H["identf_s"][0:hss, 0:hss]),
                     waits=[t2, self.tok.get("pstf_free"), self.tok["consts"]])
        c1 = self.V(lambda e, H: e.tensor_copy(H["aT"][:, 0:hss], H["pstf"][:, 0:hss]),
                    waits=[tr1, self.tok.get("aT_free")])
        c1n = self.V(lambda e, H: e.tensor_scalar(H["aTn"][:, 0:hss], H["aT"][:, 0:hss],
                                                  -1.0, None, ALU.mult, ALU.bypass),
                     waits=[c1])
        tr2 = self.T(lambda e, H: e.transpose(H["pstf"][0:1, 128:128 + hss],
                                              H["afm"][0:hss, 127:128],
                                              H["identf_s"][0:hss, 0:hss]),
                     waits=[t2, c1])
        c2 = self.V(lambda e, H: e.tensor_copy(H["alr"][0:1, 0:hss],
                                               H["pstf"][0:1, 128:128 + hss]),
                    waits=[tr2, self.tok.get("alr_free")])
        tr3 = self.T(lambda e, H, col0=col0: e.transpose(
            H["pstf"][:, 0:hss], H["dts"][0:hss, col0:col0 + 128],
            H["identf_s"][0:hss, 0:hss]),
            waits=[dt_tok, c1, c2])
        self.tok["pstf_free"] = tr3
        c3 = self.V(lambda e, H: e.tensor_copy(H["dw"][:, 0:hss], H["pstf"][:, 0:hss]),
                    waits=[tr3, self.tok.get("dw_free")])
        self.tok["pstf_free"] = c3
        wm, _ = self.pbc("ps4", hss, lambda H: H["alr"][0:1, 0:hss], waits=[c2])
        w2 = self.V(lambda e, H: e.tensor_tensor(H["wexp"][:, 0:hss],
                                                 H["ps4"][0:128, 0:hss],
                                                 H["aT"][:, 0:hss], ALU.subtract),
                    waits=[wm, c1, self.tok.get("wexp_free")])
        self.bankfree["ps4"] = w2
        w3 = self.S(lambda e, H: e.activation(H["wexp"][:, 0:hss], H["wexp"][:, 0:hss],
                                              AF.Exp), waits=[w2])
        w4 = self.V(lambda e, H: e.tensor_tensor(H["wexp"][:, 0:hss], H["wexp"][:, 0:hss],
                                                 H["dw"][:, 0:hss], ALU.mult),
                    waits=[w3, c3])
        if os.environ.get("KSCAN_CUT") == "a":
            return
        xts = []
        for ch in range(ich):
            sl = ch % 4
            trx = self.T(lambda e, H, ch=ch, sl=sl, q=q: e.transpose(
                H["pst"][:, sl * 128:(sl + 1) * 128],
                H["big"][:, RB + ch * hh + q * 128: RB + ch * hh + q * 128 + 128],
                H["ident_s"][:]),
                waits=[cv_tok, self.tok.get("pst_read")])
            cx = self.V(lambda e, H, ch=ch, sl=sl: e.tensor_copy(
                H["dxt"][:, ch * 128:(ch + 1) * 128], H["pst"][:, sl * 128:(sl + 1) * 128]),
                waits=[trx, self.tok.get("dxt_free")])
            self.tok["pst_read"] = cx
            xts.append(cx)
        xt_tok = b.barrier_token("v_m")
        ihalf = inter // 2
        for h in ([] if os.environ.get("KSCAN_CUT") == "b1" else range(hss)):
            self.V(lambda e, H, h=h: e.tensor_scalar(
                H["dxw"][:, h * 64:(h + 1) * 64], H["dxt"][:, h * 64:(h + 1) * 64],
                H["wexp"][:, h:h + 1], None, ALU.mult, ALU.bypass),
                waits=[w4, xt_tok] + ([self.tok.get("dxw_free")] if h == 0 else []))
        m1 = b.barrier_token("v_m")
        self.tok["wexp_free"] = m1
        if os.environ.get("KSCAN_CUT") == "b":
            return
        sst = self.T(lambda e, H, q=q: e.matmul(
            H["ps3"][:, 0:128],
            H["big"][:, BOF + q * 128: BOF + q * 128 + 128],
            H["big"][:, COF + q * 128: COF + q * 128 + 128],
            start=True, stop=True), waits=[cv_tok, self.bankfree.get("ps3")])
        sste = self.V(lambda e, H: e.tensor_copy(H["sst"][:, 0:128], H["ps3"][:, 0:128]),
                      waits=[sst, self.tok.get("sst_free")])
        self.bankfree["ps3"] = sste
        cp1 = self.V(lambda e, H, cc=cc, q=q, sd=sd: e.tensor_copy(
            H["cp"][:, sd * hc + cc * 128: sd * hc + (cc + 1) * 128],
            H["big"][:, COF + q * 128: COF + q * 128 + 128]),
            waits=[cv_tok, self.tok.get("cp_free")])
        trb = self.T(lambda e, H, q=q: e.transpose(
            H["pst"][:, 0:128], H["big"][:, BOF + q * 128: BOF + q * 128 + 128],
            H["ident_s"][:]), waits=[cv_tok, self.tok.get("pst_read")])
        cb = self.V(lambda e, H: e.tensor_copy(H["gtl"][:, 0:128], H["pst"][:, 0:128]),
                    waits=[trb, self.tok.get("bT_free")])
        self.tok["pst_read"] = cb

        if os.environ.get("KSCAN_CUT") == "c":
            return
        ydone = []
        for hp in range(npair):
            ybank = "ps2"
            for par in range(2):
                h = 2 * hp + par
                g1 = self.T(lambda e, H, h=h: e.matmul(
                    H["ps4"][:, 0:128],
                    H["identf_s"][0:hss, h:h + 1].broadcast_to([hss, 128]),
                    H["afm"][0:hss, 0:128], start=True, stop=True),
                    waits=[t2, self.bankfree.get("ps4"), self.tok["consts"]])
                g2 = self.V(lambda e, H: e.tensor_tensor(
                    H["emt"][:, 0:128], H["ps4"][:, 0:128], H["ssdm_s"][:], ALU.add),
                    waits=[g1, self.tok["consts"], self.tok.get("emt_free")])
                self.bankfree["ps4"] = g2
                g3 = self.S(lambda e, H, h=h: e.activation(
                    H["emt"][:, 0:128], H["emt"][:, 0:128], AF.Exp,
                    bias=H["aTn"][:, h:h + 1]), waits=[g2, c1n])
                g4 = self.V(lambda e, H, h=h: e.scalar_tensor_tensor(
                    H["gtl"][:, 128:256], H["emt"][:, 0:128], H["dw"][:, h:h + 1],
                    H["sst"][:, 0:128], ALU.mult, ALU.mult),
                    waits=[g3, sste, c3, self.tok.get("g4_free")])
                self.tok["emt_free"] = g4
                g5 = self.T(lambda e, H, h=h: e.matmul(
                    H["ps5"][:, 0:128],
                    H["identf_s"][0:hss, h:h + 1].broadcast_to([hss, 128]),
                    H["eav"][0:hss, 0:128], start=True, stop=True),
                    waits=[ev1, self.bankfree.get("ps5")])
                g6 = self.V(lambda e, H, q=q: e.tensor_tensor(
                    H["gtl"][:, 256:384], H["ps5"][:, 0:128],
                    H["big"][:, COF + q * 128: COF + q * 128 + 128], ALU.mult),
                    waits=[g5, cv_tok, self.tok.get("g6_free")])
                self.bankfree["ps5"] = g6
                self.tok["ct_free"] = g6
                tp = (0, 64) if par == 1 else None
                ps_ap = (slice(64, 128), slice(0, 128)) if par == 1 else (slice(0, 64), slice(0, 128))
                mi = self.T(lambda e, H, h=h, ps_ap=ps_ap, tp=tp: e.matmul(
                    H["ps2"][ps_ap], H["dxt"][:, h * 64:(h + 1) * 64],
                    H["gtl"][:, 128:256],
                    start=True, stop=False, tile_position=tp,
                    skip_group_check=True),
                    waits=[xt_tok, g4] + ([self.bankfree.get("ps2")] if par == 0 else []))
                mo = self.T(lambda e, H, h=h, ps_ap=ps_ap, tp=tp: e.matmul(
                    H["ps2"][ps_ap], H["ssm_sb"][:, h * 64:(h + 1) * 64],
                    H["gtl"][:, 256:384],
                    start=False, stop=True, tile_position=tp, skip_group_check=True),
                    waits=[g6, self.tok["sstate"]])
                self.tok["g4_free"] = mo
                self.tok["g6_free"] = mo
            grp = hp // 4
            par2 = grp % 2
            slot = par2 * 4 + hp % 4
            ye = self.V(lambda e, H, hp=hp, q=q, slot=slot: e.scalar_tensor_tensor(
                H["stg"][:, slot * 128:(slot + 1) * 128],
                H["big"][:, RB + hp * hh + q * 128: RB + hp * hh + q * 128 + 128],
                H["dcolx_s"][:, hp:hp + 1], H["ps2"][:, 0:128], ALU.mult, ALU.add),
                waits=[b.barrier_token("t_m"), cv_tok, self.tok["consts"],
                       self.tok.get(("ydma_p", par2)),
                       self.tok.get("stg_free") if hp == 0 else None])
            self.bankfree["ps2"] = ye
            ydone.append(ye)
            if hp % 4 == 3 or hp == npair - 1:
                n4 = (hp % 4) + 1
                chv = f"d_v{2 + par2}"
                tcol = sd * hc + half * hh + q * 128
                tk = self.D(lambda e, H, tcol=tcol, grp=grp, n4=n4, par2=par2: e.dma_start(
                    out=self.dram_ap("ytmp").rearrange(
                        "(i p) t -> p i t", p=128)[:, grp * 4:grp * 4 + n4,
                        tcol:tcol + 128],
                    in_=H["stg"][:, par2 * 512:par2 * 512 + n4 * 128].rearrange(
                        "p (i t) -> p i t", i=n4)),
                    waits=[ye], ch=chv)
                self.tok[("ydma_p", par2)] = tk
                self.tok[("ydma_last", par2)] = tk
        if os.environ.get("KSCAN_CUT") == "d":
            return
        er = self.S(lambda e, H: e.activation(H["alr"][0:1, hss:2 * hss],
                                              H["alr"][0:1, 0:hss], AF.Exp),
                    waits=[c2])
        sprev = self.tok["sstate"]
        _, ec_ = self.pbc("ps4", hss, lambda H: H["alr"][0:1, hss:2 * hss],
                          lambda H: H["expand"][:, 0:hss],
                          waits=[er], cwaits=[self.tok.get("exp2_free")])
        bw = min(512, ihalf)
        for blk in range(inter // bw):
            dsb = "ps0" if blk % 2 == 0 else "ps1"
            ds = self.T(lambda e, H, blk=blk, dsb=dsb, bw=bw: e.matmul(
                H[dsb][:, 0:bw], H["gtl"][:, 0:128],
                H["dxw"][:, blk * bw:(blk + 1) * bw],
                start=True, stop=True),
                waits=[cb, m1, self.bankfree.get(dsb)])
            hpb = bw // 64
            ulast = None
            for hi in range(hpb):
                h = blk * hpb + hi
                ulast = self.V(lambda e, H, h=h, hi=hi, dsb=dsb: e.scalar_tensor_tensor(
                    H["ssm_s"][:, h * 64:(h + 1) * 64],
                    H["ssm_s"][:, h * 64:(h + 1) * 64],
                    H["expand"][:, h:h + 1],
                    H[dsb][:, hi * 64:(hi + 1) * 64], ALU.mult, ALU.add),
                    waits=[ec_, ds, sprev])
            self.bankfree[dsb] = ulast
        self.tok["exp2_free"] = b.barrier_token("v_m")
        supd = b.barrier_token("v_m")
        sb1 = self.V(lambda e, H: e.tensor_copy(H["ssm_sb"][:, :], H["ssm_s"][:, :]),
                     waits=[supd, b.barrier_token("t_m")])
        self.tok["sstate"] = sb1
        self.tok["expand_free"] = b.barrier_token("v_m")
        self.tok["dta_free"] = t2
        self.tok["afm_free"] = [b.barrier_token("s_m"), b.barrier_token("t_m"),
                                b.barrier_token("g_m")]
        self.tok["eav_free"] = b.barrier_token("t_m")
        self.tok["aT_free"] = [b.barrier_token("v_m"), b.barrier_token("s_m")]
        self.tok["dxt_free"] = b.barrier_token("t_m")
        self.tok["dxw_free"] = b.barrier_token("t_m")
        self.tok["sst_free"] = b.barrier_token("v_m")
        self.tok["bT_free"] = b.barrier_token("t_m")
    self.tok["scan_rb_done"] = b.barrier_token("t_m")
    self.tok["scan_y_done"] = b.barrier_token("v_m")


def _ssm_ag_send(self, sd):
    b, c = self.b, self.c
    hss, inter = c["hssm"], c["inter"]
    cvch = c["convdim"] // 128
    ich = inter // 128
    hh = c["hc"] // 2
    Y0 = 2 * cvch * hh
    ytoks = [self.tok.get(("ydma_last", 0)), self.tok.get(("ydma_last", 1))]
    self.tok[("ydma_side", sd)] = ytoks
    self.tok["ra_free"] = self.tok["scan_y_done"]
    t1 = self.D(lambda e, H, sd=sd: e.dma_start(
        out=self.dram_ap("dsin")[sd * 128:(sd + 1) * 128, :], in_=H["ssm_s"][:, :]),
        waits=[self.tok["sstate"], self.tok.get("sm2_chain")], ch="d_sm2")
    self.tok["sm2_chain"] = t1
    nch = c["hc"] // 128
    tr = self.T(lambda e, H, nch=nch: e.transpose(H["pstf"][0:1, 128:128 + hss],
                                         H["facc"][0:hss, nch:nch + 1],
                                         H["identf_s"][0:hss, 0:hss]),
                waits=[self.tok["facc"], self.tok.get("pstf_free")])
    c2 = self.V(lambda e, H: e.tensor_copy(H["alr"][0:1, 2 * hss:3 * hss],
                                           H["pstf"][0:1, 128:128 + hss]), waits=[tr])
    self.tok["pstf_free"] = c2
    t2 = self.D(lambda e, H, sd=sd: e.dma_start(
        out=self.dram_ap("dcin")[sd:sd + 1, :], in_=H["alr"][0:1, 2 * hss:3 * hss]),
        waits=[c2, self.tok.get("sm2_chain")], ch="d_sm2")
    self.tok["sm2_chain"] = t2
    self.tok["alr_free"] = t2
    self.tok["ssm_s_free"] = t1
    self.tok[("ag_send", sd)] = b.barrier_token("d_sm2")


Prog._ssm_scan = _ssm_scan
Prog._ssm_ag_send = _ssm_ag_send


def _ssm_post(self):
    b, c = self.b, self.c
    d, hc, dch, T = c["d"], c["hc"], c["dch"], c["T"]
    hss, inter, cvd = c["hssm"], c["inter"], c["convdim"]
    cvch, ich = cvd // 128, inter // 128
    hh = hc // 2
    nch = hc // 128
    wrot = ["wa", "wb"]
    wfree = self.wfree
    ydma_fin = [t for t in (self.tok.get(("ydma_last", 0)),
                            self.tok.get(("ydma_last", 1))) if t]
    self.tok["stgc_free"] = list(ydma_fin)
    t1 = self.CC(lambda e, H: e.collective_compute(
        "AllGather", ALU.bypass, replica_groups=[list(range(c["nc"]))],
        ins=[self.dram_h("dsin").ap()], outs=[self.dram_h("dsout").ap()]),
        waits=[self.tok[("ag_send", 0)], self.tok[("ag_send", 1)]])
    t2 = self.CC(lambda e, H: e.collective_compute(
        "AllGather", ALU.bypass, replica_groups=[list(range(c["nc"]))],
        ins=[self.dram_h("dcin").ap()], outs=[self.dram_h("dcout").ap()]),
        waits=[self.tok[("ag_send", 0)], self.tok[("ag_send", 1)]])
    tl = self.D(lambda e, H: e.dma_start(
        out=H["hcd"][0:1, 0:16 * hss], in_=self.dram_ap("dcout").flatten().unsqueeze(0)),
        waits=[t2], ch="d_kg")

    for sd in (0, 1):
        t0 = sd * hc
        def hrow(u):
            rk = u if sd == 0 else (c["nc"] - 1 - u)
            return (slice(0, 1), slice((2 * rk + sd) * hss, (2 * rk + sd) * hss + hss))
        z = self.V(lambda e, H: e.memset(H["cgr"][0:1, 7 * hss:8 * hss], 0.0),
                   waits=[self.tok.get("cgr_free")])
        pv = z
        for g in range(6, -1, -1):
            pv = self.V(lambda e, H, g=g, sd=sd: e.scalar_tensor_tensor(
                H["cgr"][0:1, g * hss:(g + 1) * hss], H["hcd"][hrow(g + 1)],
                H["smk_s"][0:1, sd * 8 + g + 1:sd * 8 + g + 2],
                H["cgr"][0:1, (g + 1) * hss:(g + 2) * hss], ALU.mult, ALU.add),
                waits=[tl, pv, self.tok["consts"]])
        ce = self.S(lambda e, H: e.activation(H["cgr"][0:1, 0:8 * hss],
                                              H["cgr"][0:1, 0:8 * hss], AF.Exp),
                    waits=[pv])
        cm = pv
        cms = []
        for g in range(8):
            cg = self.V(lambda e, H, g=g, sd=sd: e.tensor_scalar(
                H["cgr"][0:1, g * hss:(g + 1) * hss], H["cgr"][0:1, g * hss:(g + 1) * hss],
                H["smk_s"][0:1, sd * 8 + g:sd * 8 + g + 1], None, ALU.mult, ALU.bypass),
                waits=[ce])
            cms.append(cg)
        sz = self.V(lambda e, H: e.memset(H["ssm_s"][:, :], 0.0),
                    waits=[self.tok.get("ssm_s_free")])
        sprev = sz
        dsl = None
        cw = inter // 2
        hpb2 = hss // 2
        for g in range(8):
            _, cb_ = self.pbc("ps4", hss, lambda H, g=g: H["cgr"][0:1, g * hss:(g + 1) * hss],
                              lambda H: H["expand"][:, 0:hss],
                              waits=[cms[g]], cwaits=[self.tok.get("expand_free")])
            for half2 in range(2):
                rk = g if sd == 0 else (c["nc"] - 1 - g)
                r2 = self.D(lambda e, H, rk=rk, sd=sd, half2=half2, cw=cw: e.dma_start(
                    out=H["wsp"][:, 0:cw],
                    in_=self.dram_ap("dsout")[rk * 256 + sd * 128:rk * 256 + sd * 128 + 128,
                                              half2 * cw:(half2 + 1) * cw]),
                    waits=[t1, dsl], ch="d_v0")
                ul = None
                for hi in range(hpb2):
                    h = half2 * hpb2 + hi
                    ul = self.V(lambda e, H, h=h, hi=hi: e.scalar_tensor_tensor(
                        H["ssm_s"][:, h * 64:(h + 1) * 64],
                        H["wsp"][:, hi * 64:(hi + 1) * 64],
                        H["expand"][:, h:h + 1],
                        H["ssm_s"][:, h * 64:(h + 1) * 64], ALU.mult, ALU.add),
                        waits=[cb_, r2, sprev])
                dsl = ul
                sprev = ul
            self.tok["expand_free"] = ul
        sb1 = self.V(lambda e, H: e.tensor_copy(H["ssm_sb"][:, :], H["ssm_s"][:, :]),
                     waits=[sprev, self.tok.get("ssm_sb_free")])
        self.tok["cgr_free"] = b.barrier_token("t_m")
        yre = []
        for half in (0, 1):
            yo2 = half * ich * hh
            tk = self.D(lambda e, H, yo2=yo2, half=half, sd=sd: e.dma_start(
                out=H["big"][:, yo2:yo2 + ich * hh].rearrange("p (i t) -> p i t", t=hh),
                in_=self.dram_ap("ytmp").rearrange(
                    "(i p) t -> p i t", p=128)[:, :, sd * hc + half * hh:
                                               sd * hc + half * hh + hh]),
                waits=[self.tok.get("yb_free"), self.tok[("ag_send", sd)]]
                      + self.tok[("ydma_side", sd)], ch="d_v1")
            yre.append(tk)
        for half in range(2):
            yo2 = half * ich * hh
            for hp in range(hss // 2):
                for par in range(2):
                    h = 2 * hp + par
                    g5 = self.T(lambda e, H, h=h, half=half, sd=sd: e.matmul(
                        H["ps3"][:, 0:hh],
                        H["identf_s"][0:hss, h:h + 1].broadcast_to([hss, 128]),
                        H["ea2"][0:hss, sd * hc + half * hh: sd * hc + (half + 1) * hh],
                        start=True, stop=True),
                        waits=[self.bankfree.get("ps3"), b.barrier_token("s_m")])
                    g6 = self.V(lambda e, H, half=half, sd=sd: e.tensor_tensor(
                        H["stg"][:, 0:hh],
                        H["cp"][:, sd * hc + half * hh: sd * hc + (half + 1) * hh],
                        H["ps3"][:, 0:hh], ALU.mult),
                        waits=[g5, self.tok.get("stgc_free")])
                    self.bankfree["ps3"] = g6
                    self.tok["bcc_free"] = g6
                    tp = (0, 64) if par == 1 else None
                    ps_ap = (slice(64, 128), slice(0, hh)) if par == 1 else (slice(0, 64), slice(0, hh))
                    mo = self.T(lambda e, H, h=h, ps_ap=ps_ap, tp=tp: e.matmul(
                        H["ps2"][ps_ap], H["ssm_sb"][:, h * 64:(h + 1) * 64],
                        H["stg"][:, 0:hh], start=True, stop=True,
                        tile_position=tp, skip_group_check=True),
                        waits=[g6, sb1] + ([self.bankfree.get("ps2")] if par == 0 else []))
                    self.tok["stgc_free"] = mo
                ya = self.V(lambda e, H, hp=hp, yo2=yo2: e.tensor_tensor(
                    H["big"][:, yo2 + hp * hh: yo2 + (hp + 1) * hh],
                    H["big"][:, yo2 + hp * hh: yo2 + (hp + 1) * hh], H["ps2"][:, 0:hh],
                    ALU.add), waits=[b.barrier_token("t_m")] + yre)
                self.bankfree["ps2"] = ya
        ycorr = b.barrier_token("v_m")
        self.tok["stg_free"] = b.barrier_token("v_m")
        lt2 = self.ln_quant(t0, False, res_tok=self.tok.get("res"))
        self.tok["act_free"] = b.barrier_token("t_m")
        for half in range(2):
            for gi, oc0 in enumerate(range(0, ich, 3)):
                gsz = min(3, ich - oc0)
                banks = ["ps0", "ps1", "ps3"] if gi % 2 == 0 else ["ps4", "ps5", "ps2"]
                mlast = {}
                for icb in range(0, dch, 4):
                    ics = min(4, dch - icb)
                    wbuf = wrot[self._dk % 2]
                    wch = "d_a" if self._dk % 2 == 0 else "d_b"
                    self._dk += 1
                    tl2 = self.D(lambda e, H, wbuf=wbuf, icb=icb, ics=ics, oc0=oc0,
                                 gsz=gsz: e.dma_start(
                        out=H[wbuf][:, 0:ics * gsz * 128].rearrange(
                            "p (i f) -> p i f", i=ics),
                        in_=self.dram_ap("q_l1zx").rearrange(
                            "(i p) f -> p i f", p=128)[:, icb:icb + ics,
                            oc0 * 128:(oc0 + gsz) * 128]),
                        waits=[wfree[wbuf], self.tok["q_l1zx"]], ch=wch)
                    for ii in range(ics):
                        ic = icb + ii
                        for oi in range(gsz):
                            bk = banks[oi]
                            wt = [tl2, lt2]
                            if ic == 0:
                                wt.append(self.bankfree.get(bk))
                            mm = self.T(lambda e, H, bk=bk, wbuf=wbuf, ii=ii, oi=oi,
                                        gsz=gsz, ic=ic, half=half: e.matmul(
                                H[bk][:, 0:hh],
                                H[wbuf][:, (ii * gsz + oi) * 128:(ii * gsz + oi + 1) * 128],
                                H["act"][:, ic * hc + half * hh: ic * hc + half * hh + hh],
                                start=(ic == 0), stop=(ic == dch - 1)), waits=wt)
                            mlast[oi] = mm
                    wfree[wbuf] = mlast[gsz - 1]
                yo2 = half * ich * hh
                for oi in range(gsz):
                    oc = oc0 + oi
                    bk = banks[oi]
                    slot = oc % 4
                    sl = self.S(lambda e, H, bk=bk, slot=slot: e.activation(
                        H["stg"][:, slot * hh:(slot + 1) * hh], H[bk][:, 0:hh], AF.Silu),
                        waits=[mlast[oi], self.tok.get(("zslot", slot)),
                               self.tok.get("stgc_free"), ydma_fin])
                    gm = self.V(lambda e, H, oc=oc, yo2=yo2, slot=slot: e.tensor_tensor(
                        H["big"][:, yo2 + oc * hh: yo2 + (oc + 1) * hh],
                        H["big"][:, yo2 + oc * hh: yo2 + (oc + 1) * hh],
                        H["stg"][:, slot * hh:(slot + 1) * hh], ALU.mult),
                        waits=[sl, ycorr])
                    self.bankfree[bk] = sl
                    self.tok[("zslot", slot)] = gm
            gdone = b.barrier_token("v_m")
            yo2 = half * ich * hh
            sms = None
            for ch in range(ich):
                sq = self.V(lambda e, H, ch=ch, yo2=yo2: e.tensor_tensor(
                    H["sqb"][:, 0:hh], H["big"][:, yo2 + ch * hh: yo2 + (ch + 1) * hh],
                    H["big"][:, yo2 + ch * hh: yo2 + (ch + 1) * hh], ALU.mult),
                    waits=[gdone, self.tok.get("sqb_free")])
                sm = self.T(lambda e, H, ch=ch: e.matmul(
                    H["ps0"][0:1, 0:hh], H["ones_s"][:, 0:1], H["sqb"][:, 0:hh],
                    start=(ch == 0), stop=(ch == ich - 1)),
                    waits=[sq, self.tok["consts"]] + ([self.bankfree.get("ps0")] if ch == 0 else []))
                self.tok["sqb_free"] = sm
                sms = sm
            v1 = self.V(lambda e, H: e.tensor_scalar(
                H["rb"][0:1, 0:hh], H["ps0"][0:1, 0:hh], 1.0 / inter, None,
                ALU.mult, ALU.bypass), waits=[sms])
            self.bankfree["ps0"] = v1
            s1 = self.S(lambda e, H: e.activation(H["rb"][0:1, 0:hh], H["rb"][0:1, 0:hh],
                                                  AF.Ln, bias=H["cc_s"][0:1, 0:1]),
                        waits=[v1])
            s2 = self.S(lambda e, H: e.activation(H["rb"][0:1, 0:hh], H["rb"][0:1, 0:hh],
                                                  AF.Exp, scale=-0.5), waits=[s1])
            _, pb1 = self.pbc("ps1", hh, lambda H: H["rb"][0:1, 0:hh],
                              lambda H: H["bc"][:, 0:hh],
                              waits=[s2], cwaits=[self.tok.get("bcc_free")])
            napply = []
            for ch in range(ich):
                na = self.V(lambda e, H, ch=ch, yo2=yo2: e.scalar_tensor_tensor(
                    H["big"][:, yo2 + ch * hh: yo2 + (ch + 1) * hh],
                    H["big"][:, yo2 + ch * hh: yo2 + (ch + 1) * hh],
                    H["normw_s"][:, ch:ch + 1], H["bc"][:, 0:hh], ALU.mult, ALU.mult),
                    waits=[pb1, lt2, self.tok.get("nrm_free")])
                napply.append(na)
            self.tok["bcc_free"] = napply[-1]
            ntok = b.barrier_token("v_m")
            for gi, oc0 in enumerate(range(0, dch, 3)):
                gsz = min(3, dch - oc0)
                banks = ["ps0", "ps1", "ps2"] if gi % 2 == 0 else ["ps3", "ps4", "ps5"]
                mlast = {}
                for icb in range(0, ich, 4):
                    ics = min(4, ich - icb)
                    wbuf = wrot[self._dk % 2]
                    wch = "d_a" if self._dk % 2 == 0 else "d_b"
                    self._dk += 1
                    tl3 = self.D(lambda e, H, wbuf=wbuf, icb=icb, ics=ics, oc0=oc0,
                                 gsz=gsz: e.dma_start(
                        out=H[wbuf][:, 0:ics * gsz * 128].rearrange(
                            "p (i f) -> p i f", i=ics),
                        in_=self.dram_ap("q_l1wo").rearrange(
                            "(i p) f -> p i f", p=128)[:, icb:icb + ics,
                            oc0 * 128:(oc0 + gsz) * 128]),
                        waits=[wfree[wbuf], self.tok["q_l1wo"]], ch=wch)
                    for ii in range(ics):
                        ic = icb + ii
                        for oi in range(gsz):
                            bk = banks[oi]
                            wt = [tl3, ntok]
                            if ic == 0:
                                wt.append(self.bankfree.get(bk))
                            mm = self.T(lambda e, H, bk=bk, wbuf=wbuf, ii=ii, oi=oi,
                                        gsz=gsz, ic=ic, yo2=yo2: e.matmul(
                                H[bk][:, 0:hh],
                                H[wbuf][:, (ii * gsz + oi) * 128:(ii * gsz + oi + 1) * 128],
                                H["big"][:, yo2 + ic * hh: yo2 + (ic + 1) * hh],
                                start=(ic == 0), stop=(ic == ich - 1)), waits=wt)
                            mlast[oi] = mm
                    wfree[wbuf] = mlast[gsz - 1]
                for oi in range(gsz):
                    oc = oc0 + oi
                    bk = banks[oi]
                    rc = (slice(None), slice(oc * T + t0 + half * hh,
                                             oc * T + t0 + half * hh + hh))
                    e2 = self.V(lambda e, H, rc=rc, bk=bk: e.tensor_tensor(
                        H["res"][rc], H["res"][rc], H[bk][:, 0:hh], ALU.add),
                        waits=[mlast[oi], b.barrier_token("v_m")])
                    self.bankfree[bk] = e2
            self.tok["nrm_free"] = b.barrier_token("t_m")
        self.tok["yb_free"] = b.barrier_token("v_m")
        self.tok["ssm_s_free"] = b.barrier_token("t_m")
        self.tok["ssm_sb_free"] = b.barrier_token("t_m")
    self.tok["res"] = b.barrier_token("v_m")
    self.tok["act_free"] = b.barrier_token("t_m")
    self.tok["big_free"] = [b.barrier_token("t_m"), b.barrier_token("v_m")]
    self.tok["stg_free"] = [b.barrier_token("t_m"), b.barrier_token("v_m")]
    self.tok["uf_free"] = [b.barrier_token(s) for s in ("v_m", "s_m", "g_m", "t_m")]


Prog._ssm_post = _ssm_post


# version 11
# speedup vs baseline: 1.1303x; 1.0023x over previous
import os
import numpy as np
import ml_dtypes
from contextlib import ExitStack

import concourse.bass as bass
import concourse.mybir as mybir
from concourse.bass_utils import run_bass_kernel_spmd
from concourse import library_config
from bass_rust import ReduceOp

F32 = mybir.dt.float32
BF16 = mybir.dt.bfloat16
I32 = mybir.dt.int32
AF = mybir.ActivationFunctionType
ALU = mybir.AluOpType
AX = mybir.AxisListType

MAGIC = 16384.0
_TRUNC_CAST = (os.environ.get("BASS_SIM_FP64") == "1"
               or os.environ.get("BASS_INTERP_TRUNC") == "1")
RND = MAGIC + 0.5 if _TRUNC_CAST else 0.0
NMAG = -MAGIC if _TRUNC_CAST else 0.0
NEG = -1e30

def real_cfg():
    return dict(d=2048, hc=512, B=2, nc=8, dff0=8192, dff1=5632,
                inter=4096, N=128, P=64, K=4, eps=1e-5)

def derived(cfg):
    c = dict(cfg)
    c["T"] = 2 * c["hc"]
    c["L"] = c["nc"] * c["hc"]
    c["heads0"] = c["d"] // 128
    c["hssm"] = c["inter"] // c["P"]
    c["convdim"] = c["inter"] + 2 * c["N"]
    c["dch"] = c["d"] // 128
    return c


class Builder:

    ENGINES = ("tensor", "vector", "scalar", "gpsimd", "sync")

    def __init__(self, cfg):
        self.cfg = derived(cfg)
        self.nc = bass.Bass("TRN2", target_bir_lowering=False, debug=False)
        self.ops = {e: [] for e in self.ENGINES}
        self.cnt = {}
        self.waited = {e: {} for e in self.ENGINES}
        self.sems = {}
        self.sem_names = []
        self.stack = None
        self._sb = {}
        self._ps = {}
        self._dram = {}
        self._finalized = False

    def sem(self, name):
        if name not in self.cnt:
            self.cnt[name] = 0
            self.sem_names.append(name)
        return name

    def sbuf(self, name, shape, dtype):
        self._sb[name] = (shape, dtype)
        return name

    def alias(self, name, base, c0, shape, dtype=None):
        self._alias = getattr(self, "_alias", {})
        self._alias[name] = (base, c0, shape, dtype)
        return name

    def psum(self, name, shape, dtype=F32):
        self._ps[name] = (shape, dtype)
        return name

    def dram(self, name, shape, dtype, kind=None):
        if kind is None:
            h = self.nc.dram_tensor(name, list(shape), dtype)
        else:
            h = self.nc.dram_tensor(name, list(shape), dtype, kind=kind)
        self._dram[name] = h
        return name

    SEM_ENGINE = {"t": "tensor", "v": "vector", "s": "scalar", "g": "gpsimd",
                  "d": None, "c": None}

    def eng_of_sem(self, s):
        return self.SEM_ENGINE.get(s.split("_")[0], None)

    def op(self, eng, fn, waits=(), inc=None, inc_k=1):
        flat = []
        for w in waits:
            if w is None:
                continue
            if isinstance(w, list):
                flat.extend(x for x in w if x is not None)
            else:
                flat.append(w)
        waits = flat
        for (s, v) in waits:
            assert v <= self.cnt[s], (s, v, self.cnt[s])
        tok = None
        if inc is not None:
            self.cnt[inc] += inc_k
            tok = (inc, self.cnt[inc])
        byname = {}
        for (s, v) in waits:
            byname[s] = max(v, byname.get(s, 0))
        waits = list(byname.items())
        wcur = self.waited[eng]
        need = []
        drain = False
        for (s, v) in waits:
            if self.eng_of_sem(s) == eng:
                if wcur.get(s, 0) < v:
                    drain = True
                    wcur[s] = v
                continue
            if wcur.get(s, 0) < v:
                need.append((s, v))
                wcur[s] = v
        self.ops[eng].append((fn, tuple(need), inc, inc_k, drain))
        return tok

    def barrier_token(self, sem):
        return (sem, self.cnt[sem])

    def finalize(self):
        assert not self._finalized
        self._finalized = True
        nc = self.nc
        self.stack = ExitStack()
        E = self.stack.enter_context
        sb = {n: E(nc.sbuf_tensor(n, list(s), d)) for n, (s, d) in self._sb.items()}
        ps = {n: E(nc.psum_tensor(n, list(s), d)) for n, (s, d) in self._ps.items()}
        sems = {n: E(nc.semaphore(n)) for n in self.sem_names}
        self.sb_h, self.ps_h, self.sem_h = sb, ps, sems
        block = E(nc.Block())
        handles = dict(sb)
        handles.update(ps)
        for name, (base, c0, shape, dtype) in getattr(self, "_alias", {}).items():
            ap = sb[base].ap()
            if dtype is not None:
                ap = ap.bitcast(dtype)
            handles[name] = ap[0:shape[0], c0:c0 + shape[1]]

        def run_engine(eng_name):
            def body(eng):
                for (fn, waits, inc, inc_k, drain) in self.ops[eng_name]:
                    if drain:
                        eng.drain()
                    for (s, v) in waits:
                        eng.wait_ge(sems[s], v)
                    inst = fn(eng, handles)
                    if inc is not None:
                        assert inst is not None, f"op on {eng_name} has inc but returned None"
                        inst.then_inc(sems[inc], inc_k)
            return body

        block.sync(run_engine("sync"))
        block.vector(run_engine("vector"))
        block.scalar(run_engine("scalar"))
        block.tensor(run_engine("tensor"))
        block.gpsimd(run_engine("gpsimd"))
        self.stack.close()
        return nc


class Prog:
    def __init__(self, b: Builder):
        self.b = b
        self.c = b.cfg
        self.tok = {}
        self.bankfree = {}
        self.wfree = {"wa": None, "wb": None}
        self._dk = 0
        self._qk = 0

    def declare(self):
        b, c = self.b, self.c
        d, T, hc = c["d"], c["T"], c["hc"]
        dch = c["dch"]
        s8 = d // 8
        b.dram("xs", [d, T], F32, kind="ExternalInput")
        b.dram("l0w1t_s", [s8, c["dff0"]], F32, kind="ExternalInput")
        b.dram("l0w2t_s", [c["dff0"] // 8, d], F32, kind="ExternalInput")
        b.dram("l0att_s", [s8, 4 * d], F32, kind="ExternalInput")
        b.dram("l1w1t_s", [s8, c["dff1"]], F32, kind="ExternalInput")
        b.dram("l1w3t_s", [s8, c["dff1"]], F32, kind="ExternalInput")
        b.dram("l1w2t_s", [c["dff1"] // 8, d], F32, kind="ExternalInput")
        b.dram("l1zx_s", [s8, c["inter"] + c["convdim"]], F32, kind="ExternalInput")
        b.dram("l1wo_s", [c["inter"] // 8, d], F32, kind="ExternalInput")
        b.dram("l1wdt", [d, c["hssm"]], F32, kind="ExternalInput")
        b.dram("convw", [128, (c["convdim"] // 128) * c["K"]], F32, kind="ExternalInput")
        b.dram("convb", [128, c["convdim"] // 128], F32, kind="ExternalInput")
        b.dram("smallv", [c["hssm"], 4], F32, kind="ExternalInput")
        b.dram("normw", [128, c["inter"] // 128], F32, kind="ExternalInput")
        b.dram("ident", [128, 128], BF16, kind="ExternalInput")
        b.dram("identf", [128, 128], F32, kind="ExternalInput")
        b.dram("ssdmask", [128, 128], F32, kind="ExternalInput")
        b.dram("onesb", [128, 8], BF16, kind="ExternalInput")
        b.dram("cconst", [128, 8], F32, kind="ExternalInput")
        b.dram("qidxb", [128, hc], F32, kind="ExternalInput")
        b.dram("qposr", [1, 2], F32, kind="ExternalInput")
        b.dram("dcolx", [128, c["inter"] // 128], F32, kind="ExternalInput")
        b.dram("onest", [c["hssm"], 128], F32, kind="ExternalInput")
        b.dram("smk", [1, 16], F32, kind="ExternalInput")
        b.dram("bsel", [128, 16], F32, kind="ExternalInput")
        b.dram("knm_i", [1, c["heads0"]], F32)
        b.dram("qtmp", [d, hc], BF16)
        b.dram("ytmp", [c["inter"], T], BF16)
        b.dram("drow", [1, 64], F32)
        b.dram("knm_o", [1, c["heads0"]], F32)
        b.dram("ys", [d, T], F32, kind="ExternalOutput")
        if os.environ.get("KDBG_F1"):
            b.dram("dbg_h", [128, 24576], BF16, kind="ExternalOutput")
            b.dram("dbg_q", [128, 24576], BF16, kind="ExternalOutput")
            b.dram("dbg_s", [128, 512], F32, kind="ExternalOutput")
        b.dram("q_l0w1", [d, c["dff0"]], BF16)
        b.dram("q_l0w2", [c["dff0"], d], BF16)
        b.dram("q_l0att", [d, 4 * d], BF16)
        b.dram("q_l1w1", [d, c["dff1"]], BF16)
        b.dram("q_l1w3", [d, c["dff1"]], BF16)
        b.dram("q_l1w2", [c["dff1"], d], BF16)
        b.dram("q_l1zx", [d, c["inter"] + c["convdim"]], BF16)
        b.dram("q_l1wo", [c["inter"], d], BF16)
        for nm, sh in [("i_l0w1", [s8, c["dff0"]]), ("i_l0w2", [c["dff0"] // 8, d]),
                       ("i_l0att", [s8, 4 * d]), ("i_l1w1", [s8, c["dff1"]]),
                       ("i_l1w3", [s8, c["dff1"]]), ("i_l1w2", [c["dff1"] // 8, d]),
                       ("i_l1zx", [s8, c["inter"] + c["convdim"]]), ("i_l1wo", [c["inter"] // 8, d])]:
            b.dram(nm, sh, BF16)
        b.dram("arin", [1, 8], F32)
        b.dram("arout", [1, 8], F32)
        b.dram("kin", [d, T], BF16)
        b.dram("vin", [T, d], BF16)
        b.dram("kout", [8 * d, T], BF16)
        b.dram("vout", [8 * T, d], BF16)
        b.dram("bdin", [d, 6], F32)
        b.dram("bdout", [8 * d, 6], F32)
        b.dram("dsin", [256, c["inter"]], F32)
        b.dram("dsout", [8 * 256, c["inter"]], F32)
        b.dram("dcin", [2, c["hssm"]], F32)
        b.dram("dcout", [16, c["hssm"]], F32)

        cvch = c["convdim"] // 128
        ich = c["inter"] // 128
        hh = hc // 2
        big_cols = max((c["dff0"] // 128) * hc,
                       2 * (c["dff1"] // 128) * hh,
                       2 * cvch * hh + 2 * c["inter"] + 8 * hh + c["inter"],
                       3 * dch * hc + 2 * hc + (hc // 128) * hc)
        self.big_cols = big_cols
        b.sbuf("res", [128, dch * T], F32)
        b.sbuf("act", [128, dch * hc], BF16)
        b.sbuf("big", [128, big_cols], BF16)
        b.sbuf("wa", [128, 1536], BF16)
        b.sbuf("wb", [128, 1536], BF16)
        b.sbuf("wsp", [128, 2048], F32)
        b.sbuf("bc", [128, 5 * hc], F32)
        b.sbuf("rb", [1, 3 * hc], F32)
        b.sbuf("stg", [128, 1024], BF16)
        b.sbuf("sml", [128, 64], F32)
        b.sbuf("sqb", [128, 512], BF16)
        b.sbuf("gtl", [128, 384], BF16)
        b.sbuf("bdx", [128, cvch * 3], BF16)
        b.sbuf("bdx2", [128, cvch * 3], BF16)
        b.sbuf("blnx", [128, dch * 3], BF16)
        b.sbuf("ident_s", [128, 128], BF16)
        b.sbuf("identf_s", [128, 128], F32)
        b.sbuf("ssdm_s", [128, 128], F32)
        b.sbuf("ones_s", [128, 8], BF16)
        b.sbuf("cc_s", [128, 8], F32)
        b.sbuf("normw_s", [128, ich], F32)
        b.sbuf("dcolx_s", [128, ich], F32)
        b.sbuf("bsel_s", [128, 16], F32)
        b.sbuf("smk_s", [1, 16], F32)
        b.sbuf("qpos_s", [1, 2], F32)
        b.sbuf("smv_s", [c["hssm"], 4], F32)
        hss = c["hssm"]
        UF = 7040
        b.sbuf("uf32", [128, UF], F32)
        al = b.alias
        al("expand", "uf32", 0, [128, c["inter"] // 2])
        al("dts", "uf32", c["inter"] // 2, [hss, hc])
        al("ea2", "uf32", c["inter"] // 2 + hc, [hss, T])
        al("hcd", "uf32", c["inter"] // 2 + hc + T, [1, 16 * hss])
        base = c["inter"] // 2 + hc + T + 16 * hss
        for nm, cols, rows in [("alr", 4 * hss, 1), ("cgr", 8 * hss, 1),
                               ("eav", 128, hss), ("dta", 128, hss), ("afm", 128, hss),
                               ("aT", hss, 128), ("aTn", hss, 128), ("dw", hss, 128),
                               ("wexp", hss, 128), ("sst", 128, 128), ("emt", 128, 128),
                               ("bres", dch * 3, 128), ("facc", 16, hss),
                               ("onest_s", 128, hss), ("convw_s", cvch * c["K"], 128),
                               ("convb_s", cvch, 128), ("bdl", dch * 4 * 6, 128)]:
            al(nm, "uf32", base, [rows, cols])
            base += cols
        assert base <= UF, base
        al("dn", "uf32", 0, [1, c["heads0"] * hc], BF16)
        al("attM", "uf32", c["heads0"] * hc // 2, [128, 40])
        al("arow", "uf32", c["heads0"] * hc // 2 + 40, [1, 64])
        al("qsq", "uf32", c["heads0"] * hc // 2 + 104, [128, hc])
        al("qidx_s", "uf32", c["heads0"] * hc // 2 + 104 + hc, [128, hc])
        al("chmx", "uf32", 0, [128, hc])
        al("chmn", "uf32", hc, [128, hc])
        al("wld", "uf32", 0, [128, hc])
        al("wld2", "uf32", hc, [128, hc])
        al("wi32", "uf32", 2 * hc, [128, hc], I32)
        al("wfq", "uf32", 6 * hc, [128, 2 * hc], BF16)
        al("dxt", "big", 0, [128, c["inter"]])
        al("dxw", "big", c["inter"], [128, c["inter"]])
        CP0 = 2 * cvch * hh
        al("cp", "big", CP0, [128, T])
        al("ssm_sb", "big", CP0 + T, [128, c["inter"]])
        al("ssm_s", "big", (CP0 + T + c["inter"]) // 2, [128, c["inter"]], F32)
        assert CP0 + T + 3 * c["inter"] <= big_cols, (CP0, big_cols)
        assert 2 * c["inter"] <= cvch * hh, (c["inter"], cvch * hh)
        for i in range(6):
            b.psum(f"ps{i}", [128, 512], F32)
        b.psum("pst", [128, 1024], BF16)
        b.psum("pstf", [128, 512], F32)
        for s in ["t_m", "v_m", "s_m", "g_m", "d_c", "d_a", "d_as", "d_b", "d_s",
                  "d_y", "d_kv", "d_kv2", "d_q", "d_kg", "d_v0", "d_v1", "d_v2", "d_v3",
                  "d_sm1", "d_sm2", "d_g", "c_c"]:
            b.sem(s)

    def T(self, fn, waits=(), inc=True):
        return self.b.op("tensor", fn, waits, "t_m" if inc else None)

    def V(self, fn, waits=(), inc=True):
        return self.b.op("vector", fn, waits, "v_m" if inc else None)

    def S(self, fn, waits=(), inc=True):
        return self.b.op("scalar", fn, waits, "s_m" if inc else None)

    def G(self, fn, waits=(), inc=True):
        return self.b.op("gpsimd", fn, waits, "g_m" if inc else None)

    CHQ = {"d_as": "scalar"}

    def D(self, fn, waits=(), ch="d_a"):
        return self.b.op(self.CHQ.get(ch, "sync"), fn, waits, ch, inc_k=16)

    def GD(self, fn, waits=()):
        return self.b.op("gpsimd", fn, waits, "d_g", inc_k=16)

    def CC(self, fn, waits=()):
        t = self.b.op("gpsimd", fn, list(waits) + [self.tok.get("cc_prev")], "c_c", inc_k=1)
        self.tok["cc_prev"] = t
        return t

    def pbc(self, bank, W, srcf, dstf=None, waits=(), cwaits=(), bf16=False, P=128):
        ones = (lambda H: H["ones_s"][0:1, 0:1].broadcast_to([1, P])) if bf16 \
            else (lambda H: H["identf_s"][0:1, 0:1].broadcast_to([1, P]))
        mm = self.T(lambda e, H: e.matmul(
            H[bank][0:P, 0:W], ones(H), srcf(H), start=True, stop=True),
            waits=list(waits) + [self.bankfree.get(bank), self.tok["consts"]])
        if dstf is None:
            return mm, None
        cp = self.V(lambda e, H: e.tensor_copy(dstf(H), H[bank][0:P, 0:W]),
                    waits=[mm] + list(cwaits))
        self.bankfree[bank] = cp
        return mm, cp

    def prologue(self):
        b, c = self.b, self.c
        d, T, hc, dch = c["d"], c["T"], c["hc"], c["dch"]

        consts = [("ident_s", "ident"),
                  ("ssdm_s", "ssdmask"), ("ones_s", "onesb"), ("cc_s", "cconst"),
                  ("normw_s", "normw"),
                  ("identf_s", "identf"),
                  ("qpos_s", "qposr"), ("dcolx_s", "dcolx"),
                  ("smk_s", "smk"), ("bsel_s", "bsel")]
        for dst, src in consts:
            self.D(lambda e, H, dst=dst, src=src: e.dma_start(out=H[dst][:], in_=self.dram_ap(src)),
                   ch="d_c")
        self.D(lambda e, H: e.dma_start(out=H["smv_s"][:], in_=self.dram_ap("smallv")), ch="d_c")
        for ch in range(dch):
            self.D(lambda e, H, ch=ch: e.dma_start(
                out=H["res"][:, ch * T:(ch + 1) * T],
                in_=self.dram_ap("xs")[ch * 128:(ch + 1) * 128, :]), ch="d_c")
        self.tok["consts"] = b.barrier_token("d_c")
        self.tok["x"] = self.tok["consts"]

        tc = self.S(lambda e, H: e.activation(H["smv_s"][:, 3:4], H["smv_s"][:, 1:2], AF.Exp),
                    waits=[self.tok["consts"]])
        self.tok["acol"] = self.V(lambda e, H: e.tensor_scalar(
            H["smv_s"][:, 1:2], H["smv_s"][:, 3:4], -1.0, None, ALU.mult, ALU.bypass),
            waits=[tc])



        self.WMETA = [
            ("l0w1", "l0w1t_s", "i_l0w1", "q_l0w1", d // 8, c["dff0"], d * c["dff0"]),
            ("l0w2", "l0w2t_s", "i_l0w2", "q_l0w2", c["dff0"] // 8, d, d * c["dff0"]),
            ("l1w1", "l1w1t_s", "i_l1w1", "q_l1w1", d // 8, c["dff1"], d * c["dff1"]),
            ("l1w3", "l1w3t_s", "i_l1w3", "q_l1w3", d // 8, c["dff1"], d * c["dff1"]),
            ("l1w2", "l1w2t_s", "i_l1w2", "q_l1w2", c["dff1"] // 8, d, d * c["dff1"]),
        ]
        CW = c["hc"]
        vdone = [None, None]
        tz = self.V(lambda e, H: e.memset(H["sml"][:, 0:64], 0.0), waits=[])
        last_ta = tz
        k = 0
        for j, (nm, snm, inm, qnm, rows, cols, nw) in enumerate(self.WMETA):
            for r0 in range(0, rows, 128):
                rh = min(128, rows - r0)
                for c0 in range(0, cols, CW):
                    cw = min(CW, cols - c0)
                    buf = "wld" if (k % 2 == 0) else "wld2"
                    tl = self.D(lambda e, H, buf=buf, snm=snm, r0=r0, rh=rh, c0=c0, cw=cw:
                                e.dma_start(out=H[buf][0:rh, 0:cw],
                                            in_=self.dram_ap(snm)[r0:r0 + rh, c0:c0 + cw]),
                                waits=[vdone[k % 2]], ch=("d_a" if k % 2 == 0 else "d_as"))
                    tr = self.V(lambda e, H, buf=buf, rh=rh, cw=cw: e.tensor_reduce(
                        H["sml"][0:rh, 8:9], H[buf][0:rh, 0:cw], axis=AX.X, op=ALU.add,
                        apply_absolute_value=True), waits=[tl, last_ta])
                    ta = self.V(lambda e, H, j=j, rh=rh: e.tensor_tensor(
                        H["sml"][0:rh, j:j + 1], H["sml"][0:rh, j:j + 1],
                        H["sml"][0:rh, 8:9], ALU.add), waits=[tr])
                    vdone[k % 2] = ta
                    last_ta = ta
                    k += 1
        tsum = b.barrier_token("v_m")
        tp = self.G(lambda e, H: e.reduce_sum(
            H["sml"][0:1, 16:24], H["sml"][:, 0:8], axis=AX.C),
            waits=[tsum])
        tg1 = self.GD(lambda e, H: e.dma_start(out=self.dram_ap("arin"),
                                               in_=H["sml"][0:1, 16:24]), waits=[tp])
        tar = self.CC(lambda e, H: e.collective_compute(
            "AllReduce", ALU.add, replica_groups=[list(range(c["nc"]))],
            ins=[self.dram_h("arin").ap()], outs=[self.dram_h("arout").ap()]),
            waits=[tg1])
        tg2 = self.GD(lambda e, H: e.dma_start(out=H["sml"][0:1, 24:32],
                                               in_=self.dram_ap("arout")), waits=[tar])
        t2s = []
        for j, (nm, snm, inm, qnm, rows, cols, nw) in enumerate(self.WMETA):
            t1 = self.V(lambda e, H, j=j, nw=nw: e.tensor_scalar(
                H["sml"][0:1, 32 + j:33 + j], H["sml"][0:1, 24 + j:25 + j],
                1.0 / nw, 1e-5, ALU.mult, ALU.max), waits=[tg2])
            t2 = self.V(lambda e, H, j=j: e.reciprocal(
                H["sml"][0:1, 40 + j:41 + j], H["sml"][0:1, 32 + j:33 + j]), waits=[t1])
            t2s.append(t2)
        _, tb2 = self.pbc("ps0", 16, lambda H: H["sml"][0:1, 32:48],
                          lambda H: H["sml"][:, 48:64], waits=t2s)
        self.tok["scales"] = tb2

        wfq_free = [None]
        prev = {"t2": None, "t3": None}
        def quant_matrix(j, nm, snm, inm, qnm, rows, cols):
            for r0 in range(0, rows, 128):
                rh = min(128, rows - r0)
                for c0 in range(0, cols, CW):
                    cw = min(CW, cols - c0)
                    buf = "wld" if (self._qk % 2 == 0) else "wld2"
                    tl = self.D(lambda e, H, buf=buf, snm=snm, r0=r0, rh=rh, c0=c0, cw=cw:
                                e.dma_start(out=H[buf][0:rh, 0:cw],
                                            in_=self.dram_ap(snm)[r0:r0 + rh, c0:c0 + cw]),
                                waits=[self._qv[self._qk % 2]],
                                ch=("d_a" if self._qk % 2 == 0 else "d_as"))
                    t1 = self.V(lambda e, H, buf=buf, j=j, rh=rh, cw=cw: e.tensor_scalar(
                        H["wi32"][0:rh, 0:cw], H[buf][0:rh, 0:cw],
                        H["sml"][0:rh, 56 + j:57 + j], RND, ALU.mult, ALU.add),
                        waits=[tl, self.tok["scales"], prev["t2"]])
                    self._qv[self._qk % 2] = t1
                    t2 = self.V(lambda e, H, rh=rh, cw=cw: e.tensor_scalar(
                        H["wsp"][0:rh, 0:cw], H["wi32"][0:rh, 0:cw],
                        NMAG, 1.0, ALU.add, ALU.min), waits=[t1, prev["t3"]])
                    prev["t2"] = t2
                    t3 = self.V(lambda e, H, rh=rh, cw=cw: e.tensor_scalar(
                        H["wfq"][0:rh, 0:cw], H["wsp"][0:rh, 0:cw],
                        -1.0, None, ALU.max, ALU.bypass), waits=[t2, wfq_free[0]])
                    prev["t3"] = t3
                    ts = self.D(lambda e, H, inm=inm, r0=r0, rh=rh, c0=c0, cw=cw:
                                e.dma_start(out=self.dram_ap(inm)[r0:r0 + rh, c0:c0 + cw],
                                            in_=H["wfq"][0:rh, 0:cw]), waits=[t3], ch="d_s")
                    wfq_free[0] = ts
                    self._qk += 1
            tag = self.CC(lambda e, H, inm=inm, qnm=qnm: e.collective_compute(
                "AllGather", ALU.bypass, replica_groups=[list(range(c["nc"]))],
                ins=[self.dram_h(inm).ap()], outs=[self.dram_h(qnm).ap()]),
                waits=[wfq_free[0]])
            self.tok[qnm] = tag

        def cast_matrix(snm, inm, qnm, rows, cols):
            for r0 in range(0, rows, 128):
                rh = min(128, rows - r0)
                for c0 in range(0, cols, CW):
                    cw = min(CW, cols - c0)
                    buf = "wld" if (self._qk % 2 == 0) else "wld2"
                    tl = self.D(lambda e, H, buf=buf, snm=snm, r0=r0, rh=rh, c0=c0, cw=cw:
                                e.dma_start(out=H[buf][0:rh, 0:cw],
                                            in_=self.dram_ap(snm)[r0:r0 + rh, c0:c0 + cw]),
                                waits=[self._qv[self._qk % 2]],
                                ch=("d_a" if self._qk % 2 == 0 else "d_as"))
                    t3 = self.V(lambda e, H, buf=buf, rh=rh, cw=cw: e.tensor_copy(
                        H["wfq"][0:rh, 0:cw], H[buf][0:rh, 0:cw]), waits=[tl, wfq_free[0]])
                    self._qv[self._qk % 2] = t3
                    ts = self.D(lambda e, H, inm=inm, r0=r0, rh=rh, c0=c0, cw=cw:
                                e.dma_start(out=self.dram_ap(inm)[r0:r0 + rh, c0:c0 + cw],
                                            in_=H["wfq"][0:rh, 0:cw]), waits=[t3], ch="d_s")
                    wfq_free[0] = ts
                    self._qk += 1
            tag = self.CC(lambda e, H, inm=inm, qnm=qnm: e.collective_compute(
                "AllGather", ALU.bypass, replica_groups=[list(range(c["nc"]))],
                ins=[self.dram_h(inm).ap()], outs=[self.dram_h(qnm).ap()]),
                waits=[wfq_free[0]])
            self.tok[qnm] = tag

        self._qk = 0
        self._qv = [vdone[0], vdone[1]]
        for j in (0, 1):
            nm, snm, inm, qnm, rows, cols, nw = self.WMETA[j]
            quant_matrix(j, nm, snm, inm, qnm, rows, cols)
        cast_matrix("l0att_s", "i_l0att", "q_l0att", d // 8, 4 * d)
        for j in (2, 3, 4):
            nm, snm, inm, qnm, rows, cols, nw = self.WMETA[j]
            quant_matrix(j, nm, snm, inm, qnm, rows, cols)
        cast_matrix("l1zx_s", "i_l1zx", "q_l1zx", d // 8, c["inter"] + c["convdim"])
        cast_matrix("l1wo_s", "i_l1wo", "q_l1wo", c["inter"] // 8, d)
        self.tok["wfq_free"] = wfq_free[0]
        self.tok["uf_free"] = [b.barrier_token(s) for s in ("v_m", "s_m", "g_m", "t_m")] + \
                              ([wfq_free[0]] if wfq_free[0] else [])

    def dram_h(self, name):
        return self.b._dram[name]

    def dram_ap(self, name):
        return self.b._dram[name].ap()

    def ln_quant(self, t0, quant, wj=None, res_tok=None):
        b, c = self.b, self.c
        d, hc, dch, T = c["d"], c["hc"], c["dch"], c["T"]
        W = hc
        stats_mm = []
        pv = {"cast": [None, None], "sq": [None, None], "mx": None, "mn": None,
              "m1": [self.tok.get("stg_free"), self.tok.get("stg_free")],
              "m2": [self.tok.get("stg_free"), self.tok.get("stg_free")]}
        for ch in range(dch):
            sl = 0
            o0, o1 = 0, W
            rc = (slice(None), slice(ch * T + t0, ch * T + t0 + W))
            tcast = self.V(lambda e, H, rc=rc, o0=o0: e.tensor_copy(
                H["stg"][:, o0:o0 + W], H["res"][rc]),
                waits=[res_tok, pv["sq"][sl], self.tok["consts"], pv["m1"][sl],
                       self.tok.get("stg_free2"), self.tok.get("stg_free3")])
            tsq = self.V(lambda e, H, o0=o0, o1=o1: e.tensor_tensor(
                H["stg"][:, o1:o1 + W], H["stg"][:, o0:o0 + W],
                H["stg"][:, o0:o0 + W], ALU.mult), waits=[tcast, pv["m2"][sl]])
            pv["sq"][sl] = tsq
            m1 = self.T(lambda e, H, ch=ch, o0=o0: e.matmul(
                H["ps5"][0:1, 0:W], H["ones_s"][:, 0:1], H["stg"][:, o0:o0 + W],
                start=(ch == 0), stop=(ch == dch - 1)),
                waits=[tcast, self.tok["consts"]] + ([self.bankfree.get("ps5")] if ch == 0 else []))
            m2 = self.T(lambda e, H, ch=ch, o1=o1: e.matmul(
                H["ps4"][0:1, 0:W], H["ones_s"][:, 0:1], H["stg"][:, o1:o1 + W],
                start=(ch == 0), stop=(ch == dch - 1)),
                waits=[tsq] + ([self.bankfree.get("ps4")] if ch == 0 else []))
            stats_mm += [m1, m2]
            pv["m1"][sl] = m1
            pv["m2"][sl] = m2
            if quant:
                if ch == 0:
                    tmx = self.V(lambda e, H, rc=rc: e.tensor_copy(H["chmx"][:, 0:W], H["res"][rc]),
                                 waits=[res_tok, self.tok.get("chmx_free"),
                                        self.tok.get("uf_free")])
                    tmn = self.V(lambda e, H, rc=rc: e.tensor_copy(H["chmn"][:, 0:W], H["res"][rc]),
                                 waits=[res_tok, self.tok.get("chmx_free"),
                                        self.tok.get("uf_free")])
                else:
                    tmx = self.V(lambda e, H, rc=rc: e.tensor_tensor(
                        H["chmx"][:, 0:W], H["chmx"][:, 0:W], H["res"][rc], ALU.max),
                        waits=[pv["mx"]])
                    tmn = self.V(lambda e, H, rc=rc: e.tensor_tensor(
                        H["chmn"][:, 0:W], H["chmn"][:, 0:W], H["res"][rc], ALU.min),
                        waits=[pv["mn"]])
                pv["mx"], pv["mn"] = tmx, tmn
        tS1, tS2 = stats_mm[-2], stats_mm[-1]
        self.tok["stg_free"] = self.b.barrier_token("t_m")
        t_mean = self.V(lambda e, H: e.tensor_scalar(H["rb"][0:1, 0 * W:0 * W + W], H["ps5"][0:1, 0:W],
                                                     1.0 / d, None, ALU.mult, ALU.bypass),
                        waits=[tS1])
        t_ex2 = self.V(lambda e, H: e.tensor_scalar(H["rb"][0:1, 1 * W:1 * W + W], H["ps4"][0:1, 0:W],
                                                    1.0 / d, None, ALU.mult, ALU.bypass),
                       waits=[tS2])
        self.bankfree["ps5"] = t_mean
        self.bankfree["ps4"] = t_ex2
        t_m2 = self.V(lambda e, H: e.tensor_tensor(H["rb"][0:1, 2 * W:2 * W + W], H["rb"][0:1, 0 * W:0 * W + W],
                                                   H["rb"][0:1, 0 * W:0 * W + W], ALU.mult), waits=[t_mean])
        t_var = self.V(lambda e, H: e.tensor_tensor(H["rb"][0:1, 2 * W:2 * W + W], H["rb"][0:1, 1 * W:1 * W + W],
                                                    H["rb"][0:1, 2 * W:2 * W + W], ALU.subtract),
                       waits=[t_ex2, t_m2])
        t_ln = self.S(lambda e, H: e.activation(H["rb"][0:1, 1 * W:1 * W + W], H["rb"][0:1, 2 * W:2 * W + W], AF.Ln,
                                                bias=H["cc_s"][0:1, 0:1], scale=1.0),
                      waits=[t_var])
        t_rstd = self.S(lambda e, H: e.activation(H["rb"][0:1, 1 * W:1 * W + W], H["rb"][0:1, 1 * W:1 * W + W], AF.Exp,
                                                  scale=-0.5), waits=[t_ln])
        _, t_bm = self.pbc("ps5", W, lambda H: H["rb"][0:1, 0 * W:0 * W + W],
                           lambda H: H["bc"][:, 0:W], waits=[t_mean])
        _, t_brs = self.pbc("ps4", W, lambda H: H["rb"][0:1, 1 * W:1 * W + W],
                            lambda H: H["bc"][:, W:2 * W], waits=[t_rstd])
        B0 = (slice(None), slice(0, W))
        B1 = (slice(None), slice(W, 2 * W))
        B2 = (slice(None), slice(2 * W, 3 * W))
        B3 = (slice(None), slice(3 * W, 4 * W))
        B4 = (slice(None), slice(4 * W, 5 * W))
        B5 = None
        if quant:
            t1 = self.V(lambda e, H: e.tensor_tensor(H["chmx"][:, 0:W], H["chmx"][:, 0:W],
                                                     H["bc"][B0], ALU.subtract),
                        waits=[pv["mx"], t_bm])
            t2 = self.V(lambda e, H: e.tensor_tensor(H["chmn"][:, 0:W], H["bc"][B0],
                                                     H["chmn"][:, 0:W], ALU.subtract),
                        waits=[pv["mn"], t_bm])
            t3 = self.V(lambda e, H: e.tensor_tensor(H["chmx"][:, 0:W], H["chmx"][:, 0:W],
                                                     H["chmn"][:, 0:W], ALU.max), waits=[t1, t2])
            t4 = self.V(lambda e, H: e.tensor_tensor(H["chmx"][:, 0:W], H["chmx"][:, 0:W],
                                                     H["bc"][B1], ALU.mult), waits=[t3, t_brs])
            t5 = self.V(lambda e, H: e.tensor_scalar(H["chmx"][:, 0:W], H["chmx"][:, 0:W],
                                                     1e-5, None, ALU.max, ALU.bypass), waits=[t4])
            t6a = self.G(lambda e, H: e.reduce_max(
                H["chmn"][0:1, 0:W], H["chmx"][:, 0:W], axis=AX.C),
                waits=[t5, t3])
            self.tok["chmx_free"] = t6a
            _, t6 = self.pbc("ps5", W, lambda H: H["chmn"][0:1, 0:W],
                             lambda H: H["chmn"][:, 0:W], waits=[t6a])
            t7 = self.V(lambda e, H: e.tensor_scalar(H["bc"][B4], H["chmn"][:, 0:W], 1.0 / 127.0,
                                                     None, ALU.mult, ALU.bypass), waits=[t6])
            t8 = self.V(lambda e, H: e.reciprocal(H["chmn"][:, 0:W], H["chmn"][:, 0:W]), waits=[t7, t6])
            t9 = self.V(lambda e, H: e.tensor_scalar(H["bc"][B2], H["chmn"][:, 0:W], 127.0, None,
                                                     ALU.mult, ALU.bypass), waits=[t8])
            t10 = self.V(lambda e, H: e.tensor_tensor(H["bc"][B2], H["bc"][B2], H["bc"][B1],
                                                      ALU.mult), waits=[t9, t_brs])
            t11 = self.V(lambda e, H: e.tensor_tensor(H["bc"][B3], H["bc"][B0], H["bc"][B2],
                                                      ALU.mult), waits=[t10, t_bm])
            prevq = {"p1": None, "p2": None, "p3": None}
            self.tok["ln_act_toks"] = []
            for ch in range(dch):
                rc = (slice(None), slice(ch * T + t0, ch * T + t0 + W))
                ac = (slice(None), slice(ch * hc, (ch + 1) * hc))
                p1 = self.V(lambda e, H, rc=rc: e.scalar_tensor_tensor(
                    H["wsp"][:, 0:W], H["res"][rc], 1.0, H["bc"][B2], ALU.mult, ALU.mult),
                    waits=[t10, prevq["p2"], res_tok])
                p2 = self.V(lambda e, H: e.scalar_tensor_tensor(
                    H["wi32"][:, 0:W], H["wsp"][:, 0:W], RND, H["bc"][B3],
                    ALU.add, ALU.subtract), waits=[p1, t11, prevq["p3"]])
                prevq["p2"] = p2
                p3 = self.V(lambda e, H, ac=ac: e.tensor_scalar(
                    H["act"][ac], H["wi32"][:, 0:W], NMAG, None, ALU.add, ALU.bypass),
                    waits=[p2, self.tok.get("act_free")])
                prevq["p3"] = p3
                self.tok["ln_act_toks"].append(p3)
            return b.barrier_token("v_m")
        else:
            prevq = {"p1": None, "p2": None}
            self.tok["ln_act_toks"] = []
            for ch in range(dch):
                rc = (slice(None), slice(ch * T + t0, ch * T + t0 + W))
                ac = (slice(None), slice(ch * hc, (ch + 1) * hc))
                p1 = self.V(lambda e, H, rc=rc: e.scalar_tensor_tensor(
                    H["wsp"][:, 0:W], H["res"][rc], 1.0, H["bc"][B0], ALU.mult, ALU.subtract),
                    waits=[t_bm, prevq["p2"], res_tok])
                p2 = self.V(lambda e, H, ac=ac: e.tensor_tensor(
                    H["act"][ac], H["wsp"][:, 0:W], H["bc"][B1], ALU.mult),
                    waits=[p1, t_brs, self.tok.get("act_free")])
                prevq["p2"] = p2
                self.tok["ln_act_toks"].append(p2)
            return b.barrier_token("v_m")

    def ffn(self, layer):
        b, c = self.b, self.c
        d, hc, dch, T = c["d"], c["hc"], c["dch"], c["T"]
        if layer == 0:
            dff, tw = c["dff0"], hc
            qw1, qw2, qw3 = "q_l0w1", "q_l0w2", None
            j1, j2, j3 = 0, 1, None
        else:
            dff, tw = c["dff1"], hc // 2
            qw1, qw2, qw3 = "q_l1w1", "q_l1w2", "q_l1w3"
            j1, j2, j3 = 2, 4, 3
        fch = dff // 128
        A, Bk = ["ps0", "ps1", "ps2"], ["ps3", "ps4", "ps5"]
        wrot = ["wa", "wb"]
        wfree = self.wfree
        wsl = {}
        hc_ = hc

        def wsp_slot(i):
            return (slice(None), slice(i * 512, i * 512 + tw))

        for tt in range(T // hc):
            res_tok = self.tok.get("res")
            lt = self.ln_quant(tt * hc, True, wj=j1, res_tok=res_tok)
            lntoks = list(self.tok["ln_act_toks"])
            for p in range(hc // tw):
                hq_off = 0
                h3_off = fch * tw
                chain = {"mx": None, "mn": None}

                def h_sweep(qname, wj, mode):
                    for gi, fc0 in enumerate(range(0, fch, 3)):
                        gsz = min(3, fch - fc0)
                        banks = A if gi % 2 == 0 else Bk
                        mlast = {}
                        for ich in range(0, dch, 4):
                            ics = min(4, dch - ich)
                            wbuf = wrot[self._dk % 2]
                            wch = "d_a" if self._dk % 2 == 0 else "d_b"
                            self._dk += 1
                            tl = self.D(lambda e, H, wbuf=wbuf, qname=qname, ich=ich, ics=ics,
                                        fc0=fc0, gsz=gsz: e.dma_start(
                                out=H[wbuf][:, 0:ics * gsz * 128].rearrange(
                                    "p (i f) -> p i f", i=ics),
                                in_=self.dram_ap(qname).rearrange(
                                    "(i p) f -> p i f", p=128)[:, ich:ich + ics,
                                    fc0 * 128:(fc0 + gsz) * 128]),
                                waits=[wfree[wbuf], self.tok[qname]], ch=wch)
                            for ii in range(ics):
                                ic = ich + ii
                                for fi in range(gsz):
                                    bk = banks[fi]
                                    wt = [tl, lntoks[ic]]
                                    if ic == 0:
                                        wt.append(self.bankfree.get(bk))
                                    mm = self.T(lambda e, H, bk=bk, wbuf=wbuf, ii=ii, fi=fi,
                                                gsz=gsz, ic=ic, p=p: e.matmul(
                                        H[bk][:, 0:tw],
                                        H[wbuf][:, (ii * gsz + fi) * 128:(ii * gsz + fi + 1) * 128],
                                        H["act"][:, ic * hc_ + p * tw: ic * hc_ + p * tw + tw],
                                        start=(ic == 0), stop=(ic == dch - 1)),
                                        waits=wt)
                                    mlast[fi] = mm
                            wfree[wbuf] = mlast[gsz - 1]
                        for fi in range(gsz):
                            fc = fc0 + fi
                            bk = banks[fi]
                            sl = fi % 2
                            if mode == "h3":
                                ev = self.V(lambda e, H, bk=bk, fc=fc, wj=wj, p=p: e.scalar_tensor_tensor(
                                    H["big"][:, h3_off + fc * tw: h3_off + (fc + 1) * tw],
                                    H[bk][:, 0:tw], H["sml"][:, 48 + wj:49 + wj],
                                    H["bc"][:, 4 * hc_ + p * tw: 4 * hc_ + p * tw + tw],
                                    ALU.mult, ALU.mult),
                                    waits=[mlast[fi], lt, self.tok["scales"],
                                           self.tok.get("big_free")])
                                self.bankfree[bk] = ev
                            else:
                                e1 = self.V(lambda e, H, bk=bk, sl=sl, wj=wj, p=p: e.scalar_tensor_tensor(
                                    H["wsp"][wsp_slot(sl)], H[bk][:, 0:tw],
                                    H["sml"][:, 48 + wj:49 + wj],
                                    H["bc"][:, 4 * hc_ + p * tw: 4 * hc_ + p * tw + tw],
                                    ALU.mult, ALU.mult),
                                    waits=[mlast[fi], lt, self.tok["scales"], wsl.get(("s", sl))])
                                self.bankfree[bk] = e1
                                e2 = self.S(lambda e, H, sl=sl: e.activation(
                                    H["wsp"][wsp_slot(2 + sl)], H["wsp"][wsp_slot(sl)], AF.Silu),
                                    waits=[e1, wsl.get(("u", sl))])
                                wsl[("s", sl)] = e2
                                if mode == "l0":
                                    e3 = self.V(lambda e, H, sl=sl, fc=fc: e.tensor_copy(
                                        H["big"][:, hq_off + fc * tw: hq_off + (fc + 1) * tw],
                                        H["wsp"][wsp_slot(2 + sl)]),
                                        waits=[e2, self.tok.get("big_free")])
                                else:
                                    e3 = self.V(lambda e, H, sl=sl, fc=fc: e.tensor_tensor(
                                        H["big"][:, hq_off + fc * tw: hq_off + (fc + 1) * tw],
                                        H["wsp"][wsp_slot(2 + sl)],
                                        H["big"][:, h3_off + fc * tw: h3_off + (fc + 1) * tw],
                                        ALU.mult), waits=[e2])
                                wsl[("u", sl)] = e3
                                if chain["mx"] is None:
                                    cmx = self.V(lambda e, H, fc=fc: e.tensor_copy(
                                        H["chmx"][:, 0:tw],
                                        H["big"][:, hq_off + fc * tw: hq_off + (fc + 1) * tw]),
                                        waits=[e3, self.tok.get("chmx_free")])
                                    cmn = self.V(lambda e, H, fc=fc: e.tensor_copy(
                                        H["chmn"][:, 0:tw],
                                        H["big"][:, hq_off + fc * tw: hq_off + (fc + 1) * tw]),
                                        waits=[e3, self.tok.get("chmx_free")])
                                else:
                                    cmx = self.V(lambda e, H, fc=fc: e.tensor_tensor(
                                        H["chmx"][:, 0:tw], H["chmx"][:, 0:tw],
                                        H["big"][:, hq_off + fc * tw: hq_off + (fc + 1) * tw],
                                        ALU.max), waits=[e3, chain["mx"]])
                                    cmn = self.V(lambda e, H, fc=fc: e.tensor_tensor(
                                        H["chmn"][:, 0:tw], H["chmn"][:, 0:tw],
                                        H["big"][:, hq_off + fc * tw: hq_off + (fc + 1) * tw],
                                        ALU.min), waits=[e3, chain["mn"]])
                                chain["mx"], chain["mn"] = cmx, cmn

                if layer == 1:
                    h_sweep(qw3, j3, "h3")
                    h_sweep(qw1, j1, "h1")
                else:
                    h_sweep(qw1, j1, "l0")

                if os.environ.get("KDBG_F1") and layer == 1 and tt == 0:
                    hbar = self.b.barrier_token("v_m")
                    self.D(lambda e, H, p=p: e.dma_start(
                        out=self.dram_ap("dbg_h")[:, p * 12288:p * 12288 + fch * tw],
                        in_=H["big"][:, hq_off:hq_off + fch * tw]),
                        waits=[hbar], ch="d_y")
                BH_IS = (slice(None), slice(3 * hc_, 3 * hc_ + tw))
                self.tok["act_free"] = self.b.barrier_token("t_m")
                q2 = self.V(lambda e, H: e.tensor_scalar(
                    H["chmn"][:, 0:tw], H["chmn"][:, 0:tw], -1.0, None, ALU.mult, ALU.bypass),
                    waits=[chain["mn"]])
                q3 = self.V(lambda e, H: e.tensor_tensor(
                    H["chmx"][:, 0:tw], H["chmx"][:, 0:tw], H["chmn"][:, 0:tw], ALU.max),
                    waits=[q2, chain["mx"]])
                q4 = self.V(lambda e, H: e.tensor_scalar(
                    H["chmx"][:, 0:tw], H["chmx"][:, 0:tw], 1e-5, None, ALU.max, ALU.bypass),
                    waits=[q3])
                q5a = self.G(lambda e, H: e.reduce_max(
                    H["chmn"][0:1, 0:tw], H["chmx"][:, 0:tw], axis=AX.C),
                    waits=[q4, q3])
                self.tok["chmx_free"] = q5a
                _, q5 = self.pbc("ps0", tw, lambda H: H["chmn"][0:1, 0:tw],
                                 lambda H: H["chmn"][:, 0:tw], waits=[q5a])
                q6 = self.V(lambda e, H: e.tensor_scalar(
                    H["bc"][BH_IS], H["chmn"][:, 0:tw], 0.5 / 127.0, None, ALU.mult, ALU.bypass),
                    waits=[q5])
                q7 = self.V(lambda e, H: e.reciprocal(H["chmn"][:, 0:tw], H["chmn"][:, 0:tw]),
                            waits=[q6, q5])
                q8 = self.V(lambda e, H: e.tensor_scalar(
                    H["chmn"][:, 0:tw], H["chmn"][:, 0:tw], 127.0, None, ALU.mult, ALU.bypass),
                    waits=[q7])
                pq = {"p1": None, "p2": None, "p3": None}
                hq_toks = []
                for fc in range(fch):
                    p1 = self.V(lambda e, H, fc=fc: e.scalar_tensor_tensor(
                        H["wsp"][wsp_slot(0)],
                        H["big"][:, hq_off + fc * tw: hq_off + (fc + 1) * tw], 1.0,
                        H["chmn"][:, 0:tw], ALU.mult, ALU.mult),
                        waits=[q8, pq["p2"]])
                    p2 = self.V(lambda e, H: e.tensor_scalar(
                        H["wi32"][:, 0:tw], H["wsp"][wsp_slot(0)], RND, None,
                        ALU.add, ALU.bypass), waits=[p1, pq["p3"]])
                    pq["p2"] = p2
                    p3 = self.V(lambda e, H, fc=fc: e.tensor_scalar(
                        H["big"][:, hq_off + fc * tw: hq_off + (fc + 1) * tw],
                        H["wi32"][:, 0:tw], NMAG, None, ALU.add, ALU.bypass), waits=[p2])
                    pq["p3"] = p3
                    hq_toks.append(p3)
                hq_tok = self.b.barrier_token("v_m")
                if os.environ.get("KDBG_F1") and layer == 1 and tt == 0:
                    self.D(lambda e, H, p=p: e.dma_start(
                        out=self.dram_ap("dbg_q")[:, p * 12288:p * 12288 + fch * tw],
                        in_=H["big"][:, hq_off:hq_off + fch * tw]),
                        waits=[hq_tok], ch="d_y")
                    self.D(lambda e, H, p=p: e.dma_start(
                        out=self.dram_ap("dbg_s")[:, p * 256:p * 256 + tw],
                        in_=H["bc"][BH_IS]), waits=[hq_tok, q6], ch="d_y")

                for gi, oc0 in enumerate(range(0, dch, 3)):
                    gsz = min(3, dch - oc0)
                    banks = A if gi % 2 == 0 else Bk
                    mlast = {}
                    for fcb in range(0, fch, 4):
                        fcs = min(4, fch - fcb)
                        wbuf = wrot[self._dk % 2]
                        wch = "d_a" if self._dk % 2 == 0 else "d_b"
                        self._dk += 1
                        tl = self.D(lambda e, H, wbuf=wbuf, fcb=fcb, fcs=fcs, oc0=oc0,
                                    gsz=gsz: e.dma_start(
                            out=H[wbuf][:, 0:fcs * gsz * 128].rearrange(
                                "p (i f) -> p i f", i=fcs),
                            in_=self.dram_ap(qw2).rearrange(
                                "(i p) f -> p i f", p=128)[:, fcb:fcb + fcs,
                                oc0 * 128:(oc0 + gsz) * 128]),
                            waits=[wfree[wbuf], self.tok[qw2]], ch=wch)
                        for ii in range(fcs):
                            fc = fcb + ii
                            for oi in range(gsz):
                                bk = banks[oi]
                                wt = [tl, hq_toks[fc]]
                                if fc == 0:
                                    wt.append(self.bankfree.get(bk))
                                mm = self.T(lambda e, H, bk=bk, wbuf=wbuf, ii=ii, oi=oi,
                                            gsz=gsz, fc=fc: e.matmul(
                                    H[bk][:, 0:tw],
                                    H[wbuf][:, (ii * gsz + oi) * 128:(ii * gsz + oi + 1) * 128],
                                    H["big"][:, hq_off + fc * tw: hq_off + (fc + 1) * tw],
                                    start=(fc == 0), stop=(fc == fch - 1)),
                                    waits=wt)
                                mlast[oi] = mm
                        wfree[wbuf] = mlast[gsz - 1]
                    for oi in range(gsz):
                        oc = oc0 + oi
                        bk = banks[oi]
                        sl = 2 + (oc % 2)
                        e1 = self.V(lambda e, H, bk=bk, sl=sl: e.scalar_tensor_tensor(
                            H["wsp"][wsp_slot(sl)], H[bk][:, 0:tw],
                            H["sml"][:, 48 + j2:49 + j2], H["bc"][BH_IS], ALU.mult, ALU.mult),
                            waits=[mlast[oi], self.tok["scales"], wsl.get(("o", sl))])
                        self.bankfree[bk] = e1
                        rc = (slice(None), slice(oc * T + tt * hc + p * tw,
                                                 oc * T + tt * hc + p * tw + tw))
                        e2 = self.V(lambda e, H, rc=rc, sl=sl: e.tensor_tensor(
                            H["res"][rc], H["res"][rc], H["wsp"][wsp_slot(sl)], ALU.add),
                            waits=[e1, lt])
                        wsl[("o", sl)] = e2
                self.tok["big_free"] = self.b.barrier_token("t_m")
                self.tok["uf_free"] = [self.b.barrier_token(s) for s in ("v_m", "s_m", "g_m", "t_m")]
            self.tok["res"] = self.b.barrier_token("v_m")

    def epilogue(self):
        c, T, dch = self.c, self.c["T"], self.c["dch"]
        rt = self.tok.get("res")
        for ch in range(dch):
            self.D(lambda e, H, ch=ch: e.dma_start(
                out=self.dram_ap("ys")[ch * 128:(ch + 1) * 128, :],
                in_=H["res"][:, ch * T:(ch + 1) * T]), waits=[rt], ch="d_y")
        fin = self.b.barrier_token("d_y")
        self.b.op("sync", lambda e, H: None, waits=[fin], inc=None)


FULL_STEPS = ("f0", "at", "f0", "f1", "ss", "f1")

def build_program(cfg, steps=FULL_STEPS):
    b = Builder(cfg)
    p = Prog(b)
    p.declare()
    p.prologue()
    for s in steps:
        if s == "f0":
            p.ffn(0)
        elif s == "f1":
            p.ffn(1)
        elif s == "at":
            p.attn()
        elif s == "ss":
            p.ssm()
    p.epilogue()
    nc = b.finalize()
    return nc, p


def _bf(x):
    return np.ascontiguousarray(x).astype(ml_dtypes.bfloat16)


def prep_inputs(cfg, inp):
    c = derived(cfg)
    d, hc, T, L = c["d"], c["hc"], c["T"], c["L"]
    ncores = c["nc"]
    K = c["K"]
    cvch = c["convdim"] // 128
    ich = c["inter"] // 128
    f32 = np.float32

    x = np.asarray(inp["L0_attn_x"], f32)
    w1t = np.ascontiguousarray(np.asarray(inp["L0_ffn_W1"], f32).T)
    w2t = np.ascontiguousarray(np.asarray(inp["L0_ffn_W2"], f32).T)
    att = np.ascontiguousarray(np.concatenate(
        [np.asarray(inp["L0_attn_w_q"], f32), np.asarray(inp["L0_attn_w_k"], f32),
         np.asarray(inp["L0_attn_w_v"], f32), np.asarray(inp["L0_attn_w_o"], f32)], axis=1))
    l1w1t = np.ascontiguousarray(np.asarray(inp["L1_ffn_w1"], f32).T)
    l1w3t = np.ascontiguousarray(np.asarray(inp["L1_ffn_w3"], f32).T)
    l1w2t = np.ascontiguousarray(np.asarray(inp["L1_ffn_w2"], f32).T)
    zx = np.ascontiguousarray(np.concatenate(
        [np.asarray(inp["L1_attn_W_z"], f32), np.asarray(inp["L1_attn_W_xBC"], f32)], axis=1))
    wout = np.ascontiguousarray(np.asarray(inp["L1_attn_W_out"], f32))
    wdt = np.ascontiguousarray(np.asarray(inp["L1_attn_W_dt"], f32))
    convw = np.ascontiguousarray(
        np.asarray(inp["L1_attn_conv_w"], f32).reshape(cvch, 128, K).transpose(1, 0, 2)
        .reshape(128, cvch * K))
    convb = np.ascontiguousarray(np.asarray(inp["L1_attn_conv_b"], f32).reshape(cvch, 128).T)
    smallv = np.stack([np.asarray(inp["L1_attn_dt_bias"], f32),
                       np.asarray(inp["L1_attn_A_log"], f32),
                       np.asarray(inp["L1_attn_D"], f32),
                       np.zeros(c["hssm"], f32)], axis=1)
    normw = np.ascontiguousarray(np.asarray(inp["L1_attn_norm_w"], f32).reshape(ich, 128).T)
    ident = np.eye(128)
    ssdm = np.where(np.arange(128)[:, None] <= np.arange(128)[None, :], 0.0, NEG).astype(f32)
    onesb = np.zeros((128, 8), f32); onesb[:, 0] = 1.0; onesb[:, 1] = 1.0
    cconst = np.zeros((128, 8), f32)
    cconst[:, 0] = c["eps"]; cconst[:, 1] = 1.0; cconst[:, 2] = 10.0
    cconst[:, 3] = np.arange(128); cconst[:, 4] = 1e-19
    cconst[:, 5] = -0.5 * np.log(128.0)
    qidxb = np.broadcast_to(np.arange(hc, dtype=f32)[None, :], (128, hc)).copy()

    dcolx = np.ascontiguousarray(
        np.repeat(np.asarray(inp["L1_attn_D"], f32), c["P"]).reshape(ich, 128).T)
    onest = np.ones((c["hssm"], 128), f32)
    s8 = d // 8
    in_maps = []
    for i in range(ncores):
        j = ncores - 1 - i
        smkr = np.zeros((1, 16), f32)
        smkr[0, 0:i] = 1.0
        smkr[0, 8:8 + j] = 1.0
        bselr = np.zeros((128, 16), f32)
        if i > 0:
            bselr[:, i - 1] = 1.0
        if j > 0:
            bselr[:, 8 + (ncores - 1 - (j - 1))] = 1.0

        xs = np.ascontiguousarray(
            np.concatenate([x[0, i * hc:(i + 1) * hc], x[1, j * hc:(j + 1) * hc]], 0).T)
        m = {
            "xs": xs,
            "l0w1t_s": w1t[i * s8:(i + 1) * s8],
            "l0w2t_s": w2t[i * (c["dff0"] // 8):(i + 1) * (c["dff0"] // 8)],
            "l0att_s": att[i * s8:(i + 1) * s8],
            "l1w1t_s": l1w1t[i * s8:(i + 1) * s8],
            "l1w3t_s": l1w3t[i * s8:(i + 1) * s8],
            "l1w2t_s": l1w2t[i * (c["dff1"] // 8):(i + 1) * (c["dff1"] // 8)],
            "l1zx_s": zx[i * s8:(i + 1) * s8],
            "l1wo_s": wout[i * (c["inter"] // 8):(i + 1) * (c["inter"] // 8)],
            "l1wdt": wdt,
            "convw": convw, "convb": convb, "smallv": smallv, "normw": normw,
            "ident": _bf(ident), "identf": ident.astype(f32),
            "ssdmask": ssdm, "onesb": _bf(onesb), "cconst": cconst,
            "qidxb": qidxb,
            "qposr": np.array([[i * hc, j * hc]], f32),
            "dcolx": dcolx, "onest": onest, "smk": smkr, "bsel": bselr,
        }
        m = {k: np.ascontiguousarray(v) for k, v in m.items()}
        in_maps.append(m)
    return in_maps


def assemble_output(cfg, results):
    c = derived(cfg)
    d, hc, L = c["d"], c["hc"], c["L"]
    ncores = c["nc"]
    out = np.empty((2, L, d), np.float32)
    for i in range(ncores):
        j = ncores - 1 - i
        y = np.asarray(results[i]["ys"]).T
        out[0, i * hc:(i + 1) * hc] = y[0:hc]
        out[1, j * hc:(j + 1) * hc] = y[hc:2 * hc]
    return out


def kernel(**inputs):
    cfg = real_cfg()
    nc, _ = build_program(cfg)
    in_maps = prep_inputs(cfg, inputs)
    res = run_bass_kernel_spmd(nc, in_maps, core_ids=list(range(8)))
    return assemble_output(cfg, res.results)


def _attn(self):
    b, c = self.b, self.c
    d, hc, dch, T = c["d"], c["hc"], c["dch"], c["T"]
    nh = c["heads0"]
    nkt = hc // 128
    wrot = ["wa", "wb"]
    wfree = self.wfree
    csc = 1.0 / np.sqrt(128.0)
    Q0 = 0
    KG = nh * hc
    VG = KG + dch * hc
    PP = VG + nkt * d
    M0 = PP + 2 * hc
    assert M0 + nkt * hc <= self.big_cols

    def qkv_proj(s, lt):
        t0 = s * hc
        for proj in (0, 1):
            for gi, oc0 in enumerate(range(0, dch, 2)):
                gsz = min(2, dch - oc0)
                banks = ["ps0", "ps1"] if gi % 2 == 0 else ["ps2", "ps3"]
                mlast = {}
                for ich in range(0, dch, 4):
                    ics = min(4, dch - ich)
                    wbuf = wrot[self._dk % 2]
                    wch = "d_a" if self._dk % 2 == 0 else "d_b"
                    self._dk += 1
                    tl = self.D(lambda e, H, wbuf=wbuf, ich=ich, ics=ics, oc0=oc0,
                                gsz=gsz, proj=proj: e.dma_start(
                        out=H[wbuf][:, 0:ics * gsz * 128].rearrange(
                            "p (i f) -> p i f", i=ics),
                        in_=self.dram_ap("q_l0att").rearrange(
                            "(i p) f -> p i f", p=128)[:, ich:ich + ics,
                            proj * d + oc0 * 128: proj * d + (oc0 + gsz) * 128]),
                        waits=[wfree[wbuf], self.tok["q_l0att"]], ch=wch)
                    for ii in range(ics):
                        ic = ich + ii
                        for oi in range(gsz):
                            bk = banks[oi]
                            wt = [tl, self.tok["qkv_ln_toks"][ic]]
                            if ic == 0:
                                wt.append(self.bankfree.get(bk))
                            mm = self.T(lambda e, H, bk=bk, wbuf=wbuf, ii=ii, oi=oi,
                                        gsz=gsz, ic=ic: e.matmul(
                                H[bk][:, 0:hc],
                                H[wbuf][:, (ii * gsz + oi) * 128:(ii * gsz + oi + 1) * 128],
                                H["act"][:, ic * hc:(ic + 1) * hc],
                                start=(ic == 0), stop=(ic == dch - 1)), waits=wt)
                            mlast[oi] = mm
                    wfree[wbuf] = mlast[gsz - 1]
                for oi in range(gsz):
                    h = oc0 + oi
                    bk = banks[oi]
                    if proj == 0:
                        if s == 0:
                            ev = self.V(lambda e, H, bk=bk, h=h: e.tensor_copy(
                                H["big"][:, Q0 + h * hc: Q0 + (h + 1) * hc],
                                H[bk][:, 0:hc]),
                                waits=[mlast[oi], self.tok.get("big_free")])
                        else:
                            ev = self.V(lambda e, H, bk=bk: e.tensor_copy(
                                H["stg"][:, 0:hc], H[bk][:, 0:hc]),
                                waits=[mlast[oi], self.tok.get("stgK_dma")])
                            td = self.D(lambda e, H, h=h: e.dma_start(
                                out=self.dram_ap("qtmp")[h * 128:(h + 1) * 128, :],
                                in_=H["stg"][:, 0:hc]), waits=[ev], ch="d_q")
                            self.tok["stgK_dma"] = td
                    else:
                        ev = self.V(lambda e, H, bk=bk: e.tensor_copy(
                            H["stg"][:, 0:hc], H[bk][:, 0:hc]),
                            waits=[mlast[oi], self.tok.get("stgK_dma")])
                        td = self.D(lambda e, H, h=h, t0=t0: e.dma_start(
                            out=self.dram_ap("kin")[h * 128:(h + 1) * 128, t0:t0 + hc],
                            in_=H["stg"][:, 0:hc]), waits=[ev], ch="d_kv")
                        self.tok["stgK_dma"] = td
                    if proj == 0 and s == 0:
                        sqsrc = (lambda H, h=h: H["big"][:, Q0 + h * hc:
                                                         Q0 + (h + 1) * hc])
                    else:
                        sqsrc = (lambda H: H["stg"][:, 0:hc])
                    sq = self.V(lambda e, H, sqsrc=sqsrc: e.tensor_tensor(
                        H["sqb"][:, 0:hc], sqsrc(H), sqsrc(H), ALU.mult),
                        waits=[ev, self.tok.get("wsp0_free")])
                    self.bankfree[bk] = [sq, ev]
                    nm = self.T(lambda e, H: e.matmul(
                        H["ps4"][0:1, 0:hc], H["ones_s"][:, 0:1], H["sqb"][:, 0:hc],
                        start=True, stop=True),
                        waits=[sq, self.bankfree.get("ps4"), self.tok["consts"]])
                    col = (s * 16 + h) if proj == 0 else (32 + s * 16 + h)
                    rd = self.V(lambda e, H, col=col: e.reduce_max(
                        H["arow"][0:1, col:col + 1], H["ps4"][0:1, 0:hc], axis=AX.X),
                        waits=[nm])
                    self.tok["wsp0_free"] = rd
                    self.bankfree["ps4"] = rd
        for gi, oc0 in enumerate(range(0, dch, 2)):
            gsz = min(2, dch - oc0)
            for ts in range(nkt):
                bk = ["ps0", "ps1", "ps2", "ps3"][(gi * nkt + ts) % 4]
                mlast2 = None
                for ich in range(0, dch, 4):
                    ics = min(4, dch - ich)
                    wbuf = wrot[self._dk % 2]
                    wch = "d_a" if self._dk % 2 == 0 else "d_b"
                    self._dk += 1
                    tl = self.D(lambda e, H, wbuf=wbuf, ich=ich, ics=ics, oc0=oc0,
                                gsz=gsz: e.dma_start(
                        out=H[wbuf][:, 0:ics * gsz * 128].rearrange(
                            "p (i f) -> p i f", i=ics),
                        in_=self.dram_ap("q_l0att").rearrange(
                            "(i p) f -> p i f", p=128)[:, ich:ich + ics,
                            2 * d + oc0 * 128: 2 * d + (oc0 + gsz) * 128]),
                        waits=[wfree[wbuf], self.tok["q_l0att"]], ch=wch)
                    for ii in range(ics):
                        ic = ich + ii
                        wt = [tl, self.tok["qkv_ln_toks"][ic]]
                        if ic == 0:
                            wt.append(self.bankfree.get(bk))
                        mm = self.T(lambda e, H, bk=bk, wbuf=wbuf, ii=ii, gsz=gsz,
                                    ic=ic, ts=ts, ics=ics: e.matmul(
                            H[bk][:, 0:gsz * 128],
                            H["act"][:, ic * hc + ts * 128: ic * hc + (ts + 1) * 128],
                            H[wbuf][:, 0:ics * gsz * 128].rearrange(
                                "p (i f) -> p i f", i=ics)[:, ii, :],
                            start=(ic == 0), stop=(ic == dch - 1)), waits=wt)
                        mlast2 = mm
                    wfree[wbuf] = mlast2
                ev = self.V(lambda e, H, bk=bk, gsz=gsz: e.tensor_copy(
                    H["stg"][:, hc:hc + gsz * 128], H[bk][:, 0:gsz * 128]),
                    waits=[mlast2, self.tok.get("stgV_dma")])
                self.bankfree[bk] = ev
                td = self.D(lambda e, H, oc0=oc0, gsz=gsz, ts=ts, t0=t0: e.dma_start(
                    out=self.dram_ap("vin")[t0 + ts * 128: t0 + (ts + 1) * 128,
                                            oc0 * 128:(oc0 + gsz) * 128],
                    in_=H["stg"][:, hc:hc + gsz * 128]), waits=[ev], ch="d_kv2")
                self.tok["stgV_dma"] = td

    res_tok = self.tok.get("res")
    tqx = self.D(lambda e, H: e.dma_start(out=H["qidx_s"][:, 0:hc],
                                          in_=self.dram_ap("qidxb")),
                 waits=self.tok.get("uf_free") or [], ch="d_sm1")
    self.tok["kg_free"] = None
    for s in (0, 1):
        if s == 1:
            self.tok["stg_free"] = b.barrier_token("t_m")
            self.tok["stg_free2"] = self.tok.get("stgK_dma")
            self.tok["stg_free3"] = self.tok.get("stgV_dma")
            self.tok["act_free"] = b.barrier_token("t_m")
        if s == 1 and os.environ.get("KATT_CUT", "").startswith("0"):
            break
        lt = self.ln_quant(s * hc, False, res_tok=res_tok)
        self.tok["lnx" + str(s)] = lt
        self.tok["qkv_ln_toks"] = list(self.tok["ln_act_toks"])
        if os.environ.get("KATT_CUT") == "0a":
            continue
        qkv_proj(s, lt)
    if os.environ.get("KATT_CUT", "").startswith("0") \
            or os.environ.get("KATT_CUT") == "1":
        return
    kv_done = [b.barrier_token("d_kv"), b.barrier_token("d_kv2")]
    t1 = self.V(lambda e, H: e.tensor_tensor(
        H["arow"][0:1, 32:32 + nh], H["arow"][0:1, 32:32 + nh],
        H["arow"][0:1, 48:48 + nh], ALU.max),
        waits=[b.barrier_token("v_m")])
    tg = self.GD(lambda e, H: e.dma_start(out=self.dram_ap("knm_i")[0:1, 0:nh],
                                          in_=H["arow"][0:1, 32:32 + nh]), waits=[t1])
    tar = self.CC(lambda e, H: e.collective_compute(
        "AllReduce", ALU.max, replica_groups=[list(range(c["nc"]))],
        ins=[self.dram_h("knm_i").ap()], outs=[self.dram_h("knm_o").ap()]),
        waits=[tg])
    tg2 = self.GD(lambda e, H: e.dma_start(out=H["arow"][0:1, 32:32 + nh],
                                           in_=self.dram_ap("knm_o")[0:1, 0:nh]),
                  waits=[tar])
    tagk = self.CC(lambda e, H: e.collective_compute(
        "AllGather", ALU.bypass, replica_groups=[list(range(c["nc"]))],
        ins=[self.dram_h("kin").ap()], outs=[self.dram_h("kout").ap()]),
        waits=kv_done)
    tagkv = self.CC(lambda e, H: e.collective_compute(
        "AllGather", ALU.bypass, replica_groups=[list(range(c["nc"]))],
        ins=[self.dram_h("vin").ap()], outs=[self.dram_h("vout").ap()]),
        waits=kv_done)
    if os.environ.get("KATT_CUT") == "2":
        return
    lnk = self.S(lambda e, H: e.activation(H["arow"][0:1, 48:48 + nh],
                                           H["arow"][0:1, 32:32 + nh],
                                           AF.Ln, bias=H["cc_s"][0:1, 4:5]), waits=[tg2])
    for s in (0, 1):
        l1 = self.S(lambda e, H, s=s: e.activation(
            H["arow"][0:1, s * 16:s * 16 + nh], H["arow"][0:1, s * 16:s * 16 + nh],
            AF.Ln, bias=H["cc_s"][0:1, 4:5]), waits=[b.barrier_token("v_m"), lnk])
        l2 = self.V(lambda e, H, s=s: e.tensor_tensor(
            H["arow"][0:1, s * 16:s * 16 + nh], H["arow"][0:1, s * 16:s * 16 + nh],
            H["arow"][0:1, 48:48 + nh], ALU.add), waits=[l1])
        l3 = self.S(lambda e, H, s=s: e.activation(
            H["arow"][0:1, s * 16:s * 16 + nh], H["arow"][0:1, s * 16:s * 16 + nh],
            AF.Exp, scale=0.5, bias=H["cc_s"][0:1, 5:6]), waits=[l2])
        l4 = self.V(lambda e, H, s=s: e.tensor_scalar(
            H["arow"][0:1, s * 16:s * 16 + nh], H["arow"][0:1, s * 16:s * 16 + nh],
            1.0, 60.0, ALU.add, ALU.min), waits=[l3])
        l5 = self.V(lambda e, H, s=s: e.tensor_scalar(
            H["arow"][0:1, s * 16:s * 16 + nh], H["arow"][0:1, s * 16:s * 16 + nh],
            -1.0, None, ALU.mult, ALU.bypass), waits=[l4])
        self.pbc("ps4", nh, lambda H, s=s: H["arow"][0:1, s * 16:s * 16 + nh],
                 lambda H, s=s: H["attM"][:, s * 16:s * 16 + nh], waits=[l5],
                 cwaits=(self.tok.get("uf_free") or []))
    mbar = b.barrier_token("v_m")
    for s in (0, 1):
        _, pp = self.pbc("ps5", 1, lambda H, s=s: H["qpos_s"][0:1, s:s + 1],
                         lambda H, s=s: H["attM"][:, 34 + s:35 + s],
                         waits=[self.tok["consts"]],
                         cwaits=(self.tok.get("uf_free") or []))
        self.V(lambda e, H, s=s: e.tensor_scalar(
            H["attM"][:, 34 + s:35 + s], H["attM"][:, 34 + s:35 + s],
            -1.0, None, ALU.mult, ALU.bypass), waits=[pp])
    qbar = b.barrier_token("v_m")
    if os.environ.get("KATT_CUT") == "3":
        return

    self.tok["slab_free"] = b.barrier_token("t_m")
    for s in (0, 1):
        if s == 1 and os.environ.get("KATT_CUT") == "4":
            break
        if s == 1:
            qload = self.D(lambda e, H: e.dma_start(
                out=H["big"][:, Q0:Q0 + nh * hc].rearrange("p (i f) -> p i f", i=nh),
                in_=self.dram_ap("qtmp").rearrange("(i p) f -> p i f", p=128)),
                waits=[self.tok["qs_free"], b.barrier_token("d_q")], ch="d_q")
        else:
            qload = None
        slab_init = [False] * nh
        for g in range(c["nc"]):
            kr = (g if s == 0 else (c["nc"] - 1 - g)) * dch
            kc = 0 if s == 0 else hc
            vr = (g * T) if s == 0 else ((c["nc"] - 1 - g) * T + hc)
            tkg = self.D(lambda e, H, kr=kr, kc=kc: e.dma_start(
                out=H["big"][:, KG:KG + dch * hc].rearrange("p (i f) -> p i f", i=dch),
                in_=self.dram_ap("kout").rearrange(
                    "(i p) f -> p i f", p=128)[:, kr:kr + dch, kc:kc + hc]),
                waits=[tagk, self.tok.get("kg_free")], ch="d_kg")
            mks = []
            for kt in range(nkt):
                th = self.V(lambda e, H, s=s, g=g, kt=kt: e.tensor_scalar(
                    H["attM"][:, 36:37], H["cc_s"][:, 3:4],
                    H["attM"][:, 34 + s:35 + s], float(g * hc + kt * 128),
                    ALU.add, ALU.add), waits=[qbar, tqx, self.tok.get("m01_free"),
                                              self.tok.get("th_free")])
                mk = self.V(lambda e, H, kt=kt: e.tensor_scalar(
                    H["big"][:, M0 + kt * hc: M0 + (kt + 1) * hc],
                    H["qidx_s"][:, 0:hc], H["attM"][:, 36:37], None,
                    ALU.is_ge, ALU.bypass), waits=[th, self.tok.get("m01_free")])
                self.tok["th_free"] = mk
                mks.append(mk)
            vtok = {}
            for kt in range(nkt):
                vr2 = vr + kt * 128
                vtok[kt] = self.D(lambda e, H, vr2=vr2, kt=kt: e.dma_start(
                    out=H["big"][:, VG + kt * d: VG + (kt + 1) * d],
                    in_=self.dram_ap("vout")[vr2:vr2 + 128, :]),
                    waits=[tagkv, self.tok.get("vg_free")], ch=f"d_v{kt}")
            for h in range(nh):
                for kt in range(nkt):
                    scb = "ps0" if (h * nkt + kt) % 2 == 0 else "ps1"
                    psl = (h * nkt + kt) % 2
                    wtq = [tkg, mbar] + ([qload] if qload else [])
                    mm = self.T(lambda e, H, scb=scb, h=h, kt=kt: e.matmul(
                        H[scb][:, 0:hc],
                        H["big"][:, KG + h * hc + kt * 128: KG + h * hc + (kt + 1) * 128],
                        H["big"][:, Q0 + h * hc: Q0 + (h + 1) * hc],
                        start=True, stop=True),
                        waits=wtq + [self.bankfree.get(scb)])
                    ex = self.S(lambda e, H, scb=scb, psl=psl, s=s, h=h: e.activation(
                        H["big"][:, PP + psl * hc: PP + (psl + 1) * hc], H[scb][:, 0:hc],
                        AF.Exp, scale=csc, bias=H["attM"][:, s * 16 + h: s * 16 + h + 1]),
                        waits=[mm, self.tok.get(("pp_free", psl))])
                    self.bankfree[scb] = ex
                    mq = self.V(lambda e, H, psl=psl, kt=kt: e.tensor_tensor(
                        H["big"][:, PP + psl * hc: PP + (psl + 1) * hc],
                        H["big"][:, PP + psl * hc: PP + (psl + 1) * hc],
                        H["big"][:, M0 + kt * hc: M0 + (kt + 1) * hc], ALU.mult),
                        waits=[ex, mks[kt]])
                    pvb = "ps2" if h % 2 == 0 else "ps3"
                    dnb = "ps4" if h % 2 == 0 else "ps5"
                    pv = self.T(lambda e, H, pvb=pvb, kt=kt, h=h, psl=psl: e.matmul(
                        H[pvb][:, 0:hc],
                        H["big"][:, VG + kt * d + h * 128: VG + kt * d + (h + 1) * 128],
                        H["big"][:, PP + psl * hc: PP + (psl + 1) * hc],
                        start=(kt == 0), stop=(kt == nkt - 1)),
                        waits=[mq, vtok[kt]] + ([self.bankfree.get(pvb)] if kt == 0 else []))
                    dnm = self.T(lambda e, H, dnb=dnb, psl=psl, kt=kt: e.matmul(
                        H[dnb][0:1, 0:hc], H["ones_s"][:, 0:1],
                        H["big"][:, PP + psl * hc: PP + (psl + 1) * hc],
                        start=(kt == 0), stop=(kt == nkt - 1)),
                        waits=[mq] + ([self.bankfree.get(dnb)] if kt == 0 else []))
                    self.tok[("pp_free", psl)] = dnm
                    if kt == nkt - 1:
                        if not slab_init[h]:
                            ev = self.V(lambda e, H, pvb=pvb, h=h: e.tensor_copy(
                                H["act"][:, h * hc:(h + 1) * hc], H[pvb][:, 0:hc]),
                                waits=[pv, self.tok.get("slab_free")])
                            dv = self.V(lambda e, H, dnb=dnb, h=h: e.tensor_copy(
                                H["dn"][0:1, h * hc:(h + 1) * hc], H[dnb][0:1, 0:hc]),
                                waits=[dnm, self.tok.get("dn_free"),
                                       self.tok.get("uf_free")])
                            slab_init[h] = True
                        else:
                            ev = self.V(lambda e, H, pvb=pvb, h=h: e.tensor_tensor(
                                H["act"][:, h * hc:(h + 1) * hc],
                                H["act"][:, h * hc:(h + 1) * hc], H[pvb][:, 0:hc],
                                ALU.add), waits=[pv])
                            dv = self.V(lambda e, H, dnb=dnb, h=h: e.tensor_tensor(
                                H["dn"][0:1, h * hc:(h + 1) * hc],
                                H["dn"][0:1, h * hc:(h + 1) * hc], H[dnb][0:1, 0:hc],
                                ALU.add), waits=[dnm])
                        self.bankfree[pvb] = ev
                        self.bankfree[dnb] = dv
            self.tok["kg_free"] = b.barrier_token("t_m")
            self.tok["vg_free"] = b.barrier_token("t_m")
            self.tok["m01_free"] = b.barrier_token("v_m")
        for h in range(nh):
            dnb2 = "ps0" if h % 2 == 0 else "ps1"
            rc1 = self.V(lambda e, H, h=h: e.reciprocal(
                H["qsq"][0:1, 0:hc], H["dn"][0:1, h * hc:(h + 1) * hc]),
                waits=[b.barrier_token("v_m"), self.tok.get("bc5_free")])
            pb1, _ = self.pbc(dnb2, hc, lambda H: H["qsq"][0:1, 0:hc],
                              waits=[rc1])
            no = self.V(lambda e, H, h=h, dnb2=dnb2: e.tensor_tensor(
                H["act"][:, h * hc:(h + 1) * hc], H["act"][:, h * hc:(h + 1) * hc],
                H[dnb2][0:128, 0:hc], ALU.mult), waits=[pb1])
            self.bankfree[dnb2] = no
            self.tok["bc5_free"] = no
        self.tok["dn_free"] = b.barrier_token("v_m")
        slab_tok = b.barrier_token("v_m")
        self.tok["qs_free"] = b.barrier_token("t_m")
        for gi, oc0 in enumerate(range(0, dch, 3)):
            gsz = min(3, dch - oc0)
            banks = ["ps0", "ps1", "ps2"] if gi % 2 == 0 else ["ps3", "ps4", "ps5"]
            mlast = {}
            for ich in range(0, dch, 4):
                ics = min(4, dch - ich)
                wbuf = wrot[self._dk % 2]
                wch = "d_a" if self._dk % 2 == 0 else "d_b"
                self._dk += 1
                tl = self.D(lambda e, H, wbuf=wbuf, ich=ich, ics=ics, oc0=oc0,
                            gsz=gsz: e.dma_start(
                    out=H[wbuf][:, 0:ics * gsz * 128].rearrange(
                        "p (i f) -> p i f", i=ics),
                    in_=self.dram_ap("q_l0att").rearrange(
                        "(i p) f -> p i f", p=128)[:, ich:ich + ics,
                        3 * d + oc0 * 128: 3 * d + (oc0 + gsz) * 128]),
                    waits=[wfree[wbuf], self.tok["q_l0att"]], ch=wch)
                for ii in range(ics):
                    ic = ich + ii
                    for oi in range(gsz):
                        bk = banks[oi]
                        wt = [tl, slab_tok]
                        if ic == 0:
                            wt.append(self.bankfree.get(bk))
                        mm = self.T(lambda e, H, bk=bk, wbuf=wbuf, ii=ii, oi=oi,
                                    gsz=gsz, ic=ic: e.matmul(
                            H[bk][:, 0:hc],
                            H[wbuf][:, (ii * gsz + oi) * 128:(ii * gsz + oi + 1) * 128],
                            H["act"][:, ic * hc:(ic + 1) * hc],
                            start=(ic == 0), stop=(ic == dch - 1)), waits=wt)
                        mlast[oi] = mm
                wfree[wbuf] = mlast[gsz - 1]
            for oi in range(gsz):
                oc = oc0 + oi
                bk = banks[oi]
                rc = (slice(None), slice(oc * T + s * hc, oc * T + s * hc + hc))
                e2 = self.V(lambda e, H, rc=rc, bk=bk: e.tensor_tensor(
                    H["res"][rc], H["res"][rc], H[bk][:, 0:hc], ALU.add),
                    waits=[mlast[oi], b.barrier_token("v_m")])
                self.bankfree[bk] = e2
        self.tok["slab_free"] = b.barrier_token("t_m")
    self.tok["res"] = b.barrier_token("v_m")
    self.tok["act_free"] = b.barrier_token("t_m")
    self.tok["big_free"] = b.barrier_token("t_m")
    self.tok["stg_free"] = b.barrier_token("t_m")
    self.tok["uf_free"] = [b.barrier_token(s) for s in ("v_m", "s_m", "g_m", "t_m")]


Prog.attn = _attn


def _ssm(self):
    b, c = self.b, self.c
    d, hc, dch, T = c["d"], c["hc"], c["dch"], c["T"]
    hss, inter, cvd = c["hssm"], c["inter"], c["convdim"]
    cvch, ich = cvd // 128, inter // 128
    hh = hc // 2
    nch = hc // 128
    wrot = ["wa", "wb"]
    wfree = self.wfree
    RA = 0
    RB = cvch * hh
    Y0 = 2 * cvch * hh
    assert Y0 + ich * hh <= self.big_cols

    def yoff(half):
        return Y0 if half == 0 else RA

    res_tok = self.tok.get("res")
    self.tok["ra_free"] = self.tok.get("big_free")
    self.tok["rb_free"] = self.tok.get("big_free")
    self.tok["y_region_free"] = self.tok.get("big_free")
    prevs = self.tok.get("bd_stage")
    for ch in range(dch):
        for sd in (0, 1):
            cc0 = ch * T + (sd + 1) * hc - 3
            pv = self.D(lambda e, H, ch=ch, sd=sd, cc0=cc0: e.dma_start(
                out=self.dram_ap("bdin")[ch * 128:(ch + 1) * 128, sd * 3:sd * 3 + 3],
                in_=H["res"][:, cc0:cc0 + 3]), waits=[res_tok], ch="d_sm1")
    for dst, srcn in (("convw_s", "convw"), ("convb_s", "convb"), ("onest_s", "onest")):
        self.D(lambda e, H, dst=dst, srcn=srcn: e.dma_start(
            out=H[dst][:, :] if dst != "onest_s" else H[dst][0:c["hssm"], :],
            in_=self.dram_ap(srcn) if dst != "onest_s"
                else self.dram_ap(srcn)[0:c["hssm"], :]),
            waits=self.tok.get("uf_free") or [], ch="d_sm1")
    self.tok["ssmconsts"] = b.barrier_token("d_sm1")
    bd_done = b.barrier_token("d_sm1")
    tag1 = self.CC(lambda e, H: e.collective_compute(
        "AllGather", ALU.bypass, replica_groups=[list(range(c["nc"]))],
        ins=[self.dram_h("bdin").ap()], outs=[self.dram_h("bdout").ap()]),
        waits=[bd_done])
    self.tok["kg_free"] = None

    for sd in (0, 1):
        t0 = sd * hc
        if sd == 1:
            self.tok["stg_free"] = [t for t in (self.tok.get(("ydma_last", 0)),
                                                self.tok.get(("ydma_last", 1))) if t]
        lt = self.ln_quant(t0, False, res_tok=res_tok)
        lnt = list(self.tok["ln_act_toks"])
        self.tok["act_free"] = b.barrier_token("t_m")
        twd = self.GD(lambda e, H: e.dma_start(
            out=H["wfq"][:, 0:dch * hss].rearrange("p (i h) -> p i h", i=dch),
            in_=self.dram_ap("l1wdt").rearrange("(i p) h -> p i h", p=128)),
            waits=[self.tok.get("wfq_free"), self.tok["consts"]])
        pdt = None
        for ic in range(dch):
            mm = self.T(lambda e, H, ic=ic: e.matmul(
                H["ps0"][0:hss, 0:hc], H["wfq"][:, ic * hss:(ic + 1) * hss],
                H["act"][:, ic * hc:(ic + 1) * hc],
                start=(ic == 0), stop=(ic == dch - 1)),
                waits=[lnt[ic], twd] + ([self.bankfree.get("ps0")] if ic == 0 else []))
            pdt = mm
        self.tok["wfq_free"] = b.barrier_token("t_m")
        e1 = self.V(lambda e, H: e.tensor_scalar(
            H["dts"][0:hss, 0:hc], H["ps0"][0:hss, 0:hc], H["smv_s"][0:hss, 0:1], None,
            ALU.add, ALU.bypass),
            waits=[pdt, self.tok.get("dts_free")] + (self.tok.get("uf_free") or []))
        self.bankfree["ps0"] = e1
        e2 = self.V(lambda e, H: e.tensor_scalar(
            H["wsp"][0:hss, 0:hc], H["dts"][0:hss, 0:hc], 10.0, None,
            ALU.min, ALU.bypass), waits=[e1])
        e3 = self.S(lambda e, H: e.activation(
            H["wsp"][0:hss, 0:hc], H["wsp"][0:hss, 0:hc], AF.Exp), waits=[e2])
        e4 = self.S(lambda e, H: e.activation(
            H["wsp"][0:hss, 0:hc], H["wsp"][0:hss, 0:hc], AF.Ln,
            bias=H["cc_s"][0:1, 1:2] if hss == 1 else H["cc_s"][0:hss, 1:2]), waits=[e3])
        e5 = self.V(lambda e, H: e.tensor_scalar(
            H["wsp"][0:hss, hc:2 * hc], H["dts"][0:hss, 0:hc], 10.0, None,
            ALU.is_ge, ALU.bypass), waits=[e2])
        e6 = self.V(lambda e, H: e.tensor_tensor(
            H["dts"][0:hss, 0:hc], H["dts"][0:hss, 0:hc], H["wsp"][0:hss, 0:hc],
            ALU.subtract), waits=[e4, e5])
        e7 = self.V(lambda e, H: e.tensor_tensor(
            H["dts"][0:hss, 0:hc], H["dts"][0:hss, 0:hc], H["wsp"][0:hss, hc:2 * hc],
            ALU.mult), waits=[e6])
        e8 = self.V(lambda e, H: e.tensor_tensor(
            H["dts"][0:hss, 0:hc], H["dts"][0:hss, 0:hc], H["wsp"][0:hss, 0:hc],
            ALU.add), waits=[e7])
        dt_tok = e8
        if os.environ.get("KSSM_CUT") == "2":
            return

        pv = None
        for rh4 in range(2):
            tbl = self.D(lambda e, H, rh4=rh4: e.dma_start(
                out=H["bdl"][:, 0:dch * 4 * 6].rearrange("p (a c) -> p a c", c=6),
                in_=self.dram_ap("bdout").rearrange(
                    "(a p) c -> p a c", p=128)[:, rh4 * 4 * dch:(rh4 + 1) * 4 * dch, :]),
                waits=[tag1, pv, b.barrier_token("v_m")]
                      + (self.tok.get("uf_free") or []), ch="d_kg")
            for ch in range(dch):
                for rr in range(4):
                    r = rh4 * 4 + rr
                    if r == 0:
                        t1 = self.V(lambda e, H, ch=ch, sd=sd: e.tensor_scalar(
                            H["bres"][:, ch * 3:(ch + 1) * 3],
                            H["bdl"][:, ch * 6 + sd * 3:ch * 6 + sd * 3 + 3],
                            H["bsel_s"][:, sd * 8:sd * 8 + 1], None,
                            ALU.mult, ALU.bypass),
                            waits=[tbl, pv, self.tok.get("bres_free")])
                    else:
                        t1 = self.V(lambda e, H, ch=ch, rr=rr, r=r, sd=sd:
                                    e.scalar_tensor_tensor(
                            H["bres"][:, ch * 3:(ch + 1) * 3],
                            H["bdl"][:, (rr * dch + ch) * 6 + sd * 3:
                                      (rr * dch + ch) * 6 + sd * 3 + 3],
                            H["bsel_s"][:, sd * 8 + r:sd * 8 + r + 1],
                            H["bres"][:, ch * 3:(ch + 1) * 3], ALU.mult, ALU.add),
                            waits=[tbl, pv])
                    pv = t1
        W3 = 3
        sm1 = sm2 = None
        for ch in range(dch):
            tc = self.V(lambda e, H, ch=ch: e.tensor_copy(
                H["sqb"][:, 0:3], H["bres"][:, ch * 3:(ch + 1) * 3]),
                waits=[pv, sm1, self.tok.get("sqb_free"), self.tok.get("ts2_prev")])
            ts2 = self.V(lambda e, H: e.tensor_tensor(
                H["sqb"][:, 4:7], H["sqb"][:, 0:3], H["sqb"][:, 0:3], ALU.mult),
                waits=[tc, sm2])
            self.tok["ts2_prev"] = ts2
            sm1 = self.T(lambda e, H, ch=ch: e.matmul(
                H["ps1"][0:1, 0:3], H["ones_s"][:, 0:1], H["sqb"][:, 0:3],
                start=(ch == 0), stop=(ch == dch - 1)),
                waits=[tc] + ([self.bankfree.get("ps1")] if ch == 0 else []))
            sm2 = self.T(lambda e, H, ch=ch: e.matmul(
                H["ps2"][0:1, 0:3], H["ones_s"][:, 0:1], H["sqb"][:, 4:7],
                start=(ch == 0), stop=(ch == dch - 1)),
                waits=[ts2] + ([self.bankfree.get("ps2")] if ch == 0 else []))
        bm = self.V(lambda e, H: e.tensor_scalar(
            H["alr"][0:1, 0:3], H["ps1"][0:1, 0:3], 1.0 / d, None, ALU.mult, ALU.bypass),
            waits=[sm1])
        bm2 = self.V(lambda e, H: e.tensor_scalar(
            H["alr"][0:1, 4:7], H["ps2"][0:1, 0:3], 1.0 / d, None, ALU.mult, ALU.bypass),
            waits=[sm2])
        self.bankfree["ps1"] = bm
        self.bankfree["ps2"] = bm2
        bv1 = self.V(lambda e, H: e.tensor_tensor(
            H["alr"][0:1, 8:11], H["alr"][0:1, 0:3], H["alr"][0:1, 0:3], ALU.mult),
            waits=[bm])
        bv2 = self.V(lambda e, H: e.tensor_tensor(
            H["alr"][0:1, 8:11], H["alr"][0:1, 4:7], H["alr"][0:1, 8:11], ALU.subtract),
            waits=[bv1, bm2])
        bl1 = self.S(lambda e, H: e.activation(H["alr"][0:1, 8:11], H["alr"][0:1, 8:11],
                                               AF.Ln, bias=H["cc_s"][0:1, 0:1]), waits=[bv2])
        bl2 = self.S(lambda e, H: e.activation(H["alr"][0:1, 8:11], H["alr"][0:1, 8:11],
                                               AF.Exp, scale=-0.5), waits=[bl1])
        _, bb1 = self.pbc("ps1", 3, lambda H: H["alr"][0:1, 0:3],
                          lambda H: H["bc"][:, 0:3],
                          waits=[bm], cwaits=[self.tok.get("bc_free")])
        _, bb2 = self.pbc("ps2", 3, lambda H: H["alr"][0:1, 8:11],
                          lambda H: H["bc"][:, 8:11],
                          waits=[bl2], cwaits=[self.tok.get("bc_free")])
        pv2 = None
        for ch in range(dch):
            a1 = self.V(lambda e, H, ch=ch: e.scalar_tensor_tensor(
                H["sqb"][:, 8:11], H["bres"][:, ch * 3:(ch + 1) * 3], 1.0,
                H["bc"][:, 0:3], ALU.mult, ALU.subtract), waits=[bb1, pv2])
            a2 = self.V(lambda e, H, ch=ch: e.tensor_tensor(
                H["blnx"][:, ch * 3:(ch + 1) * 3], H["sqb"][:, 8:11], H["bc"][:, 8:11],
                ALU.mult), waits=[a1, bb2])
            pv2 = a2
        self.tok["bres_free"] = pv2
        for ocb in range(0, cvd // 128, 2):
            ocs = min(2, cvch - ocb)
            for icb in range(0, dch, 4):
                ics = min(4, dch - icb)
                wbuf = wrot[self._dk % 2]
                wch = "d_a" if self._dk % 2 == 0 else "d_b"
                self._dk += 1
                tl = self.D(lambda e, H, wbuf=wbuf, icb=icb, ics=ics, ocb=ocb, ocs=ocs:
                            e.dma_start(
                    out=H[wbuf][:, 0:ics * ocs * 128].rearrange("p (i f) -> p i f", i=ics),
                    in_=self.dram_ap("q_l1zx").rearrange(
                        "(i p) f -> p i f", p=128)[:, icb:icb + ics,
                        inter + ocb * 128: inter + (ocb + ocs) * 128]),
                    waits=[wfree[wbuf], self.tok["q_l1zx"]], ch=wch)
                for ii in range(ics):
                    ic = icb + ii
                    for oi in range(ocs):
                        oc = ocb + oi
                        bk = f"ps{3 + (oc % 2)}"
                        wt = [tl, pv2]
                        if ic == 0:
                            wt.append(self.bankfree.get(bk))
                        mm = self.T(lambda e, H, bk=bk, wbuf=wbuf, ii=ii, oi=oi, ocs=ocs,
                                    ic=ic, oc=oc: e.matmul(
                            H[bk][:, (oc % 2) * 0:3],
                            H[wbuf][:, (ii * ocs + oi) * 128:(ii * ocs + oi + 1) * 128],
                            H["blnx"][:, ic * 3:(ic + 1) * 3],
                            start=(ic == 0), stop=(ic == dch - 1)), waits=wt)
                        self.tok[("bmm", oc)] = mm
                    wfree[wbuf] = self.tok[("bmm", ocb + ocs - 1)]
            for oi in range(ocs):
                oc = ocb + oi
                bk = f"ps{3 + (oc % 2)}"
                ev = self.V(lambda e, H, bk=bk, oc=oc: e.tensor_copy(
                    H["bdx"][:, oc * 3:(oc + 1) * 3], H[bk][:, 0:3]),
                    waits=[self.tok[("bmm", oc)], self.tok.get("bdx_free")])
                self.bankfree[bk] = ev
        bdx_tok = b.barrier_token("v_m")
        if os.environ.get("KSSM_CUT") == "3":
            return

        side_scan_done = []
        for half in range(2):
            th0 = t0 + half * hh
            for gi, oc0 in enumerate(range(0, cvch, 3)):
                gsz = min(3, cvch - oc0)
                banks = ["ps0", "ps1", "ps2"] if gi % 2 == 0 else ["ps3", "ps4", "ps5"]
                mlast = {}
                for icb in range(0, dch, 4):
                    ics = min(4, dch - icb)
                    wbuf = wrot[self._dk % 2]
                    wch = "d_a" if self._dk % 2 == 0 else "d_b"
                    self._dk += 1
                    tl = self.D(lambda e, H, wbuf=wbuf, icb=icb, ics=ics, oc0=oc0,
                                gsz=gsz: e.dma_start(
                        out=H[wbuf][:, 0:ics * gsz * 128].rearrange(
                            "p (i f) -> p i f", i=ics),
                        in_=self.dram_ap("q_l1zx").rearrange(
                            "(i p) f -> p i f", p=128)[:, icb:icb + ics,
                            inter + oc0 * 128: inter + (oc0 + gsz) * 128]),
                        waits=[wfree[wbuf], self.tok["q_l1zx"]], ch=wch)
                    for ii in range(ics):
                        ic = icb + ii
                        for oi in range(gsz):
                            bk = banks[oi]
                            wt = [tl, lnt[ic]]
                            if ic == 0:
                                wt.append(self.bankfree.get(bk))
                            mm = self.T(lambda e, H, bk=bk, wbuf=wbuf, ii=ii, oi=oi,
                                        gsz=gsz, ic=ic, half=half: e.matmul(
                                H[bk][:, 0:hh],
                                H[wbuf][:, (ii * gsz + oi) * 128:(ii * gsz + oi + 1) * 128],
                                H["act"][:, ic * hc + half * hh: ic * hc + half * hh + hh],
                                start=(ic == 0), stop=(ic == dch - 1)), waits=wt)
                            mlast[oi] = mm
                    wfree[wbuf] = mlast[gsz - 1]
                for oi in range(gsz):
                    oc = oc0 + oi
                    bk = banks[oi]
                    ev = self.V(lambda e, H, bk=bk, oc=oc: e.tensor_copy(
                        H["big"][:, RA + oc * hh: RA + (oc + 1) * hh], H[bk][:, 0:hh]),
                        waits=[mlast[oi], self.tok.get("ra_free")])
                    self.bankfree[bk] = ev
            raw_tok = b.barrier_token("v_m")
            K = c["K"]
            bdsrc = "bdx" if half == 0 else "bdx2"
            cv_toks = []
            pvc = self.tok.get("rb_free")
            for ch in range(cvch):
                acc = None
                for k in range(K):
                    sh = k - (K - 1)
                    lo = max(0, -sh)
                    t1 = self.V(lambda e, H, ch=ch, k=k, sh=sh, lo=lo, acc=acc: (
                        e.tensor_scalar(
                            H["big"][:, RB + ch * hh + lo: RB + (ch + 1) * hh],
                            H["big"][:, RA + ch * hh + lo + sh: RA + (ch + 1) * hh + sh],
                            H["convw_s"][:, ch * K + k:ch * K + k + 1], None,
                            ALU.mult, ALU.bypass)
                        if acc is None else
                        e.scalar_tensor_tensor(
                            H["big"][:, RB + ch * hh + lo: RB + (ch + 1) * hh],
                            H["big"][:, RA + ch * hh + lo + sh: RA + (ch + 1) * hh + sh],
                            H["convw_s"][:, ch * K + k:ch * K + k + 1],
                            H["big"][:, RB + ch * hh + lo: RB + (ch + 1) * hh],
                            ALU.mult, ALU.add)),
                        waits=[raw_tok, pvc, self.tok["ssmconsts"], acc,
                               self.tok.get("rb_free"), self.tok.get("rb_free2")])
                    pvc = t1
                    if lo > 0:
                        t2 = self.V(lambda e, H, ch=ch, k=k, sh=sh, lo=lo, acc=acc, bdsrc=bdsrc: (
                            e.tensor_scalar(
                                H["big"][:, RB + ch * hh: RB + ch * hh + lo],
                                H[bdsrc][:, ch * 3 + 3 + sh: ch * 3 + 3 + sh + lo],
                                H["convw_s"][:, ch * K + k:ch * K + k + 1], None,
                                ALU.mult, ALU.bypass)
                            if acc is None else
                            e.scalar_tensor_tensor(
                                H["big"][:, RB + ch * hh: RB + ch * hh + lo],
                                H[bdsrc][:, ch * 3 + 3 + sh: ch * 3 + 3 + sh + lo],
                                H["convw_s"][:, ch * K + k:ch * K + k + 1],
                                H["big"][:, RB + ch * hh: RB + ch * hh + lo],
                                ALU.mult, ALU.add)),
                            waits=[bdx_tok if half == 0 else self.tok.get("bdx2_tok"), t1])
                        pvc = t2
                    acc = pvc
                sl = self.S(lambda e, H, ch=ch: e.activation(
                    H["big"][:, RB + ch * hh: RB + (ch + 1) * hh],
                    H["big"][:, RB + ch * hh: RB + (ch + 1) * hh], AF.Silu,
                    bias=H["convb_s"][:, ch:ch + 1]),
                    waits=[pvc, self.tok["ssmconsts"]])
                cv_toks.append(sl)
                pvc = sl
            if half == 0:
                sv = None
                for ch2 in range(cvch):
                    sv = self.V(lambda e, H, ch2=ch2: e.tensor_copy(
                        H["bdx2"][:, ch2 * 3:(ch2 + 1) * 3],
                        H["big"][:, RA + (ch2 + 1) * hh - 3: RA + (ch2 + 1) * hh]),
                        waits=[raw_tok, sv])
                self.tok["bdx2_tok"] = sv
            cv_tok = b.barrier_token("s_m")
            self.tok["ra_free"] = b.barrier_token("v_m")
            if half == 1:
                self.tok["y_region_free"] = self.tok["ra_free"]
            if os.environ.get("KSSM_CUT") == "4":
                return
            self._ssm_scan(sd, half, cv_tok, dt_tok)
            if os.environ.get("KSSM_CUT") == "5":
                return
            self.tok["rb_free"] = self.tok["scan_y_done"]
            self.tok["rb_free2"] = self.tok["scan_rb_done"]
            self.tok["ra_free"] = [self.tok["scan_y_done"], self.tok["scan_rb_done"]]
            if half == 0:
                self.tok["bdx_free"] = self.tok["ra_free"]
        self._ssm_ag_send(sd)
        self.tok["dts_free"] = b.barrier_token("v_m")
        if os.environ.get("KSSM_CUT") == "6":
            return
    if os.environ.get("KSSM_CUT") == "7":
        return
    self._ssm_post()


Prog.ssm = _ssm


def _ssm_scan(self, sd, half, cv_tok, dt_tok):
    b, c = self.b, self.c
    hss, inter, cvd = c["hssm"], c["inter"], c["convdim"]
    cvch, ich = cvd // 128, inter // 128
    hc, hh = c["hc"], c["hc"] // 2
    RA = 0
    RB = cvch * hh
    Y0 = 2 * cvch * hh
    yo = Y0 if half == 0 else RA
    XOF, BOF, COF = RB, RB + ich * hh, RB + (ich + 1) * hh
    npair = hss // 2

    if half == 0:
        z1 = self.V(lambda e, H: e.memset(H["ssm_s"][:, :], 0.0),
                    waits=[self.tok.get("ssm_s_free"), dt_tok]
                          + (self.tok.get("uf_free") or []))
        z2 = self.V(lambda e, H: e.memset(H["ssm_sb"][:, :], 0.0), waits=[z1])
        self.tok["sstate"] = z2
        zf = self.V(lambda e, H: e.memset(H["facc"][0:hss, 0:1], 0.0),
                    waits=[dt_tok, b.barrier_token("t_m")])
        self.tok["facc"] = zf

    nqh = hh // 128
    for q in range(nqh):
        cc = half * nqh + q
        col0 = cc * 128
        t1 = self.V(lambda e, H, col0=col0: e.tensor_scalar(
            H["dta"][0:hss, 0:128], H["dts"][0:hss, col0:col0 + 128],
            H["smv_s"][0:hss, 1:2], None, ALU.mult, ALU.bypass),
            waits=[dt_tok, self.tok.get("dta_free"), self.tok["consts"]])
        t2 = self.V(lambda e, H: e.tensor_tensor_scan(
            H["afm"][0:hss, 0:128], H["onest_s"][0:hss, 0:128], H["dta"][0:hss, 0:128],
            0.0, ALU.mult, ALU.add),
            waits=[t1, self.tok.get("afm_free"), self.tok["ssmconsts"]])
        e2 = self.S(lambda e, H, cc=cc, sd=sd, col0=col0: e.activation(
            H["ea2"][0:hss, sd * hc + col0: sd * hc + col0 + 128],
            H["afm"][0:hss, 0:128], AF.Exp,
            bias=H["facc"][0:hss, cc:cc + 1]), waits=[t2, self.tok["facc"],
                                                      self.tok.get("ea2_free")])
        fn = self.V(lambda e, H, cc=cc: e.tensor_tensor(
            H["facc"][0:hss, cc + 1:cc + 2], H["facc"][0:hss, cc:cc + 1],
            H["afm"][0:hss, 127:128], ALU.add), waits=[t2, self.tok["facc"]])
        self.tok["facc"] = fn
        ev1 = self.S(lambda e, H: e.activation(H["eav"][0:hss, 0:128],
                                               H["afm"][0:hss, 0:128], AF.Exp),
                     waits=[t2, self.tok.get("eav_free")])
        tr1 = self.T(lambda e, H: e.transpose(H["pstf"][:, 0:hss], H["afm"][0:hss, 0:128],
                                              H["identf_s"][0:hss, 0:hss]),
                     waits=[t2, self.tok.get("pstf_free"), self.tok["consts"]])
        c1 = self.V(lambda e, H: e.tensor_copy(H["aT"][:, 0:hss], H["pstf"][:, 0:hss]),
                    waits=[tr1, self.tok.get("aT_free")])
        c1n = self.V(lambda e, H: e.tensor_scalar(H["aTn"][:, 0:hss], H["aT"][:, 0:hss],
                                                  -1.0, None, ALU.mult, ALU.bypass),
                     waits=[c1])
        tr2 = self.T(lambda e, H: e.transpose(H["pstf"][0:1, 128:128 + hss],
                                              H["afm"][0:hss, 127:128],
                                              H["identf_s"][0:hss, 0:hss]),
                     waits=[t2, c1])
        c2 = self.V(lambda e, H: e.tensor_copy(H["alr"][0:1, 0:hss],
                                               H["pstf"][0:1, 128:128 + hss]),
                    waits=[tr2, self.tok.get("alr_free")])
        tr3 = self.T(lambda e, H, col0=col0: e.transpose(
            H["pstf"][:, 0:hss], H["dts"][0:hss, col0:col0 + 128],
            H["identf_s"][0:hss, 0:hss]),
            waits=[dt_tok, c1, c2])
        self.tok["pstf_free"] = tr3
        c3 = self.V(lambda e, H: e.tensor_copy(H["dw"][:, 0:hss], H["pstf"][:, 0:hss]),
                    waits=[tr3, self.tok.get("dw_free")])
        self.tok["pstf_free"] = c3
        wm, _ = self.pbc("ps4", hss, lambda H: H["alr"][0:1, 0:hss], waits=[c2])
        w2 = self.V(lambda e, H: e.tensor_tensor(H["wexp"][:, 0:hss],
                                                 H["ps4"][0:128, 0:hss],
                                                 H["aT"][:, 0:hss], ALU.subtract),
                    waits=[wm, c1, self.tok.get("wexp_free")])
        self.bankfree["ps4"] = w2
        w3 = self.S(lambda e, H: e.activation(H["wexp"][:, 0:hss], H["wexp"][:, 0:hss],
                                              AF.Exp), waits=[w2])
        w4 = self.V(lambda e, H: e.tensor_tensor(H["wexp"][:, 0:hss], H["wexp"][:, 0:hss],
                                                 H["dw"][:, 0:hss], ALU.mult),
                    waits=[w3, c3])
        if os.environ.get("KSCAN_CUT") == "a":
            return
        xts = []
        for ch in range(ich):
            sl = ch % 4
            trx = self.T(lambda e, H, ch=ch, sl=sl, q=q: e.transpose(
                H["pst"][:, sl * 128:(sl + 1) * 128],
                H["big"][:, RB + ch * hh + q * 128: RB + ch * hh + q * 128 + 128],
                H["ident_s"][:]),
                waits=[cv_tok, self.tok.get("pst_read")])
            cx = self.V(lambda e, H, ch=ch, sl=sl: e.tensor_copy(
                H["dxt"][:, ch * 128:(ch + 1) * 128], H["pst"][:, sl * 128:(sl + 1) * 128]),
                waits=[trx, self.tok.get("dxt_free")])
            self.tok["pst_read"] = cx
            xts.append(cx)
        xt_tok = b.barrier_token("v_m")
        ihalf = inter // 2
        for h in ([] if os.environ.get("KSCAN_CUT") == "b1" else range(hss)):
            self.V(lambda e, H, h=h: e.tensor_scalar(
                H["dxw"][:, h * 64:(h + 1) * 64], H["dxt"][:, h * 64:(h + 1) * 64],
                H["wexp"][:, h:h + 1], None, ALU.mult, ALU.bypass),
                waits=[w4, xt_tok] + ([self.tok.get("dxw_free")] if h == 0 else []))
        m1 = b.barrier_token("v_m")
        self.tok["wexp_free"] = m1
        if os.environ.get("KSCAN_CUT") == "b":
            return
        sst = self.T(lambda e, H, q=q: e.matmul(
            H["ps3"][:, 0:128],
            H["big"][:, BOF + q * 128: BOF + q * 128 + 128],
            H["big"][:, COF + q * 128: COF + q * 128 + 128],
            start=True, stop=True), waits=[cv_tok, self.bankfree.get("ps3")])
        sste = self.V(lambda e, H: e.tensor_copy(H["sst"][:, 0:128], H["ps3"][:, 0:128]),
                      waits=[sst, self.tok.get("sst_free")])
        self.bankfree["ps3"] = sste
        cp1 = self.V(lambda e, H, cc=cc, q=q, sd=sd: e.tensor_copy(
            H["cp"][:, sd * hc + cc * 128: sd * hc + (cc + 1) * 128],
            H["big"][:, COF + q * 128: COF + q * 128 + 128]),
            waits=[cv_tok, self.tok.get("cp_free")])
        trb = self.T(lambda e, H, q=q: e.transpose(
            H["pst"][:, 0:128], H["big"][:, BOF + q * 128: BOF + q * 128 + 128],
            H["ident_s"][:]), waits=[cv_tok, self.tok.get("pst_read")])
        cb = self.V(lambda e, H: e.tensor_copy(H["gtl"][:, 0:128], H["pst"][:, 0:128]),
                    waits=[trb, self.tok.get("bT_free")])
        self.tok["pst_read"] = cb

        if os.environ.get("KSCAN_CUT") == "c":
            return
        ydone = []
        for hp in range(npair):
            ybank = "ps2"
            for par in range(2):
                h = 2 * hp + par
                g1 = self.T(lambda e, H, h=h: e.matmul(
                    H["ps4"][:, 0:128],
                    H["identf_s"][0:hss, h:h + 1].broadcast_to([hss, 128]),
                    H["afm"][0:hss, 0:128], start=True, stop=True),
                    waits=[t2, self.bankfree.get("ps4"), self.tok["consts"]])
                g2 = self.V(lambda e, H: e.tensor_tensor(
                    H["emt"][:, 0:128], H["ps4"][:, 0:128], H["ssdm_s"][:], ALU.add),
                    waits=[g1, self.tok["consts"], self.tok.get("emt_free")])
                self.bankfree["ps4"] = g2
                g3 = self.S(lambda e, H, h=h: e.activation(
                    H["emt"][:, 0:128], H["emt"][:, 0:128], AF.Exp,
                    bias=H["aTn"][:, h:h + 1]), waits=[g2, c1n])
                g4 = self.V(lambda e, H, h=h: e.scalar_tensor_tensor(
                    H["gtl"][:, 128:256], H["emt"][:, 0:128], H["dw"][:, h:h + 1],
                    H["sst"][:, 0:128], ALU.mult, ALU.mult),
                    waits=[g3, sste, c3, self.tok.get("g4_free")])
                self.tok["emt_free"] = g4
                g5 = self.T(lambda e, H, h=h: e.matmul(
                    H["ps5"][:, 0:128],
                    H["identf_s"][0:hss, h:h + 1].broadcast_to([hss, 128]),
                    H["eav"][0:hss, 0:128], start=True, stop=True),
                    waits=[ev1, self.bankfree.get("ps5")])
                g6 = self.V(lambda e, H, q=q: e.tensor_tensor(
                    H["gtl"][:, 256:384], H["ps5"][:, 0:128],
                    H["big"][:, COF + q * 128: COF + q * 128 + 128], ALU.mult),
                    waits=[g5, cv_tok, self.tok.get("g6_free")])
                self.bankfree["ps5"] = g6
                self.tok["ct_free"] = g6
                tp = (0, 64) if par == 1 else None
                ps_ap = (slice(64, 128), slice(0, 128)) if par == 1 else (slice(0, 64), slice(0, 128))
                mi = self.T(lambda e, H, h=h, ps_ap=ps_ap, tp=tp: e.matmul(
                    H["ps2"][ps_ap], H["dxt"][:, h * 64:(h + 1) * 64],
                    H["gtl"][:, 128:256],
                    start=True, stop=False, tile_position=tp,
                    skip_group_check=True),
                    waits=[xt_tok, g4] + ([self.bankfree.get("ps2")] if par == 0 else []))
                mo = self.T(lambda e, H, h=h, ps_ap=ps_ap, tp=tp: e.matmul(
                    H["ps2"][ps_ap], H["ssm_sb"][:, h * 64:(h + 1) * 64],
                    H["gtl"][:, 256:384],
                    start=False, stop=True, tile_position=tp, skip_group_check=True),
                    waits=[g6, self.tok["sstate"]])
                self.tok["g4_free"] = mo
                self.tok["g6_free"] = mo
            grp = hp // 4
            par2 = grp % 2
            slot = par2 * 4 + hp % 4
            ye = self.V(lambda e, H, hp=hp, q=q, slot=slot: e.scalar_tensor_tensor(
                H["stg"][:, slot * 128:(slot + 1) * 128],
                H["big"][:, RB + hp * hh + q * 128: RB + hp * hh + q * 128 + 128],
                H["dcolx_s"][:, hp:hp + 1], H["ps2"][:, 0:128], ALU.mult, ALU.add),
                waits=[b.barrier_token("t_m"), cv_tok, self.tok["consts"],
                       self.tok.get(("ydma_p", par2)),
                       self.tok.get("stg_free") if hp == 0 else None])
            self.bankfree["ps2"] = ye
            ydone.append(ye)
            if hp % 4 == 3 or hp == npair - 1:
                n4 = (hp % 4) + 1
                chv = f"d_v{2 + par2}"
                tcol = sd * hc + half * hh + q * 128
                tk = self.D(lambda e, H, tcol=tcol, grp=grp, n4=n4, par2=par2: e.dma_start(
                    out=self.dram_ap("ytmp").rearrange(
                        "(i p) t -> p i t", p=128)[:, grp * 4:grp * 4 + n4,
                        tcol:tcol + 128],
                    in_=H["stg"][:, par2 * 512:par2 * 512 + n4 * 128].rearrange(
                        "p (i t) -> p i t", i=n4)),
                    waits=[ye], ch=chv)
                self.tok[("ydma_p", par2)] = tk
                self.tok[("ydma_last", par2)] = tk
        if os.environ.get("KSCAN_CUT") == "d":
            return
        er = self.S(lambda e, H: e.activation(H["alr"][0:1, hss:2 * hss],
                                              H["alr"][0:1, 0:hss], AF.Exp),
                    waits=[c2])
        sprev = self.tok["sstate"]
        _, ec_ = self.pbc("ps4", hss, lambda H: H["alr"][0:1, hss:2 * hss],
                          lambda H: H["expand"][:, 0:hss],
                          waits=[er], cwaits=[self.tok.get("exp2_free")])
        bw = min(512, ihalf)
        for blk in range(inter // bw):
            dsb = "ps0" if blk % 2 == 0 else "ps1"
            ds = self.T(lambda e, H, blk=blk, dsb=dsb, bw=bw: e.matmul(
                H[dsb][:, 0:bw], H["gtl"][:, 0:128],
                H["dxw"][:, blk * bw:(blk + 1) * bw],
                start=True, stop=True),
                waits=[cb, m1, self.bankfree.get(dsb)])
            hpb = bw // 64
            ulast = None
            for hi in range(hpb):
                h = blk * hpb + hi
                ulast = self.V(lambda e, H, h=h, hi=hi, dsb=dsb: e.scalar_tensor_tensor(
                    H["ssm_s"][:, h * 64:(h + 1) * 64],
                    H["ssm_s"][:, h * 64:(h + 1) * 64],
                    H["expand"][:, h:h + 1],
                    H[dsb][:, hi * 64:(hi + 1) * 64], ALU.mult, ALU.add),
                    waits=[ec_, ds, sprev])
            self.bankfree[dsb] = ulast
        self.tok["exp2_free"] = b.barrier_token("v_m")
        supd = b.barrier_token("v_m")
        sb1 = self.V(lambda e, H: e.tensor_copy(H["ssm_sb"][:, :], H["ssm_s"][:, :]),
                     waits=[supd, b.barrier_token("t_m")])
        self.tok["sstate"] = sb1
        self.tok["expand_free"] = b.barrier_token("v_m")
        self.tok["dta_free"] = t2
        self.tok["afm_free"] = [b.barrier_token("s_m"), b.barrier_token("t_m"),
                                b.barrier_token("g_m")]
        self.tok["eav_free"] = b.barrier_token("t_m")
        self.tok["aT_free"] = [b.barrier_token("v_m"), b.barrier_token("s_m")]
        self.tok["dxt_free"] = b.barrier_token("t_m")
        self.tok["dxw_free"] = b.barrier_token("t_m")
        self.tok["sst_free"] = b.barrier_token("v_m")
        self.tok["bT_free"] = b.barrier_token("t_m")
    self.tok["scan_rb_done"] = b.barrier_token("t_m")
    self.tok["scan_y_done"] = b.barrier_token("v_m")


def _ssm_ag_send(self, sd):
    b, c = self.b, self.c
    hss, inter = c["hssm"], c["inter"]
    cvch = c["convdim"] // 128
    ich = inter // 128
    hh = c["hc"] // 2
    Y0 = 2 * cvch * hh
    ytoks = [self.tok.get(("ydma_last", 0)), self.tok.get(("ydma_last", 1))]
    self.tok[("ydma_side", sd)] = ytoks
    self.tok["ra_free"] = self.tok["scan_y_done"]
    t1 = self.D(lambda e, H, sd=sd: e.dma_start(
        out=self.dram_ap("dsin")[sd * 128:(sd + 1) * 128, :], in_=H["ssm_s"][:, :]),
        waits=[self.tok["sstate"], self.tok.get("sm2_chain")], ch="d_sm2")
    self.tok["sm2_chain"] = t1
    nch = c["hc"] // 128
    tr = self.T(lambda e, H, nch=nch: e.transpose(H["pstf"][0:1, 128:128 + hss],
                                         H["facc"][0:hss, nch:nch + 1],
                                         H["identf_s"][0:hss, 0:hss]),
                waits=[self.tok["facc"], self.tok.get("pstf_free")])
    c2 = self.V(lambda e, H: e.tensor_copy(H["alr"][0:1, 2 * hss:3 * hss],
                                           H["pstf"][0:1, 128:128 + hss]), waits=[tr])
    self.tok["pstf_free"] = c2
    t2 = self.D(lambda e, H, sd=sd: e.dma_start(
        out=self.dram_ap("dcin")[sd:sd + 1, :], in_=H["alr"][0:1, 2 * hss:3 * hss]),
        waits=[c2, self.tok.get("sm2_chain")], ch="d_sm2")
    self.tok["sm2_chain"] = t2
    self.tok["alr_free"] = t2
    self.tok["ssm_s_free"] = t1
    self.tok[("ag_send", sd)] = b.barrier_token("d_sm2")


Prog._ssm_scan = _ssm_scan
Prog._ssm_ag_send = _ssm_ag_send


def _ssm_post(self):
    b, c = self.b, self.c
    d, hc, dch, T = c["d"], c["hc"], c["dch"], c["T"]
    hss, inter, cvd = c["hssm"], c["inter"], c["convdim"]
    cvch, ich = cvd // 128, inter // 128
    hh = hc // 2
    nch = hc // 128
    wrot = ["wa", "wb"]
    wfree = self.wfree
    ydma_fin = [t for t in (self.tok.get(("ydma_last", 0)),
                            self.tok.get(("ydma_last", 1))) if t]
    self.tok["stgc_free"] = list(ydma_fin)
    t1 = self.CC(lambda e, H: e.collective_compute(
        "AllGather", ALU.bypass, replica_groups=[list(range(c["nc"]))],
        ins=[self.dram_h("dsin").ap()], outs=[self.dram_h("dsout").ap()]),
        waits=[self.tok[("ag_send", 0)], self.tok[("ag_send", 1)]])
    t2 = self.CC(lambda e, H: e.collective_compute(
        "AllGather", ALU.bypass, replica_groups=[list(range(c["nc"]))],
        ins=[self.dram_h("dcin").ap()], outs=[self.dram_h("dcout").ap()]),
        waits=[self.tok[("ag_send", 0)], self.tok[("ag_send", 1)]])
    tl = self.D(lambda e, H: e.dma_start(
        out=H["hcd"][0:1, 0:16 * hss], in_=self.dram_ap("dcout").flatten().unsqueeze(0)),
        waits=[t2], ch="d_kg")

    for sd in (0, 1):
        t0 = sd * hc
        def hrow(u):
            rk = u if sd == 0 else (c["nc"] - 1 - u)
            return (slice(0, 1), slice((2 * rk + sd) * hss, (2 * rk + sd) * hss + hss))
        z = self.V(lambda e, H: e.memset(H["cgr"][0:1, 7 * hss:8 * hss], 0.0),
                   waits=[self.tok.get("cgr_free")])
        pv = z
        for g in range(6, -1, -1):
            pv = self.V(lambda e, H, g=g, sd=sd: e.scalar_tensor_tensor(
                H["cgr"][0:1, g * hss:(g + 1) * hss], H["hcd"][hrow(g + 1)],
                H["smk_s"][0:1, sd * 8 + g + 1:sd * 8 + g + 2],
                H["cgr"][0:1, (g + 1) * hss:(g + 2) * hss], ALU.mult, ALU.add),
                waits=[tl, pv, self.tok["consts"]])
        ce = self.S(lambda e, H: e.activation(H["cgr"][0:1, 0:8 * hss],
                                              H["cgr"][0:1, 0:8 * hss], AF.Exp),
                    waits=[pv])
        cm = pv
        cms = []
        for g in range(8):
            cg = self.V(lambda e, H, g=g, sd=sd: e.tensor_scalar(
                H["cgr"][0:1, g * hss:(g + 1) * hss], H["cgr"][0:1, g * hss:(g + 1) * hss],
                H["smk_s"][0:1, sd * 8 + g:sd * 8 + g + 1], None, ALU.mult, ALU.bypass),
                waits=[ce])
            cms.append(cg)
        sz = self.V(lambda e, H: e.memset(H["ssm_s"][:, :], 0.0),
                    waits=[self.tok.get("ssm_s_free")])
        sprev = sz
        dsl = None
        cw = inter // 2
        hpb2 = hss // 2
        for g in range(8):
            _, cb_ = self.pbc("ps4", hss, lambda H, g=g: H["cgr"][0:1, g * hss:(g + 1) * hss],
                              lambda H: H["expand"][:, 0:hss],
                              waits=[cms[g]], cwaits=[self.tok.get("expand_free")])
            for half2 in range(2):
                rk = g if sd == 0 else (c["nc"] - 1 - g)
                r2 = self.D(lambda e, H, rk=rk, sd=sd, half2=half2, cw=cw: e.dma_start(
                    out=H["wsp"][:, 0:cw],
                    in_=self.dram_ap("dsout")[rk * 256 + sd * 128:rk * 256 + sd * 128 + 128,
                                              half2 * cw:(half2 + 1) * cw]),
                    waits=[t1, dsl], ch="d_v0")
                ul = None
                for hi in range(hpb2):
                    h = half2 * hpb2 + hi
                    ul = self.V(lambda e, H, h=h, hi=hi: e.scalar_tensor_tensor(
                        H["ssm_s"][:, h * 64:(h + 1) * 64],
                        H["wsp"][:, hi * 64:(hi + 1) * 64],
                        H["expand"][:, h:h + 1],
                        H["ssm_s"][:, h * 64:(h + 1) * 64], ALU.mult, ALU.add),
                        waits=[cb_, r2, sprev])
                dsl = ul
                sprev = ul
            self.tok["expand_free"] = ul
        sb1 = self.V(lambda e, H: e.tensor_copy(H["ssm_sb"][:, :], H["ssm_s"][:, :]),
                     waits=[sprev, self.tok.get("ssm_sb_free")])
        self.tok["cgr_free"] = b.barrier_token("t_m")
        yre = []
        for half in (0, 1):
            yo2 = half * ich * hh
            tk = self.D(lambda e, H, yo2=yo2, half=half, sd=sd: e.dma_start(
                out=H["big"][:, yo2:yo2 + ich * hh].rearrange("p (i t) -> p i t", t=hh),
                in_=self.dram_ap("ytmp").rearrange(
                    "(i p) t -> p i t", p=128)[:, :, sd * hc + half * hh:
                                               sd * hc + half * hh + hh]),
                waits=[self.tok.get("yb_free"), self.tok[("ag_send", sd)]]
                      + self.tok[("ydma_side", sd)], ch="d_v1")
            yre.append(tk)
        for half in range(2):
            yo2 = half * ich * hh
            for hp in range(hss // 2):
                for par in range(2):
                    h = 2 * hp + par
                    g5 = self.T(lambda e, H, h=h, half=half, sd=sd: e.matmul(
                        H["ps3"][:, 0:hh],
                        H["identf_s"][0:hss, h:h + 1].broadcast_to([hss, 128]),
                        H["ea2"][0:hss, sd * hc + half * hh: sd * hc + (half + 1) * hh],
                        start=True, stop=True),
                        waits=[self.bankfree.get("ps3"), b.barrier_token("s_m")])
                    g6 = self.V(lambda e, H, half=half, sd=sd: e.tensor_tensor(
                        H["stg"][:, 0:hh],
                        H["cp"][:, sd * hc + half * hh: sd * hc + (half + 1) * hh],
                        H["ps3"][:, 0:hh], ALU.mult),
                        waits=[g5, self.tok.get("stgc_free")])
                    self.bankfree["ps3"] = g6
                    self.tok["bcc_free"] = g6
                    tp = (0, 64) if par == 1 else None
                    ps_ap = (slice(64, 128), slice(0, hh)) if par == 1 else (slice(0, 64), slice(0, hh))
                    mo = self.T(lambda e, H, h=h, ps_ap=ps_ap, tp=tp: e.matmul(
                        H["ps2"][ps_ap], H["ssm_sb"][:, h * 64:(h + 1) * 64],
                        H["stg"][:, 0:hh], start=True, stop=True,
                        tile_position=tp, skip_group_check=True),
                        waits=[g6, sb1] + ([self.bankfree.get("ps2")] if par == 0 else []))
                    self.tok["stgc_free"] = mo
                ya = self.V(lambda e, H, hp=hp, yo2=yo2: e.tensor_tensor(
                    H["big"][:, yo2 + hp * hh: yo2 + (hp + 1) * hh],
                    H["big"][:, yo2 + hp * hh: yo2 + (hp + 1) * hh], H["ps2"][:, 0:hh],
                    ALU.add), waits=[b.barrier_token("t_m")] + yre)
                self.bankfree["ps2"] = ya
        ycorr = b.barrier_token("v_m")
        self.tok["stg_free"] = b.barrier_token("v_m")
        lt2 = self.ln_quant(t0, False, res_tok=self.tok.get("res"))
        lnt2 = list(self.tok["ln_act_toks"])
        self.tok["act_free"] = b.barrier_token("t_m")
        for half in range(2):
            for gi, oc0 in enumerate(range(0, ich, 3)):
                gsz = min(3, ich - oc0)
                banks = ["ps0", "ps1", "ps3"] if gi % 2 == 0 else ["ps4", "ps5", "ps2"]
                mlast = {}
                for icb in range(0, dch, 4):
                    ics = min(4, dch - icb)
                    wbuf = wrot[self._dk % 2]
                    wch = "d_a" if self._dk % 2 == 0 else "d_b"
                    self._dk += 1
                    tl2 = self.D(lambda e, H, wbuf=wbuf, icb=icb, ics=ics, oc0=oc0,
                                 gsz=gsz: e.dma_start(
                        out=H[wbuf][:, 0:ics * gsz * 128].rearrange(
                            "p (i f) -> p i f", i=ics),
                        in_=self.dram_ap("q_l1zx").rearrange(
                            "(i p) f -> p i f", p=128)[:, icb:icb + ics,
                            oc0 * 128:(oc0 + gsz) * 128]),
                        waits=[wfree[wbuf], self.tok["q_l1zx"]], ch=wch)
                    for ii in range(ics):
                        ic = icb + ii
                        for oi in range(gsz):
                            bk = banks[oi]
                            wt = [tl2, lnt2[ic]]
                            if ic == 0:
                                wt.append(self.bankfree.get(bk))
                            mm = self.T(lambda e, H, bk=bk, wbuf=wbuf, ii=ii, oi=oi,
                                        gsz=gsz, ic=ic, half=half: e.matmul(
                                H[bk][:, 0:hh],
                                H[wbuf][:, (ii * gsz + oi) * 128:(ii * gsz + oi + 1) * 128],
                                H["act"][:, ic * hc + half * hh: ic * hc + half * hh + hh],
                                start=(ic == 0), stop=(ic == dch - 1)), waits=wt)
                            mlast[oi] = mm
                    wfree[wbuf] = mlast[gsz - 1]
                yo2 = half * ich * hh
                for oi in range(gsz):
                    oc = oc0 + oi
                    bk = banks[oi]
                    slot = oc % 4
                    sl = self.S(lambda e, H, bk=bk, slot=slot: e.activation(
                        H["stg"][:, slot * hh:(slot + 1) * hh], H[bk][:, 0:hh], AF.Silu),
                        waits=[mlast[oi], self.tok.get(("zslot", slot)),
                               self.tok.get("stgc_free"), ydma_fin])
                    gm = self.V(lambda e, H, oc=oc, yo2=yo2, slot=slot: e.tensor_tensor(
                        H["big"][:, yo2 + oc * hh: yo2 + (oc + 1) * hh],
                        H["big"][:, yo2 + oc * hh: yo2 + (oc + 1) * hh],
                        H["stg"][:, slot * hh:(slot + 1) * hh], ALU.mult),
                        waits=[sl, ycorr])
                    self.bankfree[bk] = sl
                    self.tok[("zslot", slot)] = gm
            gdone = b.barrier_token("v_m")
            yo2 = half * ich * hh
            sms = None
            for ch in range(ich):
                sq = self.V(lambda e, H, ch=ch, yo2=yo2: e.tensor_tensor(
                    H["sqb"][:, 0:hh], H["big"][:, yo2 + ch * hh: yo2 + (ch + 1) * hh],
                    H["big"][:, yo2 + ch * hh: yo2 + (ch + 1) * hh], ALU.mult),
                    waits=[gdone, self.tok.get("sqb_free")])
                sm = self.T(lambda e, H, ch=ch: e.matmul(
                    H["ps0"][0:1, 0:hh], H["ones_s"][:, 0:1], H["sqb"][:, 0:hh],
                    start=(ch == 0), stop=(ch == ich - 1)),
                    waits=[sq, self.tok["consts"]] + ([self.bankfree.get("ps0")] if ch == 0 else []))
                self.tok["sqb_free"] = sm
                sms = sm
            v1 = self.V(lambda e, H: e.tensor_scalar(
                H["rb"][0:1, 0:hh], H["ps0"][0:1, 0:hh], 1.0 / inter, None,
                ALU.mult, ALU.bypass), waits=[sms])
            self.bankfree["ps0"] = v1
            s1 = self.S(lambda e, H: e.activation(H["rb"][0:1, 0:hh], H["rb"][0:1, 0:hh],
                                                  AF.Ln, bias=H["cc_s"][0:1, 0:1]),
                        waits=[v1])
            s2 = self.S(lambda e, H: e.activation(H["rb"][0:1, 0:hh], H["rb"][0:1, 0:hh],
                                                  AF.Exp, scale=-0.5), waits=[s1])
            _, pb1 = self.pbc("ps1", hh, lambda H: H["rb"][0:1, 0:hh],
                              lambda H: H["bc"][:, 0:hh],
                              waits=[s2], cwaits=[self.tok.get("bcc_free")])
            napply = []
            for ch in range(ich):
                na = self.V(lambda e, H, ch=ch, yo2=yo2: e.scalar_tensor_tensor(
                    H["big"][:, yo2 + ch * hh: yo2 + (ch + 1) * hh],
                    H["big"][:, yo2 + ch * hh: yo2 + (ch + 1) * hh],
                    H["normw_s"][:, ch:ch + 1], H["bc"][:, 0:hh], ALU.mult, ALU.mult),
                    waits=[pb1, lt2, self.tok.get("nrm_free")])
                napply.append(na)
            self.tok["bcc_free"] = napply[-1]
            ntok = b.barrier_token("v_m")
            for gi, oc0 in enumerate(range(0, dch, 3)):
                gsz = min(3, dch - oc0)
                banks = ["ps0", "ps1", "ps2"] if gi % 2 == 0 else ["ps3", "ps4", "ps5"]
                mlast = {}
                for icb in range(0, ich, 4):
                    ics = min(4, ich - icb)
                    wbuf = wrot[self._dk % 2]
                    wch = "d_a" if self._dk % 2 == 0 else "d_b"
                    self._dk += 1
                    tl3 = self.D(lambda e, H, wbuf=wbuf, icb=icb, ics=ics, oc0=oc0,
                                 gsz=gsz: e.dma_start(
                        out=H[wbuf][:, 0:ics * gsz * 128].rearrange(
                            "p (i f) -> p i f", i=ics),
                        in_=self.dram_ap("q_l1wo").rearrange(
                            "(i p) f -> p i f", p=128)[:, icb:icb + ics,
                            oc0 * 128:(oc0 + gsz) * 128]),
                        waits=[wfree[wbuf], self.tok["q_l1wo"]], ch=wch)
                    for ii in range(ics):
                        ic = icb + ii
                        for oi in range(gsz):
                            bk = banks[oi]
                            wt = [tl3, ntok]
                            if ic == 0:
                                wt.append(self.bankfree.get(bk))
                            mm = self.T(lambda e, H, bk=bk, wbuf=wbuf, ii=ii, oi=oi,
                                        gsz=gsz, ic=ic, yo2=yo2: e.matmul(
                                H[bk][:, 0:hh],
                                H[wbuf][:, (ii * gsz + oi) * 128:(ii * gsz + oi + 1) * 128],
                                H["big"][:, yo2 + ic * hh: yo2 + (ic + 1) * hh],
                                start=(ic == 0), stop=(ic == ich - 1)), waits=wt)
                            mlast[oi] = mm
                    wfree[wbuf] = mlast[gsz - 1]
                for oi in range(gsz):
                    oc = oc0 + oi
                    bk = banks[oi]
                    rc = (slice(None), slice(oc * T + t0 + half * hh,
                                             oc * T + t0 + half * hh + hh))
                    e2 = self.V(lambda e, H, rc=rc, bk=bk: e.tensor_tensor(
                        H["res"][rc], H["res"][rc], H[bk][:, 0:hh], ALU.add),
                        waits=[mlast[oi], b.barrier_token("v_m")])
                    self.bankfree[bk] = e2
            self.tok["nrm_free"] = b.barrier_token("t_m")
        self.tok["yb_free"] = b.barrier_token("v_m")
        self.tok["ssm_s_free"] = b.barrier_token("t_m")
        self.tok["ssm_sb_free"] = b.barrier_token("t_m")
    self.tok["res"] = b.barrier_token("v_m")
    self.tok["act_free"] = b.barrier_token("t_m")
    self.tok["big_free"] = [b.barrier_token("t_m"), b.barrier_token("v_m")]
    self.tok["stg_free"] = [b.barrier_token("t_m"), b.barrier_token("v_m")]
    self.tok["uf_free"] = [b.barrier_token(s) for s in ("v_m", "s_m", "g_m", "t_m")]


Prog._ssm_post = _ssm_post
